# version 01
# baseline (speedup 1.0000x reference)
import numpy as np
import ml_dtypes
import contextlib
import concourse.bass as bass
import concourse.mybir as mybir

F32 = mybir.dt.float32
BF16 = mybir.dt.bfloat16
AF = mybir.ActivationFunctionType
ALU = mybir.AluOpType
AX = mybir.AxisListType
BF = ml_dtypes.bfloat16

CHUNK = 16
W = 8
TW = 128
HP = 4
Dh = 128
D = 2048
NK = 16
EPS = 1e-6
LNEPS = 1e-5
IRT128 = float(np.sqrt(1.0 / 128.0))


class Prog:
    ENGS = ("tensor", "vector", "scalar", "sync", "gpsimd")

    def __init__(self):
        self.ops = {e: [] for e in self.ENGS}
        self.cnt = {}
        self.sems = {}

    def declare(self, *names):
        for n in names:
            self.cnt[n] = 0

    def wait(self, eng, sem, val=None):
        if val is None:
            val = self.cnt[sem]
        if val <= 0:
            return
        self.ops[eng].append(lambda e, s=sem, v=val: e.wait_ge(self.sems[s], v))

    def op(self, eng, build, inc=None, by=1, lp=False):
        if eng == "vector":
            self.wait(eng, "v")
        elif eng == "scalar":
            self.wait(eng, "s")
        if inc is not None:
            self.cnt[inc] += by
        def fn(e, build=build, inc=inc, by=by, lp=lp):
            if lp:
                with self.nc.allow_low_precision(reason="bf16 sum columns by design"):
                    ins = build(e)
            else:
                ins = build(e)
            if inc is not None:
                ins.then_inc(self.sems[inc], by)
        self.ops[eng].append(fn)

    def val(self, sem):
        return self.cnt[sem]

    def pb(self, eng, *banks):
        if not hasattr(self, "bankmarks"):
            self.bankmarks = {}
        for b in banks:
            m = self.bankmarks.get(b)
            if m and m[0] != eng:
                self.wait(eng, m[1], m[2])

    def pe(self, eng, sem, *banks):
        if not hasattr(self, "bankmarks"):
            self.bankmarks = {}
        for b in banks:
            self.bankmarks[b] = (eng, sem, self.cnt[sem])


def host_consts():
    ident = np.eye(128, dtype=BF)
    esel = np.zeros((128, W, 16), dtype=BF)
    for c in range(W):
        for t in range(16):
            esel[16 * c + t, c, t] = -1.0
    triu = np.triu(np.ones((128, 128), np.float32)).astype(BF)
    onesb = np.ones((128, 1), dtype=BF)
    ones116 = np.ones((1, 16), dtype=BF)
    ones1r = np.ones((1, 128), dtype=BF)
    selh = np.zeros((2, HP, 128, 32), dtype=BF)
    for h in range(2):
        for p in range(HP):
            for t in range(16):
                selh[h, p, 32 * p + t, 16 * h + t] = 1.0
    return ident, esel, triu, onesb, ones116, ones1r, selh


def build(T, debug_og=False, stages=99):
    NW = T // TW
    NSC = W // 2
    nc = bass.Bass("TRN2", target_bir_lowering=False)
    P = Prog()
    P.nc = nc
    P.declare("t", "v", "s", "dy0", "dy1", "dx0", "dx1", "so0", "so1", "dg", "dz", "cc")

    ap = {}
    def inp(name, shape, dt=F32):
        ap[name] = nc.dram_tensor(name, list(shape), dt, kind="ExternalInput").ap()
    inp("xb", (T, D)); inp("xbT", (D, T))
    inp("wq", (D, 512)); inp("wk", (D, 512)); inp("wv", (D, 512))
    inp("wbeta", (D, HP))
    inp("lnw", (1, HP, Dh)); inp("lnb", (1, HP, Dh))
    inp("wstk", (128, 128))
    inp("woln", (128, HP, Dh)); inp("boln", (128, HP * Dh))
    inp("wnat", (128, HP * Dh)); inp("nrm", (128, HP * Dh))
    inp("wg1", (D, 128)); inp("wg2", (128, 512))
    inp("wo", (512, D)); inp("awr", (D, 520)); inp("bstk", (128, HP, D))
    inp("brow", (1, 8))
    N3 = T // 4
    inp("x3s", (N3, D))
    if debug_og:
        ap["out"] = nc.dram_tensor("out", [NW, 512, 128], F32, kind="ExternalOutput").ap()
    else:
        ap["out"] = nc.dram_tensor("out", [N3, D], F32, kind="ExternalOutput").ap()

    og_loc = nc.dram_tensor("og_loc", [NW, 512, 128], BF16, kind="Internal").ap()
    attn_part = nc.dram_tensor("attn_part", [T, D], BF16, kind="Internal").ap()
    attn_q = nc.dram_tensor("attn_q", [N3, D], BF16, kind="Internal").ap()

    identN, eselN, triuN, onesbN, o116N, o1rN, selhN = host_consts()
    c_ident = nc.inline_tensor(identN, "c_ident").ap()
    c_esel = nc.inline_tensor(eselN, "c_esel").ap()
    c_triu = nc.inline_tensor(triuN, "c_triu").ap()
    c_onesb = nc.inline_tensor(onesbN, "c_onesb").ap()
    c_o116 = nc.inline_tensor(o116N, "c_o116").ap()
    c_o1r = nc.inline_tensor(o1rN, "c_o1r").ap()
    c_one1 = nc.inline_tensor(np.ones((1, 1), np.float32), "c_one1").ap()
    c_zer = nc.inline_tensor(np.zeros((128, 128), BF), "c_zer").ap()
    _cvals = [LNEPS, EPS, -1.0, -0.5, IRT128, float(1.0/np.sqrt(D)), -2.0, 1.0]
    c_cvals = nc.inline_tensor(
        np.broadcast_to(np.asarray(_cvals, np.float32)[None, :], (128, len(_cvals))).copy(),
        "c_cvals").ap()
    c_selh = nc.inline_tensor(selhN.reshape(2 * HP * 128, 32), "c_selh").ap()

    outer = contextlib.ExitStack()
    s12 = contextlib.ExitStack()
    def sbo(name, shape, dt, **kw): return outer.enter_context(nc.sbuf_tensor(name, list(shape), dt, **kw))
    def sb1(name, shape, dt, **kw): return s12.enter_context(nc.sbuf_tensor(name, list(shape), dt, **kw))
    def pso(name, shape): return outer.enter_context(nc.psum_tensor(name, list(shape), F32))

    jk = sbo("jk", (128, 512), F32)
    wos3 = sbo("wos3", (128, 4, D), BF16)
    aws = sbo("aws", (128, NK, 520), BF16)
    bss = sbo("bss", (128, HP, D), BF16)
    brs = sbo("brs", (1, 8), BF16)
    id3 = sbo("id3", (128, 128), BF16)
    cvals = sbo("cvals", (128, 8), F32)
    for _i, _v in enumerate(_cvals):
        nc.const_aps.register(F32, float(_v), cvals[:, _i:_i+1])
    jkb = sbo("jkb", (128, 128), BF16)
    ogfm = [sbo(f"ogfm{h}", (128, 128), BF16, is_transpose_dest=True) for h in range(HP)]
    on1 = sbo("on1", (128, 1), BF16)
    o1r = sbo("o1r", (1, 128), BF16)
    zer = sbo("zer", (128, 128), BF16)
    lnv = sbo("lnv", (128, 1), F32)
    pA = pso("pA", (128, 512)); pB = pso("pB", (128, 512))
    pKH = pso("pKH", (128, 2, 256))
    pS = pso("pS", (128, HP, 128))
    pGAQ = pso("pGAQ", (128, HP, 128))
    pWBT = pso("pWBT", (128, 2, 256))
    pVnA = pso("pVnA", (128, 2, 256)); pVnB = pso("pVnB", (128, 2, 256))
    pVn = [pVnA, pVnB]

    wqs = sb1("wqs", (128, NK, 512), BF16); wks = sb1("wks", (128, NK, 512), BF16)
    wvs = sb1("wvs", (128, NK, 512), BF16)
    wbs = sb1("wbs", (128, NK, HP), BF16); wg1s = sb1("wg1s", (128, NK, 128), BF16)
    wg2s = sb1("wg2s", (128, 512), BF16)
    idb = sb1("idb", (128, 128), BF16); esl = sb1("esl", (128, W, 16), BF16)
    tri = sb1("tri", (128, 128), BF16)
    o116 = sb1("o116", (1, 16), BF16)
    slh = sb1("slh", (128, 2, HP, 32), BF16)
    lnwr = sb1("lnwr", (1, HP, Dh), F32); lnbr = sb1("lnbr", (1, HP, Dh), F32)
    wbrow = sb1("wbrow", (1, HP, Dh), BF16)
    wstks = sb1("wstks", (128, 128), F32); w2sb = sb1("w2sb", (128, 128), BF16)
    wolnb = sb1("wolnb", (128, HP, Dh), BF16)
    bolnn = sb1("bolnn", (128, 512), F32)
    wnatb = sb1("wnatb", (128, 512), BF16); nrmb = sb1("nrmb", (128, 512), BF16)
    csw = sb1("csw", (1, 1668), BF16)
    xtm = [sb1(f"xtm{i}", (128, D), F32) for i in range(2)]
    xfm = [sb1(f"xfm{i}", (128, NK, 128), BF16) for i in range(2)]
    qtm = sb1("qtm", (128, 512), BF16); ktm = sb1("ktm", (128, 512), BF16)
    vtm = sb1("vtm", (128, 512), BF16); kie = sb1("kie", (128, 512), BF16)
    wtg = sb1("wtg", (128, 512), BF16)
    sgm = sb1("sgm", (128, 512), BF16); sg2 = sb1("sg2", (128, 512), F32)
    rec = sb1("rec", (128, 512), F32)
    qfm = [sb1(f"qfm{p}", (128, 128), BF16, is_transpose_dest=True) for p in range(HP)]
    kfm = [sb1(f"kfm{p}", (128, 128), BF16, is_transpose_dest=True) for p in range(HP)]
    kefm = [sb1(f"kefm{p}", (128, 128), BF16, is_transpose_dest=True) for p in range(HP)]
    xgt = sb1("xgt", (128, 128), BF16)
    xgf = sb1("xgf", (128, 128), BF16, is_transpose_dest=True)
    gsb = sb1("gsb", (128, HP, 128), BF16)
    gstke = sb1("gstke", (128, NSC, 16), BF16)
    asb = [sb1(f"asb{p}", (128, 128), BF16) for p in range(HP)]
    s0 = sb1("s0", (128, HP, 130), BF16)
    vnstk = sb1("vnstk", (128, W, 132), BF16)
    vnnat = sb1("vnnat", (128, HP, 130), BF16)
    khh = sb1("khh", (128, 128), BF16); t1b = sb1("t1b", (128, 128), BF16)
    gwb = sb1("gwb", (128, 128), BF16); xts = sb1("xts", (128, 128), BF16)
    mus = sb1("mus", (128, 1), F32); nv = sb1("nv", (128, 1), F32)
    rsd = sb1("rsd", (128, 1), F32)
    s2R = sb1("s2R", (128, 1), F32); mm12 = sb1("mm12", (128, 2), F32)
    m12r = sb1("m12r", (128, 2), F32)
    murp = sb1("murp", (1, 128), F32); murn = sb1("murn", (1, 128), BF16)
    one1 = sb1("one1", (1, 1), F32)
    sx2 = sb1("sx2", (128, 1), F32); mux = sb1("mux", (128, 1), F32)
    nvx = sb1("nvx", (128, 1), F32); rstdx = sb1("rstdx", (128, 1), F32)
    ngrx = sb1("ngrx", (128, 1), F32)
    rk2 = sb1("rk2", (128, HP, 1), F32); rkl = sb1("rkl", (128, HP, 1), F32)
    rstdk = sb1("rstdk", (128, HP, 1), F32)
    ebv = sb1("ebv", (128, HP), BF16); dbv = sb1("dbv", (128, HP), F32)
    rbv = sb1("rbv", (128, HP), F32); negb = sb1("negb", (128, HP), F32)
    qbv = sb1("qbv", (128, 512), BF16)
    muo = sb1("muo", (128, HP), F32); so2 = sb1("so2", (128, HP), F32)
    nvo = sb1("nvo", (128, HP), F32); rso = sb1("rso", (128, HP), F32)
    lno = sb1("lno", (128, HP), F32)
    qhh = sb1("qhh", (128, 128), BF16); obuf = sb1("obuf", (128, 128), BF16)
    onat = sb1("onat", (128, 512), BF16)
    ogb = sb1("ogb", (128, 512), BF16); ogc = sb1("ogc", (128, 512), BF16)
    ro2 = sb1("ro2", (128, HP, 1), F32); rol = sb1("rol", (128, HP, 1), F32)
    rsog = sb1("rsog", (128, HP, 1), F32)

    def g_load(dst, src):
        P.op("gpsimd", lambda e, d=dst, s=src: e.dma_start(out=d, in_=s), inc="dg", by=16)
    g_load(wqs[:, :, :], ap["wq"].rearrange("(a p) n -> p a n", p=128))
    g_load(wks[:, :, :], ap["wk"].rearrange("(a p) n -> p a n", p=128))
    g_load(wvs[:, :, :], ap["wv"].rearrange("(a p) n -> p a n", p=128))
    g_load(wbs[:, :, :], ap["wbeta"].rearrange("(a p) n -> p a n", p=128))
    g_load(wg1s[:, :, :], ap["wg1"].rearrange("(a p) n -> p a n", p=128))
    g_load(wg2s[:, :], ap["wg2"][:, :])
    g_load(idb[:, :], c_ident[:, :])
    g_load(esl[:, :, :], c_esel[:, :, :])
    g_load(tri[:, :], c_triu[:, :])
    g_load(on1[:, :], c_onesb[:, :])
    g_load(o116[:, :], c_o116[:, :])
    g_load(o1r[:, :], c_o1r[:, :])
    g_load(one1[:, :], c_one1[:, :])
    g_load(zer[:, :], c_zer[:, :])
    g_load(cvals[:, :], c_cvals[:, :])
    g_load(slh[:, :, :, :], c_selh.rearrange("(h q p) n -> p h q n", h=2, q=HP))
    g_load(lnwr[:, :, :], ap["lnw"][:, :, :])
    g_load(lnbr[:, :, :], ap["lnb"][:, :, :])
    g_load(wstks[:, :], ap["wstk"][:, :])
    g_load(wolnb[:, :, :], ap["woln"][:, :, :])
    g_load(bolnn[:, :], ap["boln"][:, :])
    g_load(wnatb[:, :], ap["wnat"][:, :])
    g_load(nrmb[:, :], ap["nrm"][:, :])
    n_wload = P.val("dg")
    P.op("sync", lambda e: e.dma_start(out=xtm[0][:, :], in_=ap["xb"][0:128, :]), inc="dy0", by=16)
    P.op("gpsimd", lambda e: e.dma_start(out=xfm[0][:, :, :],
         in_=ap["xbT"].rearrange("(a p) t -> p a t", p=128)[:, :, 0:128]), inc="dx0", by=16)

    P.wait("vector", "dg", n_wload)
    P.op("vector", lambda e: e.tensor_tensor(out=wbrow[:, :, :], in0=lnwr[:, :, :],
         in1=lnbr[:, :, :], op=ALU.mult), inc="v")
    P.op("vector", lambda e: e.memset(s0[:, :, :], 0.0), inc="v")
    P.op("vector", lambda e: e.memset(vnstk[:, :, :], 0.0), inc="v")
    P.op("vector", lambda e: e.memset(vnnat[:, :, :], 0.0), inc="v")
    for p_ in (pKH[:, :, :], pWBT[:, :, :], pGAQ[:, :, :], pS[:, :, :],
               pVnA[:, :, :], pVnB[:, :, :], pA[:, :], pB[:, :]):
        P.op("vector", lambda e, pp=p_: e.memset(pp, 0.0), inc="v")
    v_init = P.val("v")
    P.pe("vector", "v", "pA", "pB", "pKH", "pS", "pGAQ", "pWBT", "pVnA", "pVnB")
    P.wait("scalar", "dg", n_wload)
    P.op("scalar", lambda e: e.activation(out=w2sb[:, :], in_=wstks[:, :], func=AF.Square), inc="s")
    P.wait("tensor", "dg", n_wload)
    P.wait("tensor", "v", v_init)
    for wsb, width, off in [(wqs, 512, 0), (wks, 512, 512), (wvs, 512, 1024),
                            (wbs, HP, 1536), (wg1s, 128, 1540)]:
        P.wait("tensor", "v")
        P.pb("tensor", "pA")
        for kt in range(NK):
            P.op("tensor", lambda e, w_=wsb, k_=kt, n_=width: e.matmul(
                pA[0:1, 0:n_], on1[:, :], w_[:, k_, 0:n_], start=(k_ == 0),
                stop=(k_ == NK - 1)), inc="t" if kt == NK - 1 else None)
        P.pe("tensor", "t", "pA")
        P.wait("vector", "t")
        P.pb("vector", "pA")
        P.op("vector", lambda e, n_=width, o_=off: e.tensor_scalar(
            out=csw[0:1, o_:o_ + n_], in0=pA[0:1, 0:n_], scalar1=1.0, scalar2=None,
            op0=ALU.mult, op1=ALU.bypass), inc="v")
        P.pe("vector", "v", "pA")
    P.pb("tensor", "pS")
    P.op("tensor", lambda e: e.matmul(pS.ap().rearrange("p a b -> p (a b)"), zer[:, :],
         wqs[:, 0, :], start=True, stop=True, skip_group_check=True), inc="t")
    P.pe("tensor", "t", "pS")
    P.wait("tensor", "v")

    st = {"buf": 0, "xtm_dy_0": 16, "xfm_dx_0": 16,
          "pA_free_s": 0, "pB_free_v": 0, "pB_free_s": 0,
          "gaq_free_v": 0, "s0_ready_v": 0, "s0_ready_s": 0,
          "pS_free_v": 0, "pS_free_s": 0}

    def stats_phase(wi):
        buf = st["buf"]
        P.wait("tensor", "dg", n_wload)
        P.wait("tensor", f"dx{buf}", st[f"xfm_dx_{buf}"])
        P.wait("tensor", "s", st["pA_free_s"])
        P.pb("tensor", "pA")
        for kt in range(NK):
            P.op("tensor", lambda e, k_=kt, b_=buf: e.matmul(
                pA[0:1, 0:128], on1[:, :], xfm[b_][:, k_, :], start=(k_ == 0),
                stop=(k_ == NK - 1)), inc="t" if kt == NK - 1 else None)
        P.pe("tensor", "t", "pA")
        P.wait("vector", "t")
        P.pb("vector", "pA")
        P.op("vector", lambda e: e.tensor_scalar(out=murp[:, :], in0=pA[0:1, 0:128],
             scalar1=1.0 / D, scalar2=None, op0=ALU.mult, op1=ALU.bypass), inc="v")
        P.op("vector", lambda e: e.tensor_scalar(out=murn[:, :], in0=pA[0:1, 0:128],
             scalar1=-1.0 / D, scalar2=None, op0=ALU.mult, op1=ALU.bypass), inc="v")
        v_mur = P.val("v")
        P.pe("vector", "v", "pA")
        P.wait("scalar", f"dy{buf}", st[f"xtm_dy_{buf}"])
        P.op("scalar", lambda e, b_=buf: e.activation(out=xtm[1 - b_][:, :], in_=xtm[b_][:, :],
             func=AF.Square, scale=float(1.0 / np.sqrt(D)), accum_out=sx2[:, :]), inc="s")
        st["sx2_s"] = P.val("s")
        P.wait("tensor", "v", v_mur)
        P.pb("tensor", "pKH")
        P.op("tensor", lambda e: e.transpose(pKH[:, 0, 131:132], murp[:, :], one1[:, :]), inc="t")
        P.pe("tensor", "t", "pKH")
        P.wait("vector", "t")
        P.pb("vector", "pKH")
        P.op("vector", lambda e: e.tensor_scalar(out=mux[:, :], in0=pKH[:, 0, 131:132],
             scalar1=1.0, scalar2=None, op0=ALU.mult, op1=ALU.bypass), inc="v")
        P.pe("vector", "v", "pKH")
        P.wait("vector", "s", st["sx2_s"])
        P.op("vector", lambda e: e.scalar_tensor_tensor(out=nvx[:, :], in0=mux[:, :],
             scalar=mux[:, :], in1=sx2[:, :], op0=ALU.mult, op1=ALU.subtract), inc="v")
        P.wait("scalar", "v")
        P.op("scalar", lambda e: e.activation(out=lnv[:, :], in_=nvx[:, :], func=AF.Ln,
             bias=LNEPS, scale=-1.0), inc="s")
        P.op("scalar", lambda e: e.activation(out=rstdx[:, :], in_=lnv[:, :], func=AF.Exp,
             scale=-0.5), inc="s")
        P.wait("vector", "s")
        P.op("vector", lambda e: e.tensor_scalar(out=ngrx[:, :], in0=rstdx[:, :],
             scalar1=-1.0, scalar2=None, op0=ALU.mult, op1=ALU.bypass), inc="v")

    def load_tile(wi):
        b = wi % 2
        P.wait("sync", "s", st.get("sx2_s", 0))
        P.op("sync", lambda e, w_=wi, b_=b: e.dma_start(out=xtm[b_][:, :],
             in_=ap["xb"][TW*w_:TW*(w_+1), :]), inc=f"dy{b}", by=16)
        st[f"xtm_dy_{b}"] = P.val(f"dy{b}")
        P.wait("gpsimd", "t", st.get(f"xfm_free_{b}", 0))
        P.op("gpsimd", lambda e, w_=wi, b_=b: e.dma_start(out=xfm[b_][:, :, :],
             in_=ap["xbT"].rearrange("(a p) t -> p a t", p=128)[:, :, TW*w_:TW*(w_+1)]),
             inc=f"dx{b}", by=16)
        st[f"xfm_dx_{b}"] = P.val(f"dx{b}")

    def proj_phase(wi):
        buf = st["buf"]
        P.wait("tensor", "v")
        P.pb("tensor", "pA")
        for kt in range(NK):
            P.op("tensor", lambda e, k_=kt, b_=buf: e.matmul(
                pA[:, 0:512], xfm[b_][:, k_, :], wqs[:, k_, :], start=(k_ == 0), stop=False))
        P.op("tensor", lambda e: e.matmul(pA[:, 0:512], murn[:, :], csw[0:1, 0:512],
             start=False, stop=True), inc="t")
        t_q = P.val("t")
        P.pe("tensor", "t", "pA")
        P.wait("tensor", "v", st["pB_free_v"]); P.wait("tensor", "s", st["pB_free_s"])
        P.pb("tensor", "pB")
        for kt in range(NK):
            P.op("tensor", lambda e, k_=kt, b_=buf: e.matmul(
                pB[:, 0:512], xfm[b_][:, k_, :], wks[:, k_, :], start=(k_ == 0), stop=False))
        P.op("tensor", lambda e: e.matmul(pB[:, 0:512], murn[:, :], csw[0:1, 512:1024],
             start=False, stop=True), inc="t")
        t_k = P.val("t")
        P.pe("tensor", "t", "pB")
        P.wait("scalar", "t", t_q)
        P.pb("scalar", "pA")
        P.op("scalar", lambda e: e.activation(out=sgm[:, :], in_=pA[:, 0:512], func=AF.Exp,
             scale=ngrx[:, :]), inc="s")
        P.pe("scalar", "s", "pA")
        P.wait("vector", "s")
        P.op("vector", lambda e: e.tensor_scalar(out=sg2[:, :], in0=sgm[:, :], scalar1=1.0,
             scalar2=None, op0=ALU.add, op1=ALU.bypass), inc="v")
        P.op("vector", lambda e: e.reciprocal(out=rec[:, :], in_=sg2[:, :]), inc="v")
        P.pb("vector", "pA")
        P.op("vector", lambda e: e.scalar_tensor_tensor(out=qtm[:, :], in0=pA[:, 0:512],
             scalar=rstdx[:, :], in1=rec[:, :], op0=ALU.mult, op1=ALU.mult), inc="v")
        v_q = P.val("v")
        P.pe("vector", "v", "pA")
        P.wait("tensor", "v", v_q)
        P.pb("tensor", "pA")
        for kt in range(NK):
            P.op("tensor", lambda e, k_=kt, b_=buf: e.matmul(
                pA[:, 0:512], xfm[b_][:, k_, :], wvs[:, k_, :], start=(k_ == 0), stop=False))
        P.op("tensor", lambda e: e.matmul(pA[:, 0:512], murn[:, :], csw[0:1, 1024:1536],
             start=False, stop=True), inc="t")
        t_v = P.val("t")
        P.pe("tensor", "t", "pA")
        P.wait("scalar", "t", t_k)
        P.pb("scalar", "pB")
        P.op("scalar", lambda e: e.activation(out=jk[:, :], in_=pB[:, 0:512], func=AF.Square,
             scale=rstdx[:, :]), inc="s")
        P.pe("scalar", "s", "pB")
        P.wait("vector", "s")
        P.op("vector", lambda e: e.reduce_sum(out=rk2[:, :, :],
             in_=jk.ap().rearrange("p (h e) -> p h e", h=HP), axis=AX.X), inc="v")
        P.wait("scalar", "v")
        P.op("scalar", lambda e: e.activation(out=rkl[:, :, :], in_=rk2[:, :, :], func=AF.Ln,
             bias=EPS), inc="s")
        P.op("scalar", lambda e: e.activation(out=rstdk[:, :, :], in_=rkl[:, :, :],
             func=AF.Exp, scale=-0.5), inc="s")
        P.wait("vector", "s")
        P.pb("vector", "pB")
        for h in range(HP):
            P.op("vector", lambda e, h_=h: e.tensor_scalar(out=ktm[:, 128*h_:128*(h_+1)],
                 in0=pB[:, 128*h_:128*(h_+1)], scalar1=rstdx[:, :], scalar2=rstdk[:, h_, :],
                 op0=ALU.mult, op1=ALU.mult), inc="v")
        v_k = P.val("v")
        P.pe("vector", "v", "pB")
        P.wait("tensor", "v", v_k)
        P.pb("tensor", "pB")
        for kt in range(NK):
            P.op("tensor", lambda e, k_=kt, b_=buf: e.matmul(
                pB[:, 0:HP], xfm[b_][:, k_, :], wbs[:, k_, :], start=(k_ == 0), stop=False))
        P.op("tensor", lambda e: e.matmul(pB[:, 0:HP], murn[:, :], csw[0:1, 1536:1536+HP],
             start=False, stop=True), inc="t")
        for kt in range(NK):
            P.op("tensor", lambda e, k_=kt, b_=buf: e.matmul(
                pB[:, HP:HP+128], xfm[b_][:, k_, :], wg1s[:, k_, :], start=(k_ == 0), stop=False))
        P.op("tensor", lambda e: e.matmul(pB[:, HP:HP+128], murn[:, :], csw[0:1, 1540:1668],
             start=False, stop=True), inc="t")
        t_misc = P.val("t")
        P.pe("tensor", "t", "pB")
        st[f"xfm_free_{buf}"] = t_misc
        P.wait("vector", "t", t_v)
        P.pb("vector", "pA")
        P.op("vector", lambda e: e.tensor_scalar(out=vtm[:, :], in0=pA[:, 0:512],
             scalar1=rstdx[:, :], scalar2=None, op0=ALU.mult, op1=ALU.bypass), inc="v")
        P.pe("vector", "v", "pA")
        P.op("vector", lambda e: e.tensor_tensor(out=wtg[:, :], in0=vtm[:, :], in1=ktm[:, :],
             op=ALU.subtract), inc="v")
        P.op("vector", lambda e: e.tensor_tensor(out=wtg[:, :], in0=wtg[:, :],
             in1=wnatb[:, :], op=ALU.mult), inc="v")
        st["v_wtg"] = P.val("v")
        P.wait("scalar", "t", t_misc)
        P.pb("scalar", "pB")
        P.op("scalar", lambda e: e.activation(out=ebv[:, :], in_=pB[:, 0:HP], func=AF.Exp,
             scale=ngrx[:, :]), inc="s")
        st["pB_free_s"] = P.val("s")
        P.pe("scalar", "s", "pB")
        P.wait("vector", "s")
        P.op("vector", lambda e: e.tensor_scalar(out=dbv[:, :], in0=ebv[:, :], scalar1=1.0,
             scalar2=None, op0=ALU.add, op1=ALU.bypass), inc="v")
        P.op("vector", lambda e: e.reciprocal(out=rbv[:, :], in_=dbv[:, :]), inc="v")
        P.op("vector", lambda e: e.tensor_scalar(out=negb[:, :], in0=rbv[:, :],
             scalar1=-2.0, scalar2=None, op0=ALU.mult, op1=ALU.bypass), inc="v")
        for h in range(HP):
            P.op("vector", lambda e, h_=h: e.tensor_scalar(out=kie[:, 128*h_:128*(h_+1)],
                 in0=ktm[:, 128*h_:128*(h_+1)], scalar1=negb[:, h_:h_+1], scalar2=None,
                 op0=ALU.mult, op1=ALU.bypass), inc="v")
        P.pb("vector", "pB")
        P.op("vector", lambda e: e.tensor_scalar(out=xgt[:, :], in0=pB[:, HP:HP+128],
             scalar1=rstdx[:, :], scalar2=None, op0=ALU.mult, op1=ALU.bypass), inc="v")
        st["pB_free_v"] = P.val("v")
        P.pe("vector", "v", "pB")
        P.op("vector", lambda e: e.tensor_tensor(out=qbv[:, :], in0=qtm[:, :],
             in1=bolnn[:, :], op=ALU.add), inc="v")
        v_all = P.val("v")
        P.wait("sync", "v", v_q)
        for p in range(HP):
            P.op("sync", lambda e, p_=p: e.dma_start(out=qfm[p_][:, :],
                 in_=qtm[:, 128*p_:128*(p_+1)], transpose=True), inc="dz", by=16)
        P.wait("sync", "v", v_k)
        for p in range(HP):
            P.op("sync", lambda e, p_=p: e.dma_start(out=kfm[p_][:, :],
                 in_=ktm[:, 128*p_:128*(p_+1)], transpose=True), inc="dz", by=16)
        P.wait("sync", "v", v_all)
        for p in range(HP):
            P.op("sync", lambda e, p_=p: e.dma_start(out=kefm[p_][:, :],
                 in_=kie[:, 128*p_:128*(p_+1)], transpose=True), inc="dz", by=16)
        P.op("sync", lambda e: e.dma_start(out=xgf[:, :], in_=xgt[:, :], transpose=True),
             inc="dz", by=16)

    def scan_phase(wi, debug):
        if stages < 1:
            return
        dz_fm = P.val("dz")
        P.wait("tensor", "v", st.get("pvn_evac_v", 0))
        P.wait("tensor", "s", st.get("pvn_evac_s", 0))
        P.pb("tensor", "pVnA", "pVnB")
        P.op("tensor", lambda e: e.matmul(pVnA.ap().rearrange("p a b -> p (a b)"),
             zer[:, :], wqs[:, 0, :], start=True, stop=True, skip_group_check=True))
        P.op("tensor", lambda e: e.matmul(pVnB.ap().rearrange("p a b -> p (a b)"),
             zer[:, :], wqs[:, 0, :], start=True, stop=True, skip_group_check=True), inc="t")
        P.pe("tensor", "t", "pVnA", "pVnB")
        P.wait("tensor", "dz", dz_fm)
        P.wait("tensor", "v", st["gaq_free_v"])
        P.pb("tensor", "pGAQ")
        for p in range(HP):
            P.op("tensor", lambda e, p_=p: e.matmul(pGAQ[:, p_, :], kefm[p_][:, :],
                 kfm[p_][:, :], start=True, stop=True), inc="t" if p == HP-1 else None)
        P.pe("tensor", "t", "pGAQ")
        P.wait("vector", "t")
        P.pb("vector", "pGAQ")
        P.op("vector", lambda e: e.tensor_scalar(out=gsb[:, :, :], in0=pGAQ[:, :, :],
             scalar1=1.0, scalar2=None, op0=ALU.mult, op1=ALU.bypass), inc="v")
        P.pe("vector", "v", "pGAQ")
        P.wait("tensor", "v")
        P.pb("tensor", "pGAQ")
        for p in range(HP):
            P.op("tensor", lambda e, p_=p: e.matmul(pGAQ[:, p_, :], kefm[p_][:, :],
                 qfm[p_][:, :], start=True, stop=True), inc="t" if p == HP-1 else None)
        P.pe("tensor", "t", "pGAQ")
        P.wait("vector", "t")
        P.pb("vector", "pGAQ")
        for p in range(HP):
            P.op("vector", lambda e, p_=p: e.tensor_tensor(out=asb[p_][:, :],
                 in0=pGAQ[:, p_, :], in1=tri[:, :], op=ALU.mult), inc="v")
        v_asb = P.val("v")
        P.pe("vector", "v", "pGAQ")
        P.wait("tensor", "v", v_asb)
        P.pb("tensor", "pGAQ")
        for k in range(NSC):
            for p in range(HP):
                P.op("tensor", lambda e, k_=k, p_=p: e.matmul(
                    pGAQ[32*p_:32*p_+16, k_, 0:16], kefm[p_][:, 32*k_:32*k_+16],
                    kfm[p_][:, 16*(2*k_+1):16*(2*k_+2)], start=True, stop=True,
                    tile_position=(0, 32*p_)),
                    inc="t" if (k == NSC-1 and p == HP-1) else None)
        P.pe("tensor", "t", "pGAQ")
        P.wait("vector", "t")
        P.pb("vector", "pGAQ")
        P.op("vector", lambda e: e.tensor_scalar(out=gstke[:, :, :], in0=pGAQ[:, 0:NSC, 0:16],
             scalar1=1.0, scalar2=None, op0=ALU.mult, op1=ALU.bypass), inc="v")
        st["gaq_free_v"] = P.val("v")
        P.pe("vector", "v", "pGAQ")
        wbt_ready = {}
        def wbt_mm(c):
            P.pb("tensor", "pWBT")
            for p in range(HP):
                P.op("tensor", lambda e, c_=c, p_=p: e.matmul(
                    pWBT[32*p_:32*p_+16, c_ % 2, 0:128], esl[:, c_, :],
                    wtg[:, 128*p_:128*(p_+1)], start=True, stop=False,
                    tile_position=(0, 32*p_)))
                P.op("tensor", lambda e, c_=c, p_=p: e.matmul(
                    pWBT[32*p_:32*p_+16, c_ % 2, 0:128], o116[:, :],
                    wbrow[0:1, p_, :], start=False, stop=True,
                    tile_position=(0, 32*p_)),
                    inc="t" if p == HP-1 else None)
            wbt_ready[c] = P.val("t")
            P.pe("tensor", "t", "pWBT")
        P.wait("tensor", "v", st["v_wtg"])
        wbt_mm(0); wbt_mm(1)

        if stages < 2:
            return
        chain_v = {}; evac_s = {}
        for c in range(W):
            k = c // 2
            if c >= 1:
                P.wait("tensor", "v", chain_v[c - 1])
                if c % 2 == 0 and k >= 1:
                    P.wait("tensor", "s", evac_s.get(k - 1, 0))
            else:
                P.wait("tensor", "v", st["s0_ready_v"])
                P.wait("tensor", "s", st["s0_ready_s"])
            P.pb("tensor", "pKH")
            for p in range(HP):
                last = (p == HP - 1)
                Kc = 32 * k
                if Kc > 0:
                    P.op("tensor", lambda e, c_=c, p_=p, K_=Kc: e.matmul(
                        pKH[32*p_:32*p_+16, c_ % 2, 0:129], gsb[0:K_, p_, 16*c_:16*(c_+1)],
                        vnnat[0:K_, p_, 0:129], start=True, stop=False,
                        tile_position=(0, 32*p_)))
                P.op("tensor", lambda e, c_=c, p_=p, st_=(Kc == 0): e.matmul(
                    pKH[32*p_:32*p_+16, c_ % 2, 0:129], kfm[p_][:, 16*c_:16*(c_+1)],
                    s0[:, p_, 0:129], start=st_, stop=(c_ % 2 == 0),
                    tile_position=(0, 32*p_)),
                    inc="t" if (last and c % 2 == 0) else None)
                if c % 2 == 1:
                    P.op("tensor", lambda e, c_=c, p_=p, k_=k: e.matmul(
                        pKH[32*p_:32*p_+16, c_ % 2, 0:129], gstke[32*p_:32*p_+16, k_, :],
                        vnstk[32*p_:32*p_+16, 2*k_, 0:129], start=False, stop=True,
                        tile_position=(32*p_, 32*p_)),
                        inc="t" if last else None)
            t_kh = P.val("t")
            P.pe("tensor", "t", "pKH")
            P.wait("vector", "t", t_kh)
            P.pb("vector", "pKH")
            P.op("vector", lambda e, c_=c: e.tensor_scalar(out=mus[:, :],
                 in0=pKH[:, c_ % 2, 128:129], scalar1=1.0/128.0, scalar2=None,
                 op0=ALU.mult, op1=ALU.bypass), inc="v")
            P.pe("vector", "v", "pKH")
            P.wait("scalar", "t", t_kh)
            P.pb("scalar", "pKH")
            P.op("scalar", lambda e, c_=c: e.activation(out=jkb[:, :], in_=pKH[:, c_ % 2, 0:128],
                 func=AF.Square, scale=IRT128, accum_out=s2R[:, :]), inc="s")
            s_sq = P.val("s")
            P.pe("scalar", "s", "pKH")
            P.wait("vector", "s", s_sq)
            P.op("vector", lambda e: e.scalar_tensor_tensor(out=nv[:, :], in0=mus[:, :],
                 scalar=mus[:, :], in1=s2R[:, :], op0=ALU.mult, op1=ALU.subtract), inc="v")
            P.wait("scalar", "v")
            P.op("scalar", lambda e: e.activation(out=lnv[:, :], in_=nv[:, :], func=AF.Ln,
                 bias=EPS, scale=-1.0), inc="s")
            P.op("scalar", lambda e: e.activation(out=rsd[:, :], in_=lnv[:, :], func=AF.Exp,
                 scale=-0.5), inc="s")
            P.wait("vector", "s")
            P.pb("vector", "pKH")
            P.op("vector", lambda e, c_=c: e.tensor_scalar(out=khh[:, :],
                 in0=pKH[:, c_ % 2, 0:128], scalar1=mus[:, :], scalar2=rsd[:, :],
                 op0=ALU.subtract, op1=ALU.mult), inc="v")
            P.pe("vector", "v", "pKH")
            P.op("vector", lambda e: e.tensor_tensor(out=t1b[:, :], in0=khh[:, :],
                 in1=w2sb[:, :], op=ALU.mult), inc="v")
            P.pb("vector", "pWBT")
            P.op("vector", lambda e, c_=c: e.scalar_tensor_tensor(out=gwb[:, :],
                 in0=t1b[:, :], scalar=0.0, in1=pWBT[:, c_ % 2, 0:128], op0=ALU.add,
                 op1=ALU.add, accum_out=mm12[:, 0:1]), inc="v")
            P.pe("vector", "v", "pWBT")
            P.op("vector", lambda e: e.scalar_tensor_tensor(out=jkb[:, :], in0=gwb[:, :],
                 scalar=1.0, in1=khh[:, :], op0=ALU.mult, op1=ALU.mult,
                 accum_out=mm12[:, 1:2]), inc="v")
            P.op("vector", lambda e: e.tensor_scalar(out=m12r[:, :], in0=mm12[:, :],
                 scalar1=rsd[:, :], scalar2=1.0/128.0, op0=ALU.mult, op1=ALU.mult), inc="v")
            P.op("vector", lambda e: e.tensor_scalar(out=xts[:, :], in0=khh[:, :],
                 scalar1=m12r[:, 1:2], scalar2=m12r[:, 0:1], op0=ALU.mult, op1=ALU.add), inc="v")
            P.op("vector", lambda e, c_=c: e.scalar_tensor_tensor(out=vnstk[:, c_, 0:128],
                 in0=gwb[:, :], scalar=rsd[:, :], in1=xts[:, :], op0=ALU.mult,
                 op1=ALU.subtract, accum_out=vnstk[:, c_, 128:129]), inc="v", lp=True)
            chain_v[c] = P.val("v")
            if c + 2 < W:
                P.wait("tensor", "v", chain_v[c])
                wbt_mm(c + 2)
            P.wait("tensor", "v", chain_v[c])
            P.pb("tensor", "pVnA", "pVnB")
            for p in range(HP):
                P.op("tensor", lambda e, c_=c, p_=p, k_=k: e.matmul(
                    pVn[p_ // 2][32*k_:32*k_+32, p_ % 2, 0:129],
                    slh[:, c_ % 2, p_, :], vnstk[:, c_, 0:129],
                    start=False, stop=(c_ % 2 == 1),
                    tile_position=(0, 32*k_), skip_group_check=True),
                    inc="t" if p == HP-1 else None)
            t_rs = P.val("t")
            P.pe("tensor", "t", "pVnA", "pVnB")
            if c % 2 == 1:
                P.wait("vector", "t", t_rs)
                P.pb("vector", "pVnA", "pVnB")
                P.op("vector", lambda e, k_=k: e.tensor_scalar(
                     out=vnnat[32*k_:32*k_+32, 0:2, 0:129], in0=pVnA[32*k_:32*k_+32, :, 0:129],
                     scalar1=1.0, scalar2=None, op0=ALU.mult, op1=ALU.bypass), inc="v")
                P.op("vector", lambda e, k_=k: e.tensor_scalar(
                     out=vnnat[32*k_:32*k_+32, 2:4, 0:129], in0=pVnB[32*k_:32*k_+32, :, 0:129],
                     scalar1=1.0, scalar2=None, op0=ALU.mult, op1=ALU.bypass), inc="v")
                chain_v[c] = P.val("v")
                st["pvn_evac_v"] = P.val("v")
                P.pe("vector", "v", "pVnA", "pVnB")
        v_chain = P.val("v"); s_chain = P.val("s")

        P.wait("tensor", "v", v_chain)
        P.wait("tensor", "s", s_chain)
        P.pb("tensor", "pGAQ", "pWBT")
        for p in range(HP):
            P.op("tensor", lambda e, p_=p: e.matmul(pGAQ[:, p_, 0:128], qfm[p_][:, :],
                 s0[:, p_, 0:128], start=True, stop=False))
            P.op("tensor", lambda e, p_=p: e.matmul(pWBT[:, 0, 128+p_:129+p_], qfm[p_][:, :],
                 s0[:, p_, 128:129], start=True, stop=False))
            P.op("tensor", lambda e, p_=p: e.matmul(pGAQ[:, p_, 0:128], asb[p_][:, :],
                 vnnat[:, p_, 0:128], start=False, stop=True))
            P.op("tensor", lambda e, p_=p: e.matmul(pWBT[:, 0, 128+p_:129+p_], asb[p_][:, :],
                 vnnat[:, p_, 128:129], start=False, stop=True),
                 inc="t" if p == HP-1 else None)
        t_qh = P.val("t")
        P.pe("tensor", "t", "pGAQ", "pWBT")
        P.pb("tensor", "pS")
        for p in range(HP):
            P.op("tensor", lambda e, p_=p, w_=wi: e.matmul(pS[:, p_, :],
                 kie[:, 128*p_:128*(p_+1)], vnnat[:, p_, 0:128],
                 start=False, stop=(w_ == NW - 1),
                 skip_group_check=True), inc="t" if p == HP-1 else None)
        t_ds = P.val("t")
        P.pe("tensor", "t", "pS")
        P.wait("vector", "t", t_ds)
        P.pb("vector", "pS")
        for p in range(HP):
            P.op("vector", lambda e, p_=p: e.tensor_scalar(out=s0[:, p_, 0:128],
                 in0=pS[:, p_, :], scalar1=1.0, scalar2=None, op0=ALU.mult,
                 op1=ALU.add, accum_out=s0[:, p_, 128:129]), inc="v", lp=True)
        st["pS_free_v"] = P.val("v"); st["s0_ready_v"] = P.val("v")
        P.pe("vector", "v", "pS")
        st["pS_free_s"] = 0; st["s0_ready_s"] = 0

        if stages < 4:
            return
        P.wait("vector", "t", t_qh)
        P.pb("vector", "pWBT")
        P.op("vector", lambda e: e.tensor_scalar(out=muo[:, :], in0=pWBT[:, 0, 128:132],
             scalar1=1.0/128.0, scalar2=None, op0=ALU.mult, op1=ALU.bypass), inc="v")
        P.pe("vector", "v", "pWBT")
        P.wait("scalar", "t", t_qh)
        P.pb("scalar", "pGAQ")
        for p in range(HP):
            P.op("scalar", lambda e, p_=p: e.activation(out=jkb[:, :], in_=pGAQ[:, p_, 0:128],
                 func=AF.Square, scale=IRT128, accum_out=so2[:, p_:p_+1]), inc="s")
        P.pe("scalar", "s", "pGAQ")
        P.wait("vector", "s")
        for p in range(HP):
            P.op("vector", lambda e, p_=p: e.scalar_tensor_tensor(out=nvo[:, p_:p_+1],
                 in0=muo[:, p_:p_+1], scalar=muo[:, p_:p_+1], in1=so2[:, p_:p_+1],
                 op0=ALU.mult, op1=ALU.subtract), inc="v")
        P.wait("scalar", "v")
        P.op("scalar", lambda e: e.activation(out=lno[:, :], in_=nvo[:, :], func=AF.Ln,
             bias=EPS, scale=-1.0), inc="s")
        P.op("scalar", lambda e: e.activation(out=rso[:, :], in_=lno[:, :], func=AF.Exp,
             scale=-0.5), inc="s")
        P.wait("vector", "s")
        P.pb("vector", "pGAQ")
        for p in range(HP):
            P.op("vector", lambda e, p_=p: e.tensor_scalar(out=qhh[:, :],
                 in0=pGAQ[:, p_, 0:128], scalar1=muo[:, p_:p_+1], scalar2=rso[:, p_:p_+1],
                 op0=ALU.subtract, op1=ALU.mult), inc="v")
            P.op("vector", lambda e, p_=p: e.tensor_tensor(out=obuf[:, :], in0=qhh[:, :],
                 in1=wolnb[:, p_, :], op=ALU.mult), inc="v")
            P.op("vector", lambda e, p_=p: e.tensor_tensor(out=onat[:, 128*p_:128*(p_+1)],
                 in0=obuf[:, :], in1=qbv[:, 128*p_:128*(p_+1)], op=ALU.add), inc="v")
        st["gaq_free_v"] = P.val("v")
        v_onat = P.val("v")
        P.pe("vector", "v", "pGAQ")

        P.wait("tensor", "dz", dz_fm)
        P.wait("tensor", "s", st["pA_free_s"])
        P.wait("tensor", "v", v_onat)
        P.pb("tensor", "pA")
        P.op("tensor", lambda e: e.matmul(pA[:, 0:512], xgf[:, :], wg2s[:, :],
             start=True, stop=True), inc="t")
        t_gate = P.val("t")
        P.pe("tensor", "t", "pA")
        P.wait("scalar", "t", t_gate)
        P.pb("scalar", "pA")
        P.op("scalar", lambda e: e.activation(out=sgm[:, :], in_=pA[:, 0:512], func=AF.Exp,
             scale=-1.0), inc="s")
        st["pA_free_s"] = P.val("s")
        P.pe("scalar", "s", "pA")
        P.wait("vector", "s")
        P.op("vector", lambda e: e.tensor_scalar(out=sg2[:, :], in0=sgm[:, :], scalar1=1.0,
             scalar2=None, op0=ALU.add, op1=ALU.bypass), inc="v")
        P.op("vector", lambda e: e.reciprocal(out=rec[:, :], in_=sg2[:, :]), inc="v")
        P.op("vector", lambda e: e.tensor_tensor(out=ogb[:, :], in0=onat[:, :], in1=rec[:, :],
             op=ALU.mult), inc="v")
        P.wait("scalar", "v")
        P.op("scalar", lambda e: e.activation(out=jk[:, :], in_=ogb[:, :], func=AF.Square,
             scale=IRT128), inc="s")
        P.wait("vector", "s")
        P.op("vector", lambda e: e.reduce_sum(out=ro2[:, :, :],
             in_=jk.ap().rearrange("p (h e) -> p h e", h=HP), axis=AX.X), inc="v")
        P.wait("scalar", "v")
        P.op("scalar", lambda e: e.activation(out=rol[:, :, :], in_=ro2[:, :, :], func=AF.Ln,
             bias=EPS), inc="s")
        P.op("scalar", lambda e: e.activation(out=rsog[:, :, :], in_=rol[:, :, :],
             func=AF.Exp, scale=-0.5), inc="s")
        P.wait("vector", "s")
        for h in range(HP):
            P.op("vector", lambda e, h_=h: e.tensor_scalar(out=ogc[:, 128*h_:128*(h_+1)],
                 in0=ogb[:, 128*h_:128*(h_+1)], scalar1=rsog[:, h_, :], scalar2=None,
                 op0=ALU.mult, op1=ALU.bypass), inc="v")
        P.op("vector", lambda e: e.tensor_tensor(out=ogc[:, :], in0=ogc[:, :], in1=nrmb[:, :],
             op=ALU.mult), inc="v")
        v_ogc = P.val("v")
        P.wait("sync", "v", v_ogc)
        P.wait("sync", "dz")
        if debug:
            P.wait("sync", "dg")
        for h in range(HP):
            P.op("sync", lambda e, h_=h: e.dma_start(out=ogfm[h_][:, :],
                 in_=ogc[:, 128*h_:128*(h_+1)], transpose=True), inc="dz", by=16)
        dz_og = P.val("dz")
        P.wait("scalar", "dz", dz_og)
        for h in range(HP):
            P.op("scalar", lambda e, h_=h, w_=wi: e.dma_start(
                 out=og_loc[w_, 128*h_:128*(h_+1), :], in_=ogfm[h_][:, :]), inc="dz", by=16)
        if debug:
            P.wait("gpsimd", "dz", dz_og)
            for h in range(HP):
                P.op("gpsimd", lambda e, h_=h, w_=wi: e.dma_start(
                     out=ap["out"][w_, 128*h_:128*(h_+1), :], in_=ogfm[h_][:, :]),
                     inc="dg", by=16)

    for wi in range(NW):
        stats_phase(wi)
        if wi + 1 < NW:
            load_tile(wi + 1)
        proj_phase(wi)
        scan_phase(wi, debug_og)
        st["buf"] = 1 - st["buf"]

    if debug_og:
        for s_ in ("dg", "dz", "dy0", "dy1", "dx0", "dx1", "so0", "so1"):
            P.wait("gpsimd", s_)
        finalize(nc, P)
        return nc, {"NW": NW}

    s12.close()
    NT3 = N3 // TW
    sb3 = lambda name, shape, dt, **kw: outer.enter_context(nc.sbuf_tensor(name, list(shape), dt, **kw))
    ogl3 = [sb3(f"ogl3{i}", (128, 4, 128), BF16) for i in range(2)]
    atb3 = [sb3(f"atb3{i}", (128, D), BF16) for i in range(2)]

    def g3_load(dst, src):
        P.op("gpsimd", lambda e, d=dst, s=src: e.dma_start(out=d, in_=s), inc="dg", by=16)
    g3_load(wos3[:, :, :], ap["wo"].rearrange("(a p) n -> p a n", p=128))
    dg_wo3 = P.val("dg")

    attnb = [pA[:, 0:512], pB[:, 0:512],
             pGAQ.ap().rearrange("p a b -> p (a b)"),
             pS.ap().rearrange("p a b -> p (a b)")]

    st3 = {}
    v_s12 = P.val("v"); s_s12 = P.val("s")
    dz_s12 = P.val("dz")
    def load_og(w):
        bb = w % 2
        P.wait("sync", "dz", dz_s12)
        P.wait("sync", "t", st3.get(f"at_t_{bb}", 0))
        P.op("sync", lambda e, w_=w, b_=bb: e.dma_start(out=ogl3[b_][:, :, :],
             in_=og_loc[w_, :, :].rearrange("(a p) t -> p a t", p=128)), inc=f"dy{bb}", by=16)
        st3[f"og_dy_{bb}"] = P.val(f"dy{bb}")
    load_og(0)
    for w in range(NW):
        bb = w % 2
        if w + 1 < NW:
            load_og(w + 1)
        P.wait("tensor", "dg", dg_wo3)
        P.wait("tensor", f"dy{bb}", st3[f"og_dy_{bb}"])
        P.wait("tensor", "v", st3.get("at_evac", v_s12))
        P.wait("tensor", "s", s_s12)
        P.pb("tensor", "pA", "pB", "pGAQ", "pS")
        for q in range(4):
            for kt in range(4):
                P.op("tensor", lambda e, q_=q, k_=kt, b_=bb: e.matmul(
                    attnb[q_], ogl3[b_][:, k_, :], wos3[:, k_, 512*q_:512*(q_+1)],
                    start=(k_ == 0), stop=(k_ == 3)),
                    inc="t" if (q == 3 and kt == 3) else None)
        st3[f"at_t_{bb}"] = P.val("t")
        P.pe("tensor", "t", "pA", "pB", "pGAQ", "pS")
        P.wait("vector", "t", P.val("t"))
        P.wait("vector", f"so{bb}", st3.get(f"at_so_{bb}", 0))
        P.pb("vector", "pA", "pB", "pGAQ", "pS")
        for q in range(4):
            P.op("vector", lambda e, q_=q, b_=bb: e.tensor_scalar(
                 out=atb3[b_][:, 512*q_:512*(q_+1)], in0=attnb[q_], scalar1=1.0,
                 scalar2=None, op0=ALU.mult, op1=ALU.bypass), inc="v", lp=True)
        st3["at_evac"] = P.val("v")
        P.pe("vector", "v", "pA", "pB", "pGAQ", "pS")
        P.wait("scalar", "v", P.val("v"))
        P.op("scalar", lambda e, w_=w, b_=bb: e.dma_start(
             out=attn_part[TW*w_:TW*(w_+1), :], in_=atb3[b_][:, :]), inc=f"so{bb}", by=16)
        st3[f"at_so_{bb}"] = P.val(f"so{bb}")

    P.wait("gpsimd", "so0"); P.wait("gpsimd", "so1")
    P.op("gpsimd", lambda e: e.collective_compute(
        "ReduceScatter", ALU.add,
        replica_groups=[[0, 1, 2, 3], [4, 5, 6, 7]],
        ins=[attn_part[:, :]], outs=[attn_q[:, :]]), inc="cc")

    at3 = [sb3(f"at3{i}", (128, D), BF16) for i in range(2)]
    x3 = [sb3(f"x3{i}", (128, D), F32) for i in range(2)]
    xmid = sb3("xmid", (128, D), F32)
    xf3 = sb3("xf3", (128, D), BF16)
    xff = sb3("xff", (128, NK, 128), BF16, is_transpose_dest=True)
    hmsk = sb3("hmsk", (128, 512), BF16)
    hfm = [sb3(f"hfm{j}", (128, 128), BF16, is_transpose_dest=True) for j in range(HP)]
    out3 = [sb3(f"out3{i}", (128, D), F32) for i in range(2)]
    sx2r = sb3("sx2r", (128, 1), F32)
    mur3 = sb3("mur3", (1, 128), BF16)
    mux3 = sb3("mux3", (128, 1), F32)
    nv3 = sb3("nv3", (128, 1), F32); rstd3 = sb3("rstd3", (128, 1), F32)
    mx8 = sb3("mx8", (128, 1), F32); nmx8 = sb3("nmx8", (128, 1), F32)
    z8 = sb3("z8", (128, 1), F32); wg8 = sb3("wg8", (128, 1), F32)
    ex8 = sb3("ex8", (128, 8), F32)
    msk8 = sb3("msk8", (128, 8), F32)

    g3_load(aws[:, :, :], ap["awr"].rearrange("(a p) n -> p a n", p=128))
    g3_load(bss[:, :, :], ap["bstk"][:, :, :])
    g3_load(brs[:, :], ap["brow"][:, :])
    g3_load(id3[:, :], c_ident[:, :])
    dg_w3 = P.val("dg")

    st3["free_v"] = P.val("v"); st3["free_s"] = P.val("s")
    def load3(j):
        b = j % 2
        P.wait("sync", "cc")
        P.wait("sync", "v", st3.get(f"v_xmid_{b}", 0))
        P.op("sync", lambda e, j_=j, b_=b: e.dma_start(out=at3[b_][:, :],
             in_=attn_q[TW*j_:TW*(j_+1), :]), inc=f"dy{b}", by=16)
        P.op("sync", lambda e, j_=j, b_=b: e.dma_start(out=x3[b_][:, :],
             in_=ap["x3s"][TW*j_:TW*(j_+1), :]), inc=f"dy{b}", by=16)
        st3[f"ld_dy_{b}"] = P.val(f"dy{b}")
    load3(0)

    for j in range(NT3):
        b = j % 2
        if j + 1 < NT3:
            load3(j + 1)
        P.wait("vector", f"dy{b}", st3[f"ld_dy_{b}"])
        P.wait("vector", "s", st3.get("xmid_s", 0))
        P.wait("vector", "v", st3.get("out_done_v", 0))
        for q in range(4):
            P.op("vector", lambda e, q_=q, b_=b: e.tensor_tensor(
                 out=xmid[:, 512*q_:512*(q_+1)], in0=at3[b_][:, 512*q_:512*(q_+1)],
                 in1=x3[b_][:, 512*q_:512*(q_+1)], op=ALU.add), inc="v")
        v_xmid = P.val("v")
        P.op("vector", lambda e: e.reduce_sum(out=mux3[:, :], in_=xmid[:, :], axis=AX.X), inc="v")
        P.op("vector", lambda e: e.tensor_scalar(out=mux3[:, :], in0=mux3[:, :],
             scalar1=1.0/D, scalar2=None, op0=ALU.mult, op1=ALU.bypass), inc="v")
        P.wait("scalar", "v", v_xmid)
        P.op("scalar", lambda e: e.activation(out=xf3[:, :], in_=xmid[:, :], func=AF.Square,
             scale=float(1.0/np.sqrt(D)), accum_out=sx2r[:, :]), inc="s")
        st3["xmid_s"] = P.val("s")
        P.wait("vector", "s")
        P.op("vector", lambda e: e.scalar_tensor_tensor(out=nv3[:, :], in0=mux3[:, :],
             scalar=mux3[:, :], in1=sx2r[:, :], op0=ALU.mult, op1=ALU.subtract), inc="v")
        P.wait("scalar", "v")
        P.op("scalar", lambda e: e.activation(out=lnv[:, :], in_=nv3[:, :], func=AF.Ln,
             bias=LNEPS, scale=-1.0), inc="s")
        P.op("scalar", lambda e: e.activation(out=rstd3[:, :], in_=lnv[:, :], func=AF.Exp,
             scale=-0.5), inc="s")
        P.wait("vector", "s")
        P.op("vector", lambda e: e.tensor_scalar(out=xf3[:, :], in0=xmid[:, :],
             scalar1=mux3[:, :], scalar2=rstd3[:, :], op0=ALU.subtract, op1=ALU.mult), inc="v")
        v_xf = P.val("v")
        P.wait("sync", "v", v_xf)
        for kt in range(NK):
            P.op("sync", lambda e, k_=kt: e.dma_start(out=xff[:, k_, :],
                 in_=xf3[:, 128*k_:128*(k_+1)], transpose=True), inc="dz", by=16)
        dz_xf = P.val("dz")
        P.wait("tensor", "v", P.val("v"))
        P.wait("tensor", "s", st3["free_s"])
        P.wait("tensor", "dz", dz_xf)
        P.pb("tensor", "pA", "pB")
        for kt in range(NK):
            P.op("tensor", lambda e, k_=kt: e.matmul(pA[:, 0:512], xff[:, k_, :],
                 aws[:, k_, 0:512], start=(k_ == 0), stop=(k_ == NK-1)))
            P.op("tensor", lambda e, k_=kt: e.matmul(pB[:, 0:8], xff[:, k_, :],
                 aws[:, k_, 512:520], start=(k_ == 0), stop=False))
        P.op("tensor", lambda e: e.matmul(pB[:, 0:8], o1r[:, :], brs[:, :],
             start=False, stop=True), inc="t")
        t_h = P.val("t")
        P.pe("tensor", "t", "pA", "pB")
        P.wait("vector", "t", t_h)
        P.pb("vector", "pB")
        P.op("vector", lambda e: e.reduce_max(out=mx8[:, :], in_=pB[:, 0:8], axis=AX.X), inc="v")
        P.op("vector", lambda e: e.tensor_scalar(out=nmx8[:, :], in0=mx8[:, :],
             scalar1=-1.0, scalar2=None, op0=ALU.mult, op1=ALU.bypass), inc="v")
        P.pe("vector", "v", "pB")
        P.wait("scalar", "v")
        P.pb("scalar", "pB")
        P.op("scalar", lambda e: e.activation(out=ex8[:, :], in_=pB[:, 0:8], func=AF.Exp,
             bias=nmx8[:, :], accum_out=z8[:, :]), inc="s")
        P.pe("scalar", "s", "pB")
        P.wait("vector", "s")
        P.op("vector", lambda e: e.reciprocal(out=wg8[:, :], in_=z8[:, :]), inc="v")
        P.op("vector", lambda e: e.tensor_scalar(out=wg8[:, :], in0=wg8[:, :],
             scalar1=1.0/64.0, scalar2=None, op0=ALU.mult, op1=ALU.bypass), inc="v")
        P.pb("vector", "pB")
        P.op("vector", lambda e: e.tensor_scalar(out=msk8[:, :], in0=pB[:, 0:8],
             scalar1=mx8[:, :], scalar2=None, op0=ALU.is_equal, op1=ALU.bypass), inc="v")
        P.pe("vector", "v", "pB")
        P.wait("vector", "t", t_h)
        P.pb("vector", "pA")
        for ex in range(8):
            P.op("vector", lambda e, x_=ex: e.tensor_scalar(out=hmsk[:, 64*x_:64*(x_+1)],
                 in0=pA[:, 64*x_:64*(x_+1)], scalar1=msk8[:, x_:x_+1], scalar2=None,
                 op0=ALU.mult, op1=ALU.bypass), inc="v")
        v_hm = P.val("v")
        P.pe("vector", "v", "pA")
        P.wait("sync", "v", v_hm)
        for jj in range(HP):
            P.op("sync", lambda e, j_=jj: e.dma_start(out=hfm[j_][:, :],
                 in_=hmsk[:, 128*j_:128*(j_+1)], transpose=True), inc="dz", by=16)
        dz_h = P.val("dz")
        P.wait("tensor", "v", v_hm)
        P.wait("tensor", "dz", dz_h)
        P.pb("tensor", "pA", "pB", "pGAQ", "pS")
        for jj in range(HP):
            for q in range(4):
                P.op("tensor", lambda e, j_=jj, q_=q: e.matmul(
                    attnb[q_], hfm[j_][:, :], bss[:, j_, 512*q_:512*(q_+1)],
                    start=(j_ == 0), stop=(j_ == HP-1)),
                    inc="t" if (jj == HP-1 and q == 3) else None)
        t_y = P.val("t")
        P.pe("tensor", "t", "pA", "pB", "pGAQ", "pS")
        P.wait("vector", "t", t_y)
        P.wait("vector", f"so{b}", st3.get(f"store_so_{b}", 0))
        P.pb("vector", "pA", "pB", "pGAQ", "pS")
        for q in range(4):
            P.op("vector", lambda e, q_=q, b_=b: e.scalar_tensor_tensor(
                 out=out3[b_][:, 512*q_:512*(q_+1)], in0=attnb[q_], scalar=wg8[:, :],
                 in1=xmid[:, 512*q_:512*(q_+1)], op0=ALU.mult, op1=ALU.add), inc="v")
        st3["free_v"] = P.val("v")
        st3["out_done_v"] = P.val("v")
        P.pe("vector", "v", "pA", "pB", "pGAQ", "pS")
        st3["free_s"] = P.val("s")
        v_out = P.val("v")
        P.wait("scalar", "v", v_out)
        P.op("scalar", lambda e, j_=j, b_=b: e.dma_start(out=ap["out"][TW*j_:TW*(j_+1), :],
             in_=out3[b_][:, :]), inc=f"so{b}", by=16)
        st3[f"store_so_{b}"] = P.val(f"so{b}")
        st3[f"v_xmid_{b}"] = v_xmid

    for s_ in ("dg", "dz", "dy0", "dy1", "dx0", "dx1", "so0", "so1"):
        P.wait("gpsimd", s_)
    finalize(nc, P)
    return nc, {"NW": NW}


def finalize(nc, P):
    with contextlib.ExitStack() as semstack:
        for name in P.cnt:
            P.sems[name] = semstack.enter_context(nc.semaphore(name))
        with nc.Block() as block:
            @block.tensor
            def _(tensor):
                for fn in P.ops["tensor"]:
                    fn(tensor)
            @block.vector
            def _(vector):
                for fn in P.ops["vector"]:
                    fn(vector)
            @block.scalar
            def _(scalar):
                for fn in P.ops["scalar"]:
                    fn(scalar)
            @block.sync
            def _(sync):
                for fn in P.ops["sync"]:
                    fn(sync)
            @block.gpsimd
            def _(gpsimd):
                for fn in P.ops["gpsimd"]:
                    fn(gpsimd)


def prep_core_inputs(core, x, Wq, Wk, Wv, Wo, Wb, ln_w, ln_b, Wg1, Wg2, norm_w,
                     A, Bw, Wr, br):
    b = core // 4
    hg = core % 4
    cs = slice(512 * hg, 512 * (hg + 1))
    hs = slice(4 * hg, 4 * (hg + 1))
    T = x.shape[1]
    f32 = np.float32
    xb = np.ascontiguousarray(x[b]).astype(f32, copy=False)
    xbT = np.ascontiguousarray(x[b].T)
    lw = ln_w[hs]
    lb = ln_b[hs]
    wstk = np.zeros((128, 128), f32)
    for p in range(4):
        wstk[32*p:32*p+16, :] = lw[p][None, :]
    woln = np.broadcast_to(lw[None, :, :], (128, 4, 128)).copy()
    boln = np.broadcast_to(lb.reshape(1, 512), (128, 512)).copy()
    wnat = np.broadcast_to(lw.reshape(1, 512), (128, 512)).copy()
    nrm = np.broadcast_to(norm_w[cs][None, :], (128, 512)).copy()
    awr = np.concatenate([A.transpose(2, 0, 1).reshape(2048, 512), Wr.T], axis=1)
    bstk = np.zeros((128, 4, 2048), f32)
    for jj in range(4):
        bstk[0:64, jj, :] = Bw[2*jj].T
        bstk[64:128, jj, :] = Bw[2*jj+1].T
    q3 = core % 4
    N3 = T // 4
    x3s = np.ascontiguousarray(x[b][N3*q3:N3*(q3+1)])
    return {
        "xb": xb, "xbT": xbT,
        "wq": np.ascontiguousarray(Wq[:, cs]), "wk": np.ascontiguousarray(Wk[:, cs]),
        "wv": np.ascontiguousarray(Wv[:, cs]),
        "wbeta": np.ascontiguousarray(Wb[:, hs]),
        "lnw": np.ascontiguousarray(lw[None]), "lnb": np.ascontiguousarray(lb[None]),
        "wstk": wstk, "woln": woln, "boln": boln, "wnat": wnat, "nrm": nrm,
        "wg1": np.ascontiguousarray(Wg1), "wg2": np.ascontiguousarray(Wg2[:, cs]),
        "wo": np.ascontiguousarray(Wo[cs, :]), "awr": np.ascontiguousarray(awr),
        "bstk": bstk, "brow": np.ascontiguousarray(br[None, :]).astype(f32),
        "x3s": x3s,
    }


T_FULL = 8192
N3_FULL = T_FULL // 4

_cache = {}

def kernel(L0_attn_x, L0_attn_Wq, L0_attn_Wk, L0_attn_Wv, L0_attn_Wo,
           L0_attn_Wbeta, L0_attn_ln_w, L0_attn_ln_b, L0_attn_Wg1, L0_attn_Wg2,
           L0_attn_norm_w, L0_ffn_A, L0_ffn_B, L0_ffn_Wr, L0_ffn_br):
    from concourse.bass_utils import run_bass_kernel_spmd
    x = np.asarray(L0_attn_x, np.float32)
    args = [np.asarray(a, np.float32) for a in
            (L0_attn_Wq, L0_attn_Wk, L0_attn_Wv, L0_attn_Wo, L0_attn_Wbeta,
             L0_attn_ln_w, L0_attn_ln_b, L0_attn_Wg1, L0_attn_Wg2, L0_attn_norm_w,
             L0_ffn_A, L0_ffn_B, L0_ffn_Wr, L0_ffn_br)]
    ims = [prep_core_inputs(c, x, *args) for c in range(8)]
    if "nc" not in _cache:
        _cache["nc"], _ = build(T_FULL, debug_og=False)
    res = run_bass_kernel_spmd(_cache["nc"], ims, core_ids=list(range(8)))
    out = np.empty((2, T_FULL, 2048), np.float32)
    for c in range(8):
        b, q3 = c // 4, c % 4
        out[b, N3_FULL*q3:N3_FULL*(q3+1)] = res.results[c]["out"]
    return out


# version 02
# speedup vs baseline: 1.0115x; 1.0115x over previous
import numpy as np
import ml_dtypes
import contextlib
import concourse.bass as bass
import concourse.mybir as mybir

F32 = mybir.dt.float32
BF16 = mybir.dt.bfloat16
AF = mybir.ActivationFunctionType
ALU = mybir.AluOpType
AX = mybir.AxisListType
BF = ml_dtypes.bfloat16

CHUNK = 16
W = 8
TW = 128
HP = 4
Dh = 128
D = 2048
NK = 16
EPS = 1e-6
LNEPS = 1e-5
IRT128 = float(np.sqrt(1.0 / 128.0))


class Prog:
    ENGS = ("tensor", "vector", "scalar", "sync", "gpsimd")

    def __init__(self):
        self.ops = {e: [] for e in self.ENGS}
        self.cnt = {}
        self.sems = {}

    def declare(self, *names):
        for n in names:
            self.cnt[n] = 0

    def wait(self, eng, sem, val=None):
        if val is None:
            val = self.cnt[sem]
        if val <= 0:
            return
        self.ops[eng].append(lambda e, s=sem, v=val: e.wait_ge(self.sems[s], v))

    def op(self, eng, build, inc=None, by=1, lp=False):
        if eng == "vector":
            self.wait(eng, "v")
        elif eng == "scalar":
            self.wait(eng, "s")
        if inc is not None:
            self.cnt[inc] += by
        def fn(e, build=build, inc=inc, by=by, lp=lp):
            if lp:
                with self.nc.allow_low_precision(reason="bf16 sum columns by design"):
                    ins = build(e)
            else:
                ins = build(e)
            if inc is not None:
                ins.then_inc(self.sems[inc], by)
        self.ops[eng].append(fn)

    def val(self, sem):
        return self.cnt[sem]

    def pb(self, eng, *banks):
        if not hasattr(self, "bankmarks"):
            self.bankmarks = {}
        for b in banks:
            m = self.bankmarks.get(b)
            if m and m[0] != eng:
                self.wait(eng, m[1], m[2])

    def pe(self, eng, sem, *banks):
        if not hasattr(self, "bankmarks"):
            self.bankmarks = {}
        for b in banks:
            self.bankmarks[b] = (eng, sem, self.cnt[sem])


def host_consts():
    ident = np.eye(128, dtype=BF)
    esel = np.zeros((128, W, 16), dtype=BF)
    for c in range(W):
        for t in range(16):
            esel[16 * c + t, c, t] = -1.0
    triu = np.triu(np.ones((128, 128), np.float32)).astype(BF)
    onesb = np.ones((128, 1), dtype=BF)
    ones116 = np.ones((1, 16), dtype=BF)
    ones1r = np.ones((1, 128), dtype=BF)
    selh = np.zeros((2, HP, 128, 32), dtype=BF)
    for h in range(2):
        for p in range(HP):
            for t in range(16):
                selh[h, p, 32 * p + t, 16 * h + t] = 1.0
    return ident, esel, triu, onesb, ones116, ones1r, selh


def build(T, debug_og=False, stages=99):
    NW = T // TW
    NSC = W // 2
    nc = bass.Bass("TRN2", target_bir_lowering=False)
    P = Prog()
    P.nc = nc
    P.declare("t", "v", "s", "dy0", "dy1", "dx0", "dx1", "so0", "so1", "dg", "dz", "cc")

    ap = {}
    def inp(name, shape, dt=F32):
        ap[name] = nc.dram_tensor(name, list(shape), dt, kind="ExternalInput").ap()
    inp("xb", (T, D)); inp("xbT", (D, T))
    inp("wq", (D, 512)); inp("wk", (D, 512)); inp("wv", (D, 512))
    inp("wbeta", (D, HP))
    inp("lnw", (1, HP, Dh)); inp("lnb", (1, HP, Dh))
    inp("wstk", (128, 128))
    inp("woln", (128, HP, Dh)); inp("boln", (128, HP * Dh))
    inp("wnat", (128, HP * Dh)); inp("nrm", (128, HP * Dh))
    inp("wg1", (D, 128)); inp("wg2", (128, 512))
    inp("wo", (512, D)); inp("awr", (D, 520)); inp("bstk", (128, HP, D))
    inp("brow", (1, 8))
    N3 = T // 4
    inp("x3s", (N3, D))
    if debug_og:
        ap["out"] = nc.dram_tensor("out", [NW, 512, 128], F32, kind="ExternalOutput").ap()
    else:
        ap["out"] = nc.dram_tensor("out", [N3, D], F32, kind="ExternalOutput").ap()

    og_loc = nc.dram_tensor("og_loc", [NW, 512, 128], BF16, kind="Internal").ap()
    attn_part = nc.dram_tensor("attn_part", [T, D], BF16, kind="Internal").ap()
    attn_q = nc.dram_tensor("attn_q", [N3, D], BF16, kind="Internal").ap()

    identN, eselN, triuN, onesbN, o116N, o1rN, selhN = host_consts()
    c_ident = nc.inline_tensor(identN, "c_ident").ap()
    c_esel = nc.inline_tensor(eselN, "c_esel").ap()
    c_triu = nc.inline_tensor(triuN, "c_triu").ap()
    c_onesb = nc.inline_tensor(onesbN, "c_onesb").ap()
    c_o116 = nc.inline_tensor(o116N, "c_o116").ap()
    c_o1r = nc.inline_tensor(o1rN, "c_o1r").ap()
    c_one1 = nc.inline_tensor(np.ones((1, 1), np.float32), "c_one1").ap()
    c_zer = nc.inline_tensor(np.zeros((128, 128), BF), "c_zer").ap()
    _cvals = [LNEPS, EPS, -1.0, -0.5, IRT128, float(1.0/np.sqrt(D)), -2.0, 1.0]
    c_cvals = nc.inline_tensor(
        np.broadcast_to(np.asarray(_cvals, np.float32)[None, :], (128, len(_cvals))).copy(),
        "c_cvals").ap()
    c_selh = nc.inline_tensor(selhN.reshape(2 * HP * 128, 32), "c_selh").ap()

    outer = contextlib.ExitStack()
    s12 = contextlib.ExitStack()
    def sbo(name, shape, dt, **kw): return outer.enter_context(nc.sbuf_tensor(name, list(shape), dt, **kw))
    def sb1(name, shape, dt, **kw): return s12.enter_context(nc.sbuf_tensor(name, list(shape), dt, **kw))
    def pso(name, shape): return outer.enter_context(nc.psum_tensor(name, list(shape), F32))

    jk = sbo("jk", (128, 512), F32)
    wos3 = sbo("wos3", (128, 4, D), BF16)
    atb3 = [sbo(f"atb3{i}", (128, D), BF16) for i in range(2)]
    aws = sbo("aws", (128, NK, 520), BF16)
    bss = sbo("bss", (128, HP, D), BF16)
    brs = sbo("brs", (1, 8), BF16)
    id3 = sbo("id3", (128, 128), BF16)
    cvals = sbo("cvals", (128, 8), F32)
    for _i, _v in enumerate(_cvals):
        nc.const_aps.register(F32, float(_v), cvals[:, _i:_i+1])
    jkb = sbo("jkb", (128, 128), BF16)
    ogfm = [sbo(f"ogfm{h}", (128, 128), BF16, is_transpose_dest=True) for h in range(HP)]
    on1 = sbo("on1", (128, 1), BF16)
    o1r = sbo("o1r", (1, 128), BF16)
    zer = sbo("zer", (128, 128), BF16)
    lnv = sbo("lnv", (128, 1), F32)
    pA = pso("pA", (128, 512)); pB = pso("pB", (128, 512))
    pKH = pso("pKH", (128, 2, 256))
    pS = pso("pS", (128, HP, 128))
    pGAQ = pso("pGAQ", (128, HP, 128))
    pWBT = pso("pWBT", (128, 2, 256))
    pVnA = pso("pVnA", (128, 2, 256)); pVnB = pso("pVnB", (128, 2, 256))
    pVn = [pVnA, pVnB]

    wqs = sb1("wqs", (128, NK, 512), BF16); wks = sb1("wks", (128, NK, 512), BF16)
    wvs = sb1("wvs", (128, NK, 512), BF16)
    wbs = sb1("wbs", (128, NK, HP), BF16); wg1s = sb1("wg1s", (128, NK, 128), BF16)
    wg2s = sb1("wg2s", (128, 512), BF16)
    idb = sb1("idb", (128, 128), BF16); esl = sb1("esl", (128, W, 16), BF16)
    tri = sb1("tri", (128, 128), BF16)
    o116 = sb1("o116", (1, 16), BF16)
    slh = sb1("slh", (128, 2, HP, 32), BF16)
    lnwr = sb1("lnwr", (1, HP, Dh), F32); lnbr = sb1("lnbr", (1, HP, Dh), F32)
    wbrow = sb1("wbrow", (1, HP, Dh), BF16)
    wstks = sb1("wstks", (128, 128), F32); w2sb = sb1("w2sb", (128, 128), BF16)
    wolnb = sb1("wolnb", (128, HP, Dh), BF16)
    bolnn = sb1("bolnn", (128, 512), F32)
    wnatb = sb1("wnatb", (128, 512), BF16); nrmb = sb1("nrmb", (128, 512), BF16)
    csw = sb1("csw", (1, 1668), BF16)
    xtm = [sb1(f"xtm{i}", (128, D), F32) for i in range(2)]
    xfm = [sb1(f"xfm{i}", (128, NK, 128), BF16) for i in range(2)]
    qtm = sb1("qtm", (128, 512), BF16); ktm = sb1("ktm", (128, 512), BF16)
    vtm = sb1("vtm", (128, 512), BF16); kie = sb1("kie", (128, 512), BF16)
    wtg = sb1("wtg", (128, 512), BF16)
    sgm = sb1("sgm", (128, 512), BF16); sg2 = sb1("sg2", (128, 512), F32)
    rec = sb1("rec", (128, 512), F32)
    qfm = [sb1(f"qfm{p}", (128, 128), BF16, is_transpose_dest=True) for p in range(HP)]
    kfm = [sb1(f"kfm{p}", (128, 128), BF16, is_transpose_dest=True) for p in range(HP)]
    kefm = [sb1(f"kefm{p}", (128, 128), BF16, is_transpose_dest=True) for p in range(HP)]
    xgt = sb1("xgt", (128, 128), BF16)
    xgf = sb1("xgf", (128, 128), BF16, is_transpose_dest=True)
    gsb = sb1("gsb", (128, HP, 128), BF16)
    gstke = sb1("gstke", (128, NSC, 16), BF16)
    asb = [sb1(f"asb{p}", (128, 128), BF16) for p in range(HP)]
    s0 = sb1("s0", (128, HP, 130), BF16)
    vnstk = sb1("vnstk", (128, W, 132), BF16)
    vnnat = sb1("vnnat", (128, HP, 130), BF16)
    khh = sb1("khh", (128, 128), BF16); t1b = sb1("t1b", (128, 128), BF16)
    gwb = sb1("gwb", (128, 128), BF16); xts = sb1("xts", (128, 128), BF16)
    mus = sb1("mus", (128, 1), F32); nv = sb1("nv", (128, 1), F32)
    rsd = sb1("rsd", (128, 1), F32)
    s2R = sb1("s2R", (128, 1), F32); mm12 = sb1("mm12", (128, 2), F32)
    m12r = sb1("m12r", (128, 2), F32)
    murp = sb1("murp", (1, 128), F32); murn = sb1("murn", (1, 128), BF16)
    one1 = sb1("one1", (1, 1), F32)
    sx2 = sb1("sx2", (128, 1), F32); mux = sb1("mux", (128, 1), F32)
    nvx = sb1("nvx", (128, 1), F32); rstdx = sb1("rstdx", (128, 1), F32)
    ngrx = sb1("ngrx", (128, 1), F32)
    rk2 = sb1("rk2", (128, HP, 1), F32); rkl = sb1("rkl", (128, HP, 1), F32)
    rstdk = sb1("rstdk", (128, HP, 1), F32)
    ebv = sb1("ebv", (128, HP), BF16); dbv = sb1("dbv", (128, HP), F32)
    rbv = sb1("rbv", (128, HP), F32); negb = sb1("negb", (128, HP), F32)
    qbv = sb1("qbv", (128, 512), BF16)
    muo = sb1("muo", (128, HP), F32); so2 = sb1("so2", (128, HP), F32)
    nvo = sb1("nvo", (128, HP), F32); rso = sb1("rso", (128, HP), F32)
    lno = sb1("lno", (128, HP), F32)
    qhh = sb1("qhh", (128, 128), BF16); obuf = sb1("obuf", (128, 128), BF16)
    onat = sb1("onat", (128, 512), BF16)
    ogb = sb1("ogb", (128, 512), BF16); ogc = sb1("ogc", (128, 512), BF16)
    ro2 = sb1("ro2", (128, HP, 1), F32); rol = sb1("rol", (128, HP, 1), F32)
    rsog = sb1("rsog", (128, HP, 1), F32)

    def g_load(dst, src):
        P.op("gpsimd", lambda e, d=dst, s=src: e.dma_start(out=d, in_=s), inc="dg", by=16)
    g_load(wqs[:, :, :], ap["wq"].rearrange("(a p) n -> p a n", p=128))
    g_load(wks[:, :, :], ap["wk"].rearrange("(a p) n -> p a n", p=128))
    g_load(wvs[:, :, :], ap["wv"].rearrange("(a p) n -> p a n", p=128))
    g_load(wbs[:, :, :], ap["wbeta"].rearrange("(a p) n -> p a n", p=128))
    g_load(wg1s[:, :, :], ap["wg1"].rearrange("(a p) n -> p a n", p=128))
    g_load(wg2s[:, :], ap["wg2"][:, :])
    g_load(idb[:, :], c_ident[:, :])
    g_load(esl[:, :, :], c_esel[:, :, :])
    g_load(tri[:, :], c_triu[:, :])
    g_load(on1[:, :], c_onesb[:, :])
    g_load(o116[:, :], c_o116[:, :])
    g_load(o1r[:, :], c_o1r[:, :])
    g_load(one1[:, :], c_one1[:, :])
    g_load(zer[:, :], c_zer[:, :])
    g_load(cvals[:, :], c_cvals[:, :])
    g_load(slh[:, :, :, :], c_selh.rearrange("(h q p) n -> p h q n", h=2, q=HP))
    g_load(wos3[:, :, :], ap["wo"].rearrange("(a p) n -> p a n", p=128))
    dg_wo3_val = P.val("dg")
    g_load(lnwr[:, :, :], ap["lnw"][:, :, :])
    g_load(lnbr[:, :, :], ap["lnb"][:, :, :])
    g_load(wstks[:, :], ap["wstk"][:, :])
    g_load(wolnb[:, :, :], ap["woln"][:, :, :])
    g_load(bolnn[:, :], ap["boln"][:, :])
    g_load(wnatb[:, :], ap["wnat"][:, :])
    g_load(nrmb[:, :], ap["nrm"][:, :])
    n_wload = P.val("dg")
    P.op("sync", lambda e: e.dma_start(out=xtm[0][:, :], in_=ap["xb"][0:128, :]), inc="dy0", by=16)
    P.op("gpsimd", lambda e: e.dma_start(out=xfm[0][:, :, :],
         in_=ap["xbT"].rearrange("(a p) t -> p a t", p=128)[:, :, 0:128]), inc="dx0", by=16)

    P.wait("vector", "dg", n_wload)
    P.op("vector", lambda e: e.tensor_tensor(out=wbrow[:, :, :], in0=lnwr[:, :, :],
         in1=lnbr[:, :, :], op=ALU.mult), inc="v")
    P.op("vector", lambda e: e.memset(s0[:, :, :], 0.0), inc="v")
    P.op("vector", lambda e: e.memset(vnstk[:, :, :], 0.0), inc="v")
    P.op("vector", lambda e: e.memset(vnnat[:, :, :], 0.0), inc="v")
    for p_ in (pKH[:, :, :], pWBT[:, :, :], pGAQ[:, :, :], pS[:, :, :],
               pVnA[:, :, :], pVnB[:, :, :], pA[:, :], pB[:, :]):
        P.op("vector", lambda e, pp=p_: e.memset(pp, 0.0), inc="v")
    v_init = P.val("v")
    P.pe("vector", "v", "pA", "pB", "pKH", "pS", "pGAQ", "pWBT", "pVnA", "pVnB")
    P.wait("scalar", "dg", n_wload)
    P.op("scalar", lambda e: e.activation(out=w2sb[:, :], in_=wstks[:, :], func=AF.Square), inc="s")
    P.wait("tensor", "dg", n_wload)
    P.wait("tensor", "v", v_init)
    for wsb, width, off in [(wqs, 512, 0), (wks, 512, 512), (wvs, 512, 1024),
                            (wbs, HP, 1536), (wg1s, 128, 1540)]:
        P.wait("tensor", "v")
        P.pb("tensor", "pA")
        for kt in range(NK):
            P.op("tensor", lambda e, w_=wsb, k_=kt, n_=width: e.matmul(
                pA[0:1, 0:n_], on1[:, :], w_[:, k_, 0:n_], start=(k_ == 0),
                stop=(k_ == NK - 1)), inc="t" if kt == NK - 1 else None)
        P.pe("tensor", "t", "pA")
        P.wait("vector", "t")
        P.pb("vector", "pA")
        P.op("vector", lambda e, n_=width, o_=off: e.tensor_scalar(
            out=csw[0:1, o_:o_ + n_], in0=pA[0:1, 0:n_], scalar1=1.0, scalar2=None,
            op0=ALU.mult, op1=ALU.bypass), inc="v")
        P.pe("vector", "v", "pA")
    P.pb("tensor", "pS")
    P.op("tensor", lambda e: e.matmul(pS.ap().rearrange("p a b -> p (a b)"), zer[:, :],
         wqs[:, 0, :], start=True, stop=True, skip_group_check=True), inc="t")
    P.pe("tensor", "t", "pS")
    P.wait("tensor", "v")

    st = {"buf": 0, "xtm_dy_0": 16, "xfm_dx_0": 16,
          "pA_free_s": 0, "pB_free_v": 0, "pB_free_s": 0,
          "gaq_free_v": 0, "s0_ready_v": 0, "s0_ready_s": 0,
          "pS_free_v": 0, "pS_free_s": 0}

    def stats_phase(wi):
        buf = st["buf"]
        P.wait("tensor", "dg", n_wload)
        P.wait("tensor", f"dx{buf}", st[f"xfm_dx_{buf}"])
        P.wait("tensor", "s", st["pA_free_s"])
        P.pb("tensor", "pA")
        for kt in range(NK):
            P.op("tensor", lambda e, k_=kt, b_=buf: e.matmul(
                pA[0:1, 0:128], on1[:, :], xfm[b_][:, k_, :], start=(k_ == 0),
                stop=(k_ == NK - 1)), inc="t" if kt == NK - 1 else None)
        P.pe("tensor", "t", "pA")
        P.wait("vector", "t")
        P.pb("vector", "pA")
        P.op("vector", lambda e: e.tensor_scalar(out=murp[:, :], in0=pA[0:1, 0:128],
             scalar1=1.0 / D, scalar2=None, op0=ALU.mult, op1=ALU.bypass), inc="v")
        P.op("vector", lambda e: e.tensor_scalar(out=murn[:, :], in0=pA[0:1, 0:128],
             scalar1=-1.0 / D, scalar2=None, op0=ALU.mult, op1=ALU.bypass), inc="v")
        v_mur = P.val("v")
        P.pe("vector", "v", "pA")
        P.wait("scalar", f"dy{buf}", st[f"xtm_dy_{buf}"])
        P.op("scalar", lambda e, b_=buf: e.activation(out=xtm[1 - b_][:, :], in_=xtm[b_][:, :],
             func=AF.Square, scale=float(1.0 / np.sqrt(D)), accum_out=sx2[:, :]), inc="s")
        st["sx2_s"] = P.val("s")
        P.wait("tensor", "v", v_mur)
        P.pb("tensor", "pKH")
        P.op("tensor", lambda e: e.transpose(pKH[:, 0, 131:132], murp[:, :], one1[:, :]), inc="t")
        P.pe("tensor", "t", "pKH")
        P.wait("vector", "t")
        P.pb("vector", "pKH")
        P.op("vector", lambda e: e.tensor_scalar(out=mux[:, :], in0=pKH[:, 0, 131:132],
             scalar1=1.0, scalar2=None, op0=ALU.mult, op1=ALU.bypass), inc="v")
        P.pe("vector", "v", "pKH")
        P.wait("vector", "s", st["sx2_s"])
        P.op("vector", lambda e: e.scalar_tensor_tensor(out=nvx[:, :], in0=mux[:, :],
             scalar=mux[:, :], in1=sx2[:, :], op0=ALU.mult, op1=ALU.subtract), inc="v")
        P.wait("scalar", "v")
        P.op("scalar", lambda e: e.activation(out=lnv[:, :], in_=nvx[:, :], func=AF.Ln,
             bias=LNEPS, scale=-1.0), inc="s")
        P.op("scalar", lambda e: e.activation(out=rstdx[:, :], in_=lnv[:, :], func=AF.Exp,
             scale=-0.5), inc="s")
        P.wait("vector", "s")
        P.op("vector", lambda e: e.tensor_scalar(out=ngrx[:, :], in0=rstdx[:, :],
             scalar1=-1.0, scalar2=None, op0=ALU.mult, op1=ALU.bypass), inc="v")

    def load_tile(wi):
        b = wi % 2
        P.wait("sync", "s", st.get("sx2_s", 0))
        P.op("sync", lambda e, w_=wi, b_=b: e.dma_start(out=xtm[b_][:, :],
             in_=ap["xb"][TW*w_:TW*(w_+1), :]), inc=f"dy{b}", by=16)
        st[f"xtm_dy_{b}"] = P.val(f"dy{b}")
        P.wait("gpsimd", "t", st.get(f"xfm_free_{b}", 0))
        P.op("gpsimd", lambda e, w_=wi, b_=b: e.dma_start(out=xfm[b_][:, :, :],
             in_=ap["xbT"].rearrange("(a p) t -> p a t", p=128)[:, :, TW*w_:TW*(w_+1)]),
             inc=f"dx{b}", by=16)
        st[f"xfm_dx_{b}"] = P.val(f"dx{b}")

    def proj_phase(wi):
        buf = st["buf"]
        P.wait("tensor", "v")
        P.pb("tensor", "pA")
        for kt in range(NK):
            P.op("tensor", lambda e, k_=kt, b_=buf: e.matmul(
                pA[:, 0:512], xfm[b_][:, k_, :], wqs[:, k_, :], start=(k_ == 0), stop=False))
        P.op("tensor", lambda e: e.matmul(pA[:, 0:512], murn[:, :], csw[0:1, 0:512],
             start=False, stop=True), inc="t")
        t_q = P.val("t")
        P.pe("tensor", "t", "pA")
        P.wait("tensor", "v", st["pB_free_v"]); P.wait("tensor", "s", st["pB_free_s"])
        P.pb("tensor", "pB")
        for kt in range(NK):
            P.op("tensor", lambda e, k_=kt, b_=buf: e.matmul(
                pB[:, 0:512], xfm[b_][:, k_, :], wks[:, k_, :], start=(k_ == 0), stop=False))
        P.op("tensor", lambda e: e.matmul(pB[:, 0:512], murn[:, :], csw[0:1, 512:1024],
             start=False, stop=True), inc="t")
        t_k = P.val("t")
        P.pe("tensor", "t", "pB")
        P.wait("scalar", "t", t_q)
        P.pb("scalar", "pA")
        P.op("scalar", lambda e: e.activation(out=sgm[:, :], in_=pA[:, 0:512], func=AF.Exp,
             scale=ngrx[:, :]), inc="s")
        P.pe("scalar", "s", "pA")
        P.wait("vector", "s")
        P.op("vector", lambda e: e.tensor_scalar(out=sg2[:, :], in0=sgm[:, :], scalar1=1.0,
             scalar2=None, op0=ALU.add, op1=ALU.bypass), inc="v")
        P.op("vector", lambda e: e.reciprocal(out=rec[:, :], in_=sg2[:, :]), inc="v")
        P.pb("vector", "pA")
        P.op("vector", lambda e: e.scalar_tensor_tensor(out=qtm[:, :], in0=pA[:, 0:512],
             scalar=rstdx[:, :], in1=rec[:, :], op0=ALU.mult, op1=ALU.mult), inc="v")
        v_q = P.val("v")
        P.pe("vector", "v", "pA")
        P.wait("tensor", "v", v_q)
        P.pb("tensor", "pA")
        for kt in range(NK):
            P.op("tensor", lambda e, k_=kt, b_=buf: e.matmul(
                pA[:, 0:512], xfm[b_][:, k_, :], wvs[:, k_, :], start=(k_ == 0), stop=False))
        P.op("tensor", lambda e: e.matmul(pA[:, 0:512], murn[:, :], csw[0:1, 1024:1536],
             start=False, stop=True), inc="t")
        t_v = P.val("t")
        P.pe("tensor", "t", "pA")
        P.wait("scalar", "t", t_k)
        P.pb("scalar", "pB")
        P.op("scalar", lambda e: e.activation(out=jk[:, :], in_=pB[:, 0:512], func=AF.Square,
             scale=rstdx[:, :]), inc="s")
        P.pe("scalar", "s", "pB")
        P.wait("vector", "s")
        P.op("vector", lambda e: e.reduce_sum(out=rk2[:, :, :],
             in_=jk.ap().rearrange("p (h e) -> p h e", h=HP), axis=AX.X), inc="v")
        P.wait("scalar", "v")
        P.op("scalar", lambda e: e.activation(out=rkl[:, :, :], in_=rk2[:, :, :], func=AF.Ln,
             bias=EPS), inc="s")
        P.op("scalar", lambda e: e.activation(out=rstdk[:, :, :], in_=rkl[:, :, :],
             func=AF.Exp, scale=-0.5), inc="s")
        P.wait("vector", "s")
        P.pb("vector", "pB")
        for h in range(HP):
            P.op("vector", lambda e, h_=h: e.tensor_scalar(out=ktm[:, 128*h_:128*(h_+1)],
                 in0=pB[:, 128*h_:128*(h_+1)], scalar1=rstdx[:, :], scalar2=rstdk[:, h_, :],
                 op0=ALU.mult, op1=ALU.mult), inc="v")
        v_k = P.val("v")
        P.pe("vector", "v", "pB")
        P.wait("tensor", "v", v_k)
        P.pb("tensor", "pB")
        for kt in range(NK):
            P.op("tensor", lambda e, k_=kt, b_=buf: e.matmul(
                pB[:, 0:HP], xfm[b_][:, k_, :], wbs[:, k_, :], start=(k_ == 0), stop=False))
        P.op("tensor", lambda e: e.matmul(pB[:, 0:HP], murn[:, :], csw[0:1, 1536:1536+HP],
             start=False, stop=True), inc="t")
        for kt in range(NK):
            P.op("tensor", lambda e, k_=kt, b_=buf: e.matmul(
                pB[:, HP:HP+128], xfm[b_][:, k_, :], wg1s[:, k_, :], start=(k_ == 0), stop=False))
        P.op("tensor", lambda e: e.matmul(pB[:, HP:HP+128], murn[:, :], csw[0:1, 1540:1668],
             start=False, stop=True), inc="t")
        t_misc = P.val("t")
        P.pe("tensor", "t", "pB")
        st[f"xfm_free_{buf}"] = t_misc
        P.wait("vector", "t", t_v)
        P.pb("vector", "pA")
        P.op("vector", lambda e: e.tensor_scalar(out=vtm[:, :], in0=pA[:, 0:512],
             scalar1=rstdx[:, :], scalar2=None, op0=ALU.mult, op1=ALU.bypass), inc="v")
        P.pe("vector", "v", "pA")
        P.op("vector", lambda e: e.tensor_tensor(out=wtg[:, :], in0=vtm[:, :], in1=ktm[:, :],
             op=ALU.subtract), inc="v")
        P.op("vector", lambda e: e.tensor_tensor(out=wtg[:, :], in0=wtg[:, :],
             in1=wnatb[:, :], op=ALU.mult), inc="v")
        st["v_wtg"] = P.val("v")
        P.wait("scalar", "t", t_misc)
        P.pb("scalar", "pB")
        P.op("scalar", lambda e: e.activation(out=ebv[:, :], in_=pB[:, 0:HP], func=AF.Exp,
             scale=ngrx[:, :]), inc="s")
        st["pB_free_s"] = P.val("s")
        P.pe("scalar", "s", "pB")
        P.wait("vector", "s")
        P.op("vector", lambda e: e.tensor_scalar(out=dbv[:, :], in0=ebv[:, :], scalar1=1.0,
             scalar2=None, op0=ALU.add, op1=ALU.bypass), inc="v")
        P.op("vector", lambda e: e.reciprocal(out=rbv[:, :], in_=dbv[:, :]), inc="v")
        P.op("vector", lambda e: e.tensor_scalar(out=negb[:, :], in0=rbv[:, :],
             scalar1=-2.0, scalar2=None, op0=ALU.mult, op1=ALU.bypass), inc="v")
        for h in range(HP):
            P.op("vector", lambda e, h_=h: e.tensor_scalar(out=kie[:, 128*h_:128*(h_+1)],
                 in0=ktm[:, 128*h_:128*(h_+1)], scalar1=negb[:, h_:h_+1], scalar2=None,
                 op0=ALU.mult, op1=ALU.bypass), inc="v")
        P.pb("vector", "pB")
        P.op("vector", lambda e: e.tensor_scalar(out=xgt[:, :], in0=pB[:, HP:HP+128],
             scalar1=rstdx[:, :], scalar2=None, op0=ALU.mult, op1=ALU.bypass), inc="v")
        st["pB_free_v"] = P.val("v")
        P.pe("vector", "v", "pB")
        P.op("vector", lambda e: e.tensor_tensor(out=qbv[:, :], in0=qtm[:, :],
             in1=bolnn[:, :], op=ALU.add), inc="v")
        v_all = P.val("v")
        P.wait("sync", "v", v_q)
        for p in range(HP):
            P.op("sync", lambda e, p_=p: e.dma_start(out=qfm[p_][:, :],
                 in_=qtm[:, 128*p_:128*(p_+1)], transpose=True), inc="dz", by=16)
        P.wait("sync", "v", v_k)
        for p in range(HP):
            P.op("sync", lambda e, p_=p: e.dma_start(out=kfm[p_][:, :],
                 in_=ktm[:, 128*p_:128*(p_+1)], transpose=True), inc="dz", by=16)
        P.wait("sync", "v", v_all)
        for p in range(HP):
            P.op("sync", lambda e, p_=p: e.dma_start(out=kefm[p_][:, :],
                 in_=kie[:, 128*p_:128*(p_+1)], transpose=True), inc="dz", by=16)
        P.op("sync", lambda e: e.dma_start(out=xgf[:, :], in_=xgt[:, :], transpose=True),
             inc="dz", by=16)

    def scan_phase(wi, debug):
        if stages < 1:
            return
        dz_fm = P.val("dz")
        P.wait("tensor", "v", st.get("pvn_evac_v", 0))
        P.wait("tensor", "s", st.get("pvn_evac_s", 0))
        P.pb("tensor", "pVnA", "pVnB")
        P.op("tensor", lambda e: e.matmul(pVnA.ap().rearrange("p a b -> p (a b)"),
             zer[:, :], wqs[:, 0, :], start=True, stop=True, skip_group_check=True))
        P.op("tensor", lambda e: e.matmul(pVnB.ap().rearrange("p a b -> p (a b)"),
             zer[:, :], wqs[:, 0, :], start=True, stop=True, skip_group_check=True), inc="t")
        P.pe("tensor", "t", "pVnA", "pVnB")
        P.wait("tensor", "dz", dz_fm)
        P.wait("tensor", "v", st["gaq_free_v"])
        P.pb("tensor", "pGAQ")
        for p in range(HP):
            P.op("tensor", lambda e, p_=p: e.matmul(pGAQ[:, p_, :], kefm[p_][:, :],
                 kfm[p_][:, :], start=True, stop=True), inc="t" if p == HP-1 else None)
        P.pe("tensor", "t", "pGAQ")
        P.wait("vector", "t")
        P.pb("vector", "pGAQ")
        P.op("vector", lambda e: e.tensor_scalar(out=gsb[:, :, :], in0=pGAQ[:, :, :],
             scalar1=1.0, scalar2=None, op0=ALU.mult, op1=ALU.bypass), inc="v")
        P.pe("vector", "v", "pGAQ")
        P.wait("tensor", "v")
        P.pb("tensor", "pGAQ")
        for p in range(HP):
            P.op("tensor", lambda e, p_=p: e.matmul(pGAQ[:, p_, :], kefm[p_][:, :],
                 qfm[p_][:, :], start=True, stop=True), inc="t" if p == HP-1 else None)
        P.pe("tensor", "t", "pGAQ")
        P.wait("vector", "t")
        P.pb("vector", "pGAQ")
        for p in range(HP):
            P.op("vector", lambda e, p_=p: e.tensor_tensor(out=asb[p_][:, :],
                 in0=pGAQ[:, p_, :], in1=tri[:, :], op=ALU.mult), inc="v")
        v_asb = P.val("v")
        P.pe("vector", "v", "pGAQ")
        P.wait("tensor", "v", v_asb)
        P.pb("tensor", "pGAQ")
        for k in range(NSC):
            for p in range(HP):
                P.op("tensor", lambda e, k_=k, p_=p: e.matmul(
                    pGAQ[32*p_:32*p_+16, k_, 0:16], kefm[p_][:, 32*k_:32*k_+16],
                    kfm[p_][:, 16*(2*k_+1):16*(2*k_+2)], start=True, stop=True,
                    tile_position=(0, 32*p_)),
                    inc="t" if (k == NSC-1 and p == HP-1) else None)
        P.pe("tensor", "t", "pGAQ")
        P.wait("vector", "t")
        P.pb("vector", "pGAQ")
        P.op("vector", lambda e: e.tensor_scalar(out=gstke[:, :, :], in0=pGAQ[:, 0:NSC, 0:16],
             scalar1=1.0, scalar2=None, op0=ALU.mult, op1=ALU.bypass), inc="v")
        st["gaq_free_v"] = P.val("v")
        P.pe("vector", "v", "pGAQ")
        wbt_ready = {}
        def wbt_mm(c):
            P.pb("tensor", "pWBT")
            for p in range(HP):
                P.op("tensor", lambda e, c_=c, p_=p: e.matmul(
                    pWBT[32*p_:32*p_+16, c_ % 2, 0:128], esl[:, c_, :],
                    wtg[:, 128*p_:128*(p_+1)], start=True, stop=False,
                    tile_position=(0, 32*p_)))
                P.op("tensor", lambda e, c_=c, p_=p: e.matmul(
                    pWBT[32*p_:32*p_+16, c_ % 2, 0:128], o116[:, :],
                    wbrow[0:1, p_, :], start=False, stop=True,
                    tile_position=(0, 32*p_)),
                    inc="t" if p == HP-1 else None)
            wbt_ready[c] = P.val("t")
            P.pe("tensor", "t", "pWBT")
        P.wait("tensor", "v", st["v_wtg"])
        wbt_mm(0); wbt_mm(1)

        if stages < 2:
            return
        chain_v = {}; evac_s = {}
        for c in range(W):
            k = c // 2
            if c >= 1:
                P.wait("tensor", "v", chain_v[c - 1])
                if c % 2 == 0 and k >= 1:
                    P.wait("tensor", "s", evac_s.get(k - 1, 0))
            else:
                P.wait("tensor", "v", st["s0_ready_v"])
                P.wait("tensor", "s", st["s0_ready_s"])
            P.pb("tensor", "pKH")
            for p in range(HP):
                last = (p == HP - 1)
                Kc = 32 * k
                if Kc > 0:
                    P.op("tensor", lambda e, c_=c, p_=p, K_=Kc: e.matmul(
                        pKH[32*p_:32*p_+16, c_ % 2, 0:129], gsb[0:K_, p_, 16*c_:16*(c_+1)],
                        vnnat[0:K_, p_, 0:129], start=True, stop=False,
                        tile_position=(0, 32*p_)))
                P.op("tensor", lambda e, c_=c, p_=p, st_=(Kc == 0): e.matmul(
                    pKH[32*p_:32*p_+16, c_ % 2, 0:129], kfm[p_][:, 16*c_:16*(c_+1)],
                    s0[:, p_, 0:129], start=st_, stop=(c_ % 2 == 0),
                    tile_position=(0, 32*p_)),
                    inc="t" if (last and c % 2 == 0) else None)
                if c % 2 == 1:
                    P.op("tensor", lambda e, c_=c, p_=p, k_=k: e.matmul(
                        pKH[32*p_:32*p_+16, c_ % 2, 0:129], gstke[32*p_:32*p_+16, k_, :],
                        vnstk[32*p_:32*p_+16, 2*k_, 0:129], start=False, stop=True,
                        tile_position=(32*p_, 32*p_)),
                        inc="t" if last else None)
            t_kh = P.val("t")
            P.pe("tensor", "t", "pKH")
            P.wait("vector", "t", t_kh)
            P.pb("vector", "pKH")
            P.op("vector", lambda e, c_=c: e.tensor_scalar(out=mus[:, :],
                 in0=pKH[:, c_ % 2, 128:129], scalar1=1.0/128.0, scalar2=None,
                 op0=ALU.mult, op1=ALU.bypass), inc="v")
            P.pe("vector", "v", "pKH")
            P.wait("scalar", "t", t_kh)
            P.pb("scalar", "pKH")
            P.op("scalar", lambda e, c_=c: e.activation(out=jkb[:, :], in_=pKH[:, c_ % 2, 0:128],
                 func=AF.Square, scale=IRT128, accum_out=s2R[:, :]), inc="s")
            s_sq = P.val("s")
            P.pe("scalar", "s", "pKH")
            P.wait("vector", "s", s_sq)
            P.op("vector", lambda e: e.scalar_tensor_tensor(out=nv[:, :], in0=mus[:, :],
                 scalar=mus[:, :], in1=s2R[:, :], op0=ALU.mult, op1=ALU.subtract), inc="v")
            P.wait("scalar", "v")
            P.op("scalar", lambda e: e.activation(out=lnv[:, :], in_=nv[:, :], func=AF.Ln,
                 bias=EPS, scale=-1.0), inc="s")
            P.op("scalar", lambda e: e.activation(out=rsd[:, :], in_=lnv[:, :], func=AF.Exp,
                 scale=-0.5), inc="s")
            P.wait("vector", "s")
            P.pb("vector", "pKH")
            P.op("vector", lambda e, c_=c: e.tensor_scalar(out=khh[:, :],
                 in0=pKH[:, c_ % 2, 0:128], scalar1=mus[:, :], scalar2=rsd[:, :],
                 op0=ALU.subtract, op1=ALU.mult), inc="v")
            P.pe("vector", "v", "pKH")
            P.op("vector", lambda e: e.tensor_tensor(out=t1b[:, :], in0=khh[:, :],
                 in1=w2sb[:, :], op=ALU.mult), inc="v")
            P.pb("vector", "pWBT")
            P.op("vector", lambda e, c_=c: e.scalar_tensor_tensor(out=gwb[:, :],
                 in0=t1b[:, :], scalar=0.0, in1=pWBT[:, c_ % 2, 0:128], op0=ALU.add,
                 op1=ALU.add, accum_out=mm12[:, 0:1]), inc="v")
            P.pe("vector", "v", "pWBT")
            P.op("vector", lambda e: e.scalar_tensor_tensor(out=jkb[:, :], in0=gwb[:, :],
                 scalar=1.0, in1=khh[:, :], op0=ALU.mult, op1=ALU.mult,
                 accum_out=mm12[:, 1:2]), inc="v")
            P.op("vector", lambda e: e.tensor_scalar(out=m12r[:, :], in0=mm12[:, :],
                 scalar1=rsd[:, :], scalar2=1.0/128.0, op0=ALU.mult, op1=ALU.mult), inc="v")
            P.op("vector", lambda e: e.tensor_scalar(out=xts[:, :], in0=khh[:, :],
                 scalar1=m12r[:, 1:2], scalar2=m12r[:, 0:1], op0=ALU.mult, op1=ALU.add), inc="v")
            P.op("vector", lambda e, c_=c: e.scalar_tensor_tensor(out=vnstk[:, c_, 0:128],
                 in0=gwb[:, :], scalar=rsd[:, :], in1=xts[:, :], op0=ALU.mult,
                 op1=ALU.subtract, accum_out=vnstk[:, c_, 128:129]), inc="v", lp=True)
            chain_v[c] = P.val("v")
            if c + 2 < W:
                P.wait("tensor", "v", chain_v[c])
                wbt_mm(c + 2)
            P.wait("tensor", "v", chain_v[c])
            P.pb("tensor", "pVnA", "pVnB")
            for p in range(HP):
                P.op("tensor", lambda e, c_=c, p_=p, k_=k: e.matmul(
                    pVn[p_ // 2][32*k_:32*k_+32, p_ % 2, 0:129],
                    slh[:, c_ % 2, p_, :], vnstk[:, c_, 0:129],
                    start=False, stop=(c_ % 2 == 1),
                    tile_position=(0, 32*k_), skip_group_check=True),
                    inc="t" if p == HP-1 else None)
            t_rs = P.val("t")
            P.pe("tensor", "t", "pVnA", "pVnB")
            if c % 2 == 1:
                P.wait("vector", "t", t_rs)
                P.pb("vector", "pVnA", "pVnB")
                P.op("vector", lambda e, k_=k: e.tensor_scalar(
                     out=vnnat[32*k_:32*k_+32, 0:2, 0:129], in0=pVnA[32*k_:32*k_+32, :, 0:129],
                     scalar1=1.0, scalar2=None, op0=ALU.mult, op1=ALU.bypass), inc="v")
                P.op("vector", lambda e, k_=k: e.tensor_scalar(
                     out=vnnat[32*k_:32*k_+32, 2:4, 0:129], in0=pVnB[32*k_:32*k_+32, :, 0:129],
                     scalar1=1.0, scalar2=None, op0=ALU.mult, op1=ALU.bypass), inc="v")
                chain_v[c] = P.val("v")
                st["pvn_evac_v"] = P.val("v")
                P.pe("vector", "v", "pVnA", "pVnB")
        v_chain = P.val("v"); s_chain = P.val("s")

        P.wait("tensor", "v", v_chain)
        P.wait("tensor", "s", s_chain)
        P.pb("tensor", "pGAQ", "pWBT")
        for p in range(HP):
            P.op("tensor", lambda e, p_=p: e.matmul(pGAQ[:, p_, 0:128], qfm[p_][:, :],
                 s0[:, p_, 0:128], start=True, stop=False))
            P.op("tensor", lambda e, p_=p: e.matmul(pWBT[:, 0, 128+p_:129+p_], qfm[p_][:, :],
                 s0[:, p_, 128:129], start=True, stop=False))
            P.op("tensor", lambda e, p_=p: e.matmul(pGAQ[:, p_, 0:128], asb[p_][:, :],
                 vnnat[:, p_, 0:128], start=False, stop=True))
            P.op("tensor", lambda e, p_=p: e.matmul(pWBT[:, 0, 128+p_:129+p_], asb[p_][:, :],
                 vnnat[:, p_, 128:129], start=False, stop=True),
                 inc="t" if p == HP-1 else None)
        t_qh = P.val("t")
        P.pe("tensor", "t", "pGAQ", "pWBT")
        P.pb("tensor", "pS")
        for p in range(HP):
            P.op("tensor", lambda e, p_=p, w_=wi: e.matmul(pS[:, p_, :],
                 kie[:, 128*p_:128*(p_+1)], vnnat[:, p_, 0:128],
                 start=False, stop=(w_ == NW - 1),
                 skip_group_check=True), inc="t" if p == HP-1 else None)
        t_ds = P.val("t")
        P.pe("tensor", "t", "pS")
        P.wait("vector", "t", t_ds)
        P.pb("vector", "pS")
        for p in range(HP):
            P.op("vector", lambda e, p_=p: e.tensor_scalar(out=s0[:, p_, 0:128],
                 in0=pS[:, p_, :], scalar1=1.0, scalar2=None, op0=ALU.mult,
                 op1=ALU.add, accum_out=s0[:, p_, 128:129]), inc="v", lp=True)
        st["pS_free_v"] = P.val("v"); st["s0_ready_v"] = P.val("v")
        P.pe("vector", "v", "pS")
        st["pS_free_s"] = 0; st["s0_ready_s"] = 0

        if stages < 4:
            return
        P.wait("vector", "t", t_qh)
        P.pb("vector", "pWBT")
        P.op("vector", lambda e: e.tensor_scalar(out=muo[:, :], in0=pWBT[:, 0, 128:132],
             scalar1=1.0/128.0, scalar2=None, op0=ALU.mult, op1=ALU.bypass), inc="v")
        P.pe("vector", "v", "pWBT")
        P.wait("scalar", "t", t_qh)
        P.pb("scalar", "pGAQ")
        for p in range(HP):
            P.op("scalar", lambda e, p_=p: e.activation(out=jkb[:, :], in_=pGAQ[:, p_, 0:128],
                 func=AF.Square, scale=IRT128, accum_out=so2[:, p_:p_+1]), inc="s")
        P.pe("scalar", "s", "pGAQ")
        P.wait("vector", "s")
        for p in range(HP):
            P.op("vector", lambda e, p_=p: e.scalar_tensor_tensor(out=nvo[:, p_:p_+1],
                 in0=muo[:, p_:p_+1], scalar=muo[:, p_:p_+1], in1=so2[:, p_:p_+1],
                 op0=ALU.mult, op1=ALU.subtract), inc="v")
        P.wait("scalar", "v")
        P.op("scalar", lambda e: e.activation(out=lno[:, :], in_=nvo[:, :], func=AF.Ln,
             bias=EPS, scale=-1.0), inc="s")
        P.op("scalar", lambda e: e.activation(out=rso[:, :], in_=lno[:, :], func=AF.Exp,
             scale=-0.5), inc="s")
        P.wait("vector", "s")
        P.pb("vector", "pGAQ")
        for p in range(HP):
            P.op("vector", lambda e, p_=p: e.tensor_scalar(out=qhh[:, :],
                 in0=pGAQ[:, p_, 0:128], scalar1=muo[:, p_:p_+1], scalar2=rso[:, p_:p_+1],
                 op0=ALU.subtract, op1=ALU.mult), inc="v")
            P.op("vector", lambda e, p_=p: e.tensor_tensor(out=obuf[:, :], in0=qhh[:, :],
                 in1=wolnb[:, p_, :], op=ALU.mult), inc="v")
            P.op("vector", lambda e, p_=p: e.tensor_tensor(out=onat[:, 128*p_:128*(p_+1)],
                 in0=obuf[:, :], in1=qbv[:, 128*p_:128*(p_+1)], op=ALU.add), inc="v")
        st["gaq_free_v"] = P.val("v")
        v_onat = P.val("v")
        P.pe("vector", "v", "pGAQ")

        P.wait("tensor", "dz", dz_fm)
        P.wait("tensor", "s", st["pA_free_s"])
        P.wait("tensor", "v", v_onat)
        P.pb("tensor", "pA")
        P.op("tensor", lambda e: e.matmul(pA[:, 0:512], xgf[:, :], wg2s[:, :],
             start=True, stop=True), inc="t")
        t_gate = P.val("t")
        P.pe("tensor", "t", "pA")
        P.wait("scalar", "t", t_gate)
        P.pb("scalar", "pA")
        P.op("scalar", lambda e: e.activation(out=sgm[:, :], in_=pA[:, 0:512], func=AF.Exp,
             scale=-1.0), inc="s")
        st["pA_free_s"] = P.val("s")
        P.pe("scalar", "s", "pA")
        P.wait("vector", "s")
        P.op("vector", lambda e: e.tensor_scalar(out=sg2[:, :], in0=sgm[:, :], scalar1=1.0,
             scalar2=None, op0=ALU.add, op1=ALU.bypass), inc="v")
        P.op("vector", lambda e: e.reciprocal(out=rec[:, :], in_=sg2[:, :]), inc="v")
        P.op("vector", lambda e: e.tensor_tensor(out=ogb[:, :], in0=onat[:, :], in1=rec[:, :],
             op=ALU.mult), inc="v")
        P.wait("scalar", "v")
        P.op("scalar", lambda e: e.activation(out=jk[:, :], in_=ogb[:, :], func=AF.Square,
             scale=IRT128), inc="s")
        P.wait("vector", "s")
        P.op("vector", lambda e: e.reduce_sum(out=ro2[:, :, :],
             in_=jk.ap().rearrange("p (h e) -> p h e", h=HP), axis=AX.X), inc="v")
        P.wait("scalar", "v")
        P.op("scalar", lambda e: e.activation(out=rol[:, :, :], in_=ro2[:, :, :], func=AF.Ln,
             bias=EPS), inc="s")
        P.op("scalar", lambda e: e.activation(out=rsog[:, :, :], in_=rol[:, :, :],
             func=AF.Exp, scale=-0.5), inc="s")
        P.wait("vector", "s")
        for h in range(HP):
            P.op("vector", lambda e, h_=h: e.tensor_scalar(out=ogc[:, 128*h_:128*(h_+1)],
                 in0=ogb[:, 128*h_:128*(h_+1)], scalar1=rsog[:, h_, :], scalar2=None,
                 op0=ALU.mult, op1=ALU.bypass), inc="v")
        P.op("vector", lambda e: e.tensor_tensor(out=ogc[:, :], in0=ogc[:, :], in1=nrmb[:, :],
             op=ALU.mult), inc="v")
        v_ogc = P.val("v")
        P.wait("sync", "v", v_ogc)
        P.wait("sync", "dz")
        P.wait("sync", "t", st.get("at_t", 0))
        if debug:
            P.wait("sync", "dg")
        for h in range(HP):
            P.op("sync", lambda e, h_=h: e.dma_start(out=ogfm[h_][:, :],
                 in_=ogc[:, 128*h_:128*(h_+1)], transpose=True), inc="dz", by=16)
        dz_og = P.val("dz")
        if debug:
            P.wait("scalar", "dz", dz_og)
            for h in range(HP):
                P.op("scalar", lambda e, h_=h, w_=wi: e.dma_start(
                     out=og_loc[w_, 128*h_:128*(h_+1), :], in_=ogfm[h_][:, :]), inc="dz", by=16)
            P.wait("gpsimd", "dz", dz_og)
            for h in range(HP):
                P.op("gpsimd", lambda e, h_=h, w_=wi: e.dma_start(
                     out=ap["out"][w_, 128*h_:128*(h_+1), :], in_=ogfm[h_][:, :]),
                     inc="dg", by=16)
            return
        bb = wi % 2
        P.wait("tensor", "dz", dz_og)
        if wi == 0:
            P.wait("tensor", "dg", dg_wo3_val)
        P.wait("tensor", "v", P.val("v"))
        P.wait("tensor", "s", P.val("s"))
        P.pb("tensor", "pA", "pB", "pKH", "pWBT")
        ab = [pA[:, 0:512], pB[:, 0:512],
              pKH.ap().rearrange("p a b -> p (a b)"),
              pWBT.ap().rearrange("p a b -> p (a b)")]
        for q in range(4):
            for h in range(HP):
                P.op("tensor", lambda e, q_=q, h_=h: e.matmul(
                    ab[q_], ogfm[h_][:, :], wos3[:, h_, 512*q_:512*(q_+1)],
                    start=(h_ == 0), stop=(h_ == HP-1)),
                    inc="t" if (q == 3 and h == HP-1) else None)
        st["at_t"] = P.val("t")
        P.pe("tensor", "t", "pA", "pB", "pKH", "pWBT")
        P.wait("vector", "t", P.val("t"))
        P.wait("vector", f"so{bb}", st.get(f"at_so_{bb}", 0))
        P.pb("vector", "pA", "pB", "pKH", "pWBT")
        for q in range(4):
            P.op("vector", lambda e, q_=q, b_=bb, a_=ab[q]: e.tensor_scalar(
                 out=atb3[b_][:, 512*q_:512*(q_+1)], in0=a_, scalar1=1.0,
                 scalar2=None, op0=ALU.mult, op1=ALU.bypass), inc="v", lp=True)
        P.pe("vector", "v", "pA", "pB", "pKH", "pWBT")
        P.wait("scalar", "v", P.val("v"))
        P.op("scalar", lambda e, w_=wi, b_=bb: e.dma_start(
             out=attn_part[TW*w_:TW*(w_+1), :], in_=atb3[b_][:, :]), inc=f"so{bb}", by=16)
        st[f"at_so_{bb}"] = P.val(f"so{bb}")

    for wi in range(NW):
        stats_phase(wi)
        if wi + 1 < NW:
            load_tile(wi + 1)
        proj_phase(wi)
        scan_phase(wi, debug_og)
        st["buf"] = 1 - st["buf"]

    if debug_og:
        for s_ in ("dg", "dz", "dy0", "dy1", "dx0", "dx1", "so0", "so1"):
            P.wait("gpsimd", s_)
        finalize(nc, P)
        return nc, {"NW": NW}

    s12.close()
    NT3 = N3 // TW
    sb3 = lambda name, shape, dt, **kw: outer.enter_context(nc.sbuf_tensor(name, list(shape), dt, **kw))
    def g3_load(dst, src):
        P.op("gpsimd", lambda e, d=dst, s=src: e.dma_start(out=d, in_=s), inc="dg", by=16)
    attnb = [pA[:, 0:512], pB[:, 0:512],
             pGAQ.ap().rearrange("p a b -> p (a b)"),
             pS.ap().rearrange("p a b -> p (a b)")]
    st3 = {}

    P.wait("gpsimd", "so0"); P.wait("gpsimd", "so1")
    P.op("gpsimd", lambda e: e.collective_compute(
        "ReduceScatter", ALU.add,
        replica_groups=[[0, 1, 2, 3], [4, 5, 6, 7]],
        ins=[attn_part[:, :]], outs=[attn_q[:, :]]), inc="cc")

    at3 = [sb3(f"at3{i}", (128, D), BF16) for i in range(2)]
    x3 = [sb3(f"x3{i}", (128, D), F32) for i in range(2)]
    xmid = sb3("xmid", (128, D), F32)
    xf3 = sb3("xf3", (128, D), BF16)
    xff = sb3("xff", (128, NK, 128), BF16, is_transpose_dest=True)
    hmsk = sb3("hmsk", (128, 512), BF16)
    hfm = [sb3(f"hfm{j}", (128, 128), BF16, is_transpose_dest=True) for j in range(HP)]
    out3 = [sb3(f"out3{i}", (128, D), F32) for i in range(2)]
    sx2r = sb3("sx2r", (128, 1), F32)
    mur3 = sb3("mur3", (1, 128), BF16)
    mux3 = sb3("mux3", (128, 1), F32)
    nv3 = sb3("nv3", (128, 1), F32); rstd3 = sb3("rstd3", (128, 1), F32)
    mx8 = sb3("mx8", (128, 1), F32); nmx8 = sb3("nmx8", (128, 1), F32)
    z8 = sb3("z8", (128, 1), F32); wg8 = sb3("wg8", (128, 1), F32)
    ex8 = sb3("ex8", (128, 8), F32)
    msk8 = sb3("msk8", (128, 8), F32)

    g3_load(aws[:, :, :], ap["awr"].rearrange("(a p) n -> p a n", p=128))
    g3_load(bss[:, :, :], ap["bstk"][:, :, :])
    g3_load(brs[:, :], ap["brow"][:, :])
    g3_load(id3[:, :], c_ident[:, :])
    dg_w3 = P.val("dg")

    st3["free_v"] = P.val("v"); st3["free_s"] = P.val("s")
    def load3(j):
        b = j % 2
        P.wait("sync", "cc")
        P.wait("sync", "v", st3.get(f"v_xmid_{b}", 0))
        P.op("sync", lambda e, j_=j, b_=b: e.dma_start(out=at3[b_][:, :],
             in_=attn_q[TW*j_:TW*(j_+1), :]), inc=f"dy{b}", by=16)
        P.op("sync", lambda e, j_=j, b_=b: e.dma_start(out=x3[b_][:, :],
             in_=ap["x3s"][TW*j_:TW*(j_+1), :]), inc=f"dy{b}", by=16)
        st3[f"ld_dy_{b}"] = P.val(f"dy{b}")
    load3(0)

    for j in range(NT3):
        b = j % 2
        if j + 1 < NT3:
            load3(j + 1)
        P.wait("vector", f"dy{b}", st3[f"ld_dy_{b}"])
        P.wait("vector", "s", st3.get("xmid_s", 0))
        P.wait("vector", "v", st3.get("out_done_v", 0))
        for q in range(4):
            P.op("vector", lambda e, q_=q, b_=b: e.tensor_tensor(
                 out=xmid[:, 512*q_:512*(q_+1)], in0=at3[b_][:, 512*q_:512*(q_+1)],
                 in1=x3[b_][:, 512*q_:512*(q_+1)], op=ALU.add), inc="v")
        v_xmid = P.val("v")
        P.op("vector", lambda e: e.reduce_sum(out=mux3[:, :], in_=xmid[:, :], axis=AX.X), inc="v")
        P.op("vector", lambda e: e.tensor_scalar(out=mux3[:, :], in0=mux3[:, :],
             scalar1=1.0/D, scalar2=None, op0=ALU.mult, op1=ALU.bypass), inc="v")
        P.wait("scalar", "v", v_xmid)
        P.op("scalar", lambda e: e.activation(out=xf3[:, :], in_=xmid[:, :], func=AF.Square,
             scale=float(1.0/np.sqrt(D)), accum_out=sx2r[:, :]), inc="s")
        st3["xmid_s"] = P.val("s")
        P.wait("vector", "s")
        P.op("vector", lambda e: e.scalar_tensor_tensor(out=nv3[:, :], in0=mux3[:, :],
             scalar=mux3[:, :], in1=sx2r[:, :], op0=ALU.mult, op1=ALU.subtract), inc="v")
        P.wait("scalar", "v")
        P.op("scalar", lambda e: e.activation(out=lnv[:, :], in_=nv3[:, :], func=AF.Ln,
             bias=LNEPS, scale=-1.0), inc="s")
        P.op("scalar", lambda e: e.activation(out=rstd3[:, :], in_=lnv[:, :], func=AF.Exp,
             scale=-0.5), inc="s")
        P.wait("vector", "s")
        P.op("vector", lambda e: e.tensor_scalar(out=xf3[:, :], in0=xmid[:, :],
             scalar1=mux3[:, :], scalar2=rstd3[:, :], op0=ALU.subtract, op1=ALU.mult), inc="v")
        v_xf = P.val("v")
        P.wait("sync", "v", v_xf)
        for kt in range(NK):
            P.op("sync", lambda e, k_=kt: e.dma_start(out=xff[:, k_, :],
                 in_=xf3[:, 128*k_:128*(k_+1)], transpose=True), inc="dz", by=16)
        dz_xf = P.val("dz")
        P.wait("tensor", "v", P.val("v"))
        P.wait("tensor", "s", st3["free_s"])
        P.wait("tensor", "dz", dz_xf)
        P.pb("tensor", "pA", "pB")
        for kt in range(NK):
            P.op("tensor", lambda e, k_=kt: e.matmul(pA[:, 0:512], xff[:, k_, :],
                 aws[:, k_, 0:512], start=(k_ == 0), stop=(k_ == NK-1)))
            P.op("tensor", lambda e, k_=kt: e.matmul(pB[:, 0:8], xff[:, k_, :],
                 aws[:, k_, 512:520], start=(k_ == 0), stop=False))
        P.op("tensor", lambda e: e.matmul(pB[:, 0:8], o1r[:, :], brs[:, :],
             start=False, stop=True), inc="t")
        t_h = P.val("t")
        P.pe("tensor", "t", "pA", "pB")
        P.wait("vector", "t", t_h)
        P.pb("vector", "pB")
        P.op("vector", lambda e: e.reduce_max(out=mx8[:, :], in_=pB[:, 0:8], axis=AX.X), inc="v")
        P.op("vector", lambda e: e.tensor_scalar(out=nmx8[:, :], in0=mx8[:, :],
             scalar1=-1.0, scalar2=None, op0=ALU.mult, op1=ALU.bypass), inc="v")
        P.pe("vector", "v", "pB")
        P.wait("scalar", "v")
        P.pb("scalar", "pB")
        P.op("scalar", lambda e: e.activation(out=ex8[:, :], in_=pB[:, 0:8], func=AF.Exp,
             bias=nmx8[:, :], accum_out=z8[:, :]), inc="s")
        P.pe("scalar", "s", "pB")
        P.wait("vector", "s")
        P.op("vector", lambda e: e.reciprocal(out=wg8[:, :], in_=z8[:, :]), inc="v")
        P.op("vector", lambda e: e.tensor_scalar(out=wg8[:, :], in0=wg8[:, :],
             scalar1=1.0/64.0, scalar2=None, op0=ALU.mult, op1=ALU.bypass), inc="v")
        P.pb("vector", "pB")
        P.op("vector", lambda e: e.tensor_scalar(out=msk8[:, :], in0=pB[:, 0:8],
             scalar1=mx8[:, :], scalar2=None, op0=ALU.is_equal, op1=ALU.bypass), inc="v")
        P.pe("vector", "v", "pB")
        P.wait("vector", "t", t_h)
        P.pb("vector", "pA")
        for ex in range(8):
            P.op("vector", lambda e, x_=ex: e.tensor_scalar(out=hmsk[:, 64*x_:64*(x_+1)],
                 in0=pA[:, 64*x_:64*(x_+1)], scalar1=msk8[:, x_:x_+1], scalar2=None,
                 op0=ALU.mult, op1=ALU.bypass), inc="v")
        v_hm = P.val("v")
        P.pe("vector", "v", "pA")
        P.wait("sync", "v", v_hm)
        for jj in range(HP):
            P.op("sync", lambda e, j_=jj: e.dma_start(out=hfm[j_][:, :],
                 in_=hmsk[:, 128*j_:128*(j_+1)], transpose=True), inc="dz", by=16)
        dz_h = P.val("dz")
        P.wait("tensor", "v", v_hm)
        P.wait("tensor", "dz", dz_h)
        P.pb("tensor", "pA", "pB", "pGAQ", "pS")
        for jj in range(HP):
            for q in range(4):
                P.op("tensor", lambda e, j_=jj, q_=q: e.matmul(
                    attnb[q_], hfm[j_][:, :], bss[:, j_, 512*q_:512*(q_+1)],
                    start=(j_ == 0), stop=(j_ == HP-1)),
                    inc="t" if (jj == HP-1 and q == 3) else None)
        t_y = P.val("t")
        P.pe("tensor", "t", "pA", "pB", "pGAQ", "pS")
        P.wait("vector", "t", t_y)
        P.wait("vector", f"so{b}", st3.get(f"store_so_{b}", 0))
        P.pb("vector", "pA", "pB", "pGAQ", "pS")
        for q in range(4):
            P.op("vector", lambda e, q_=q, b_=b: e.scalar_tensor_tensor(
                 out=out3[b_][:, 512*q_:512*(q_+1)], in0=attnb[q_], scalar=wg8[:, :],
                 in1=xmid[:, 512*q_:512*(q_+1)], op0=ALU.mult, op1=ALU.add), inc="v")
        st3["free_v"] = P.val("v")
        st3["out_done_v"] = P.val("v")
        P.pe("vector", "v", "pA", "pB", "pGAQ", "pS")
        st3["free_s"] = P.val("s")
        v_out = P.val("v")
        P.wait("scalar", "v", v_out)
        P.op("scalar", lambda e, j_=j, b_=b: e.dma_start(out=ap["out"][TW*j_:TW*(j_+1), :],
             in_=out3[b_][:, :]), inc=f"so{b}", by=16)
        st3[f"store_so_{b}"] = P.val(f"so{b}")
        st3[f"v_xmid_{b}"] = v_xmid

    for s_ in ("dg", "dz", "dy0", "dy1", "dx0", "dx1", "so0", "so1"):
        P.wait("gpsimd", s_)
    finalize(nc, P)
    return nc, {"NW": NW}


def finalize(nc, P):
    with contextlib.ExitStack() as semstack:
        for name in P.cnt:
            P.sems[name] = semstack.enter_context(nc.semaphore(name))
        with nc.Block() as block:
            @block.tensor
            def _(tensor):
                for fn in P.ops["tensor"]:
                    fn(tensor)
            @block.vector
            def _(vector):
                for fn in P.ops["vector"]:
                    fn(vector)
            @block.scalar
            def _(scalar):
                for fn in P.ops["scalar"]:
                    fn(scalar)
            @block.sync
            def _(sync):
                for fn in P.ops["sync"]:
                    fn(sync)
            @block.gpsimd
            def _(gpsimd):
                for fn in P.ops["gpsimd"]:
                    fn(gpsimd)


def prep_core_inputs(core, x, Wq, Wk, Wv, Wo, Wb, ln_w, ln_b, Wg1, Wg2, norm_w,
                     A, Bw, Wr, br):
    b = core // 4
    hg = core % 4
    cs = slice(512 * hg, 512 * (hg + 1))
    hs = slice(4 * hg, 4 * (hg + 1))
    T = x.shape[1]
    f32 = np.float32
    xb = np.ascontiguousarray(x[b]).astype(f32, copy=False)
    xbT = np.ascontiguousarray(x[b].T)
    lw = ln_w[hs]
    lb = ln_b[hs]
    wstk = np.zeros((128, 128), f32)
    for p in range(4):
        wstk[32*p:32*p+16, :] = lw[p][None, :]
    woln = np.broadcast_to(lw[None, :, :], (128, 4, 128)).copy()
    boln = np.broadcast_to(lb.reshape(1, 512), (128, 512)).copy()
    wnat = np.broadcast_to(lw.reshape(1, 512), (128, 512)).copy()
    nrm = np.broadcast_to(norm_w[cs][None, :], (128, 512)).copy()
    awr = np.concatenate([A.transpose(2, 0, 1).reshape(2048, 512), Wr.T], axis=1)
    bstk = np.zeros((128, 4, 2048), f32)
    for jj in range(4):
        bstk[0:64, jj, :] = Bw[2*jj].T
        bstk[64:128, jj, :] = Bw[2*jj+1].T
    q3 = core % 4
    N3 = T // 4
    x3s = np.ascontiguousarray(x[b][N3*q3:N3*(q3+1)])
    return {
        "xb": xb, "xbT": xbT,
        "wq": np.ascontiguousarray(Wq[:, cs]), "wk": np.ascontiguousarray(Wk[:, cs]),
        "wv": np.ascontiguousarray(Wv[:, cs]),
        "wbeta": np.ascontiguousarray(Wb[:, hs]),
        "lnw": np.ascontiguousarray(lw[None]), "lnb": np.ascontiguousarray(lb[None]),
        "wstk": wstk, "woln": woln, "boln": boln, "wnat": wnat, "nrm": nrm,
        "wg1": np.ascontiguousarray(Wg1), "wg2": np.ascontiguousarray(Wg2[:, cs]),
        "wo": np.ascontiguousarray(Wo[cs, :]), "awr": np.ascontiguousarray(awr),
        "bstk": bstk, "brow": np.ascontiguousarray(br[None, :]).astype(f32),
        "x3s": x3s,
    }


T_FULL = 8192
N3_FULL = T_FULL // 4

_cache = {}

def kernel(L0_attn_x, L0_attn_Wq, L0_attn_Wk, L0_attn_Wv, L0_attn_Wo,
           L0_attn_Wbeta, L0_attn_ln_w, L0_attn_ln_b, L0_attn_Wg1, L0_attn_Wg2,
           L0_attn_norm_w, L0_ffn_A, L0_ffn_B, L0_ffn_Wr, L0_ffn_br):
    from concourse.bass_utils import run_bass_kernel_spmd
    x = np.asarray(L0_attn_x, np.float32)
    args = [np.asarray(a, np.float32) for a in
            (L0_attn_Wq, L0_attn_Wk, L0_attn_Wv, L0_attn_Wo, L0_attn_Wbeta,
             L0_attn_ln_w, L0_attn_ln_b, L0_attn_Wg1, L0_attn_Wg2, L0_attn_norm_w,
             L0_ffn_A, L0_ffn_B, L0_ffn_Wr, L0_ffn_br)]
    ims = [prep_core_inputs(c, x, *args) for c in range(8)]
    if "nc" not in _cache:
        _cache["nc"], _ = build(T_FULL, debug_og=False)
    res = run_bass_kernel_spmd(_cache["nc"], ims, core_ids=list(range(8)))
    out = np.empty((2, T_FULL, 2048), np.float32)
    for c in range(8):
        b, q3 = c // 4, c % 4
        out[b, N3_FULL*q3:N3_FULL*(q3+1)] = res.results[c]["out"]
    return out


# version 03
# speedup vs baseline: 1.0273x; 1.0156x over previous
import numpy as np
import ml_dtypes
import contextlib
import concourse.bass as bass
import concourse.mybir as mybir

F32 = mybir.dt.float32
BF16 = mybir.dt.bfloat16
AF = mybir.ActivationFunctionType
ALU = mybir.AluOpType
AX = mybir.AxisListType
BF = ml_dtypes.bfloat16

CHUNK = 16
W = 8
TW = 128
HP = 4
Dh = 128
D = 2048
NK = 16
EPS = 1e-6
LNEPS = 1e-5
IRT128 = float(np.sqrt(1.0 / 128.0))


class Prog:
    ENGS = ("tensor", "vector", "scalar", "sync", "gpsimd")

    def __init__(self):
        self.ops = {e: [] for e in self.ENGS}
        self.cnt = {}
        self.sems = {}

    def declare(self, *names):
        for n in names:
            self.cnt[n] = 0

    def wait(self, eng, sem, val=None):
        if val is None:
            val = self.cnt[sem]
        if val <= 0:
            return
        self.ops[eng].append(lambda e, s=sem, v=val: e.wait_ge(self.sems[s], v))

    def op(self, eng, build, inc=None, by=1, lp=False):
        if eng == "vector":
            self.wait(eng, "v")
        elif eng == "scalar":
            self.wait(eng, "s")
        if inc is not None:
            self.cnt[inc] += by
        def fn(e, build=build, inc=inc, by=by, lp=lp):
            if lp:
                with self.nc.allow_low_precision(reason="bf16 sum columns by design"):
                    ins = build(e)
            else:
                ins = build(e)
            if inc is not None:
                ins.then_inc(self.sems[inc], by)
        self.ops[eng].append(fn)

    def val(self, sem):
        return self.cnt[sem]

    def pb(self, eng, *banks):
        if not hasattr(self, "bankmarks"):
            self.bankmarks = {}
        for b in banks:
            m = self.bankmarks.get(b)
            if m and m[0] != eng:
                self.wait(eng, m[1], m[2])

    def pe(self, eng, sem, *banks):
        if not hasattr(self, "bankmarks"):
            self.bankmarks = {}
        for b in banks:
            self.bankmarks[b] = (eng, sem, self.cnt[sem])


def host_consts():
    ident = np.eye(128, dtype=BF)
    esel = np.zeros((128, W, 16), dtype=BF)
    for c in range(W):
        for t in range(16):
            esel[16 * c + t, c, t] = -1.0
    triu = np.triu(np.ones((128, 128), np.float32)).astype(BF)
    onesb = np.ones((128, 1), dtype=BF)
    ones116 = np.ones((1, 16), dtype=BF)
    ones1r = np.ones((1, 128), dtype=BF)
    selh = np.zeros((2, HP, 128, 32), dtype=BF)
    for h in range(2):
        for p in range(HP):
            for t in range(16):
                selh[h, p, 32 * p + t, 16 * h + t] = 1.0
    return ident, esel, triu, onesb, ones116, ones1r, selh


def build(T, debug_og=False, stages=99):
    NW = T // TW
    NSC = W // 2
    nc = bass.Bass("TRN2", target_bir_lowering=False)
    P = Prog()
    P.nc = nc
    P.declare("t", "v", "s", "dy0", "dy1", "dx0", "dx1", "so0", "so1", "dg", "dz", "cc")

    ap = {}
    def inp(name, shape, dt=F32):
        ap[name] = nc.dram_tensor(name, list(shape), dt, kind="ExternalInput").ap()
    inp("xb", (T, D)); inp("xbT", (D, T))
    inp("wq", (D, 512)); inp("wk", (D, 512)); inp("wv", (D, 512))
    inp("wbeta", (D, HP))
    inp("lnw", (1, HP, Dh)); inp("lnb", (1, HP, Dh))
    inp("wstk", (128, 128))
    inp("woln", (128, HP, Dh)); inp("boln", (128, HP * Dh))
    inp("wnat", (128, HP * Dh)); inp("nrm", (128, HP * Dh))
    inp("wg1", (D, 128)); inp("wg2", (128, 512))
    inp("wo", (512, D)); inp("awr", (D, 520)); inp("bstk", (128, HP, D))
    inp("brow", (1, 8))
    N3 = T // 4
    inp("x3s", (N3, D))
    if debug_og:
        ap["out"] = nc.dram_tensor("out", [NW, 512, 128], F32, kind="ExternalOutput").ap()
    else:
        ap["out"] = nc.dram_tensor("out", [N3, D], F32, kind="ExternalOutput").ap()

    og_loc = nc.dram_tensor("og_loc", [NW, 512, 128], BF16, kind="Internal").ap()
    attn_part = nc.dram_tensor("attn_part", [T, D], BF16, kind="Internal").ap()
    attn_q = nc.dram_tensor("attn_q", [N3, D], BF16, kind="Internal").ap()

    identN, eselN, triuN, onesbN, o116N, o1rN, selhN = host_consts()
    c_ident = nc.inline_tensor(identN, "c_ident").ap()
    c_esel = nc.inline_tensor(eselN, "c_esel").ap()
    c_triu = nc.inline_tensor(triuN, "c_triu").ap()
    c_onesb = nc.inline_tensor(onesbN, "c_onesb").ap()
    c_o116 = nc.inline_tensor(o116N, "c_o116").ap()
    c_o1r = nc.inline_tensor(o1rN, "c_o1r").ap()
    c_one1 = nc.inline_tensor(np.ones((1, 1), np.float32), "c_one1").ap()
    c_zer = nc.inline_tensor(np.zeros((128, 128), BF), "c_zer").ap()
    _cvals = [LNEPS, EPS, -1.0, -0.5, IRT128, float(1.0/np.sqrt(D)), -2.0, 1.0]
    c_cvals = nc.inline_tensor(
        np.broadcast_to(np.asarray(_cvals, np.float32)[None, :], (128, len(_cvals))).copy(),
        "c_cvals").ap()
    c_selh = nc.inline_tensor(selhN.reshape(2 * HP * 128, 32), "c_selh").ap()

    outer = contextlib.ExitStack()
    s12 = contextlib.ExitStack()
    def sbo(name, shape, dt, **kw): return outer.enter_context(nc.sbuf_tensor(name, list(shape), dt, **kw))
    def sb1(name, shape, dt, **kw): return s12.enter_context(nc.sbuf_tensor(name, list(shape), dt, **kw))
    def pso(name, shape): return outer.enter_context(nc.psum_tensor(name, list(shape), F32))

    jk = sbo("jk", (128, 512), F32)
    wos3 = sbo("wos3", (128, 4, D), BF16)
    atb3 = [sbo(f"atb3{i}", (128, D), BF16) for i in range(2)]
    aws = sbo("aws", (128, NK, 520), BF16)
    bss = sbo("bss", (128, HP, D), BF16)
    brs = sbo("brs", (1, 8), BF16)
    id3 = sbo("id3", (128, 128), BF16)
    cvals = sbo("cvals", (128, 8), F32)
    for _i, _v in enumerate(_cvals):
        nc.const_aps.register(F32, float(_v), cvals[:, _i:_i+1])
    jkb = sbo("jkb", (128, 128), BF16)
    ogfm = [sbo(f"ogfm{h}", (128, 128), BF16, is_transpose_dest=True) for h in range(HP)]
    on1 = sbo("on1", (128, 1), BF16)
    o1r = sbo("o1r", (1, 128), BF16)
    zer = sbo("zer", (128, 128), BF16)
    lnv = sbo("lnv", (128, 1), F32)
    pA = pso("pA", (128, 512)); pB = pso("pB", (128, 512))
    pKH = pso("pKH", (128, 2, 256))
    pS = pso("pS", (128, HP, 128))
    pGAQ = pso("pGAQ", (128, HP, 128))
    pWBT = pso("pWBT", (128, 2, 256))
    pVnA = pso("pVnA", (128, 2, 256)); pVnB = pso("pVnB", (128, 2, 256))
    pVn = [pVnA, pVnB]

    wqs = sb1("wqs", (128, NK, 512), BF16); wks = sb1("wks", (128, NK, 512), BF16)
    wvs = sb1("wvs", (128, NK, 512), BF16)
    wbs = sb1("wbs", (128, NK, HP), BF16); wg1s = sb1("wg1s", (128, NK, 128), BF16)
    wg2s = sb1("wg2s", (128, 512), BF16)
    idb = sb1("idb", (128, 128), BF16); esl = sb1("esl", (128, W, 16), BF16)
    tri = sb1("tri", (128, 128), BF16)
    o116 = sb1("o116", (1, 16), BF16)
    slh = sb1("slh", (128, 2, HP, 32), BF16)
    lnwr = sb1("lnwr", (1, HP, Dh), F32); lnbr = sb1("lnbr", (1, HP, Dh), F32)
    wbrow = sb1("wbrow", (1, HP, Dh), BF16)
    wstks = sb1("wstks", (128, 128), F32); w2sb = sb1("w2sb", (128, 128), BF16)
    wolnb = sb1("wolnb", (128, HP, Dh), BF16)
    bolnn = sb1("bolnn", (128, 512), F32)
    wnatb = sb1("wnatb", (128, 512), BF16); nrmb = sb1("nrmb", (128, 512), BF16)
    csw = sb1("csw", (1, 1668), BF16)
    xtm = [sb1(f"xtm{i}", (128, D), F32) for i in range(2)]
    xfm = [sb1(f"xfm{i}", (128, NK, 128), BF16) for i in range(2)]
    qtm = sb1("qtm", (128, 512), BF16); ktm = sb1("ktm", (128, 512), BF16)
    vtm = sb1("vtm", (128, 512), BF16); kie = sb1("kie", (128, 512), BF16)
    wtg = sb1("wtg", (128, 512), BF16)
    sgm = sb1("sgm", (128, 512), BF16); sg2 = sb1("sg2", (128, 512), F32)
    rec = sb1("rec", (128, 512), F32)
    qfm = [sb1(f"qfm{p}", (128, 128), BF16, is_transpose_dest=True) for p in range(HP)]
    kfm = [sb1(f"kfm{p}", (128, 128), BF16, is_transpose_dest=True) for p in range(HP)]
    kefm = [sb1(f"kefm{p}", (128, 128), BF16, is_transpose_dest=True) for p in range(HP)]
    xgt = sb1("xgt", (128, 128), BF16)
    xgf = sb1("xgf", (128, 128), BF16, is_transpose_dest=True)
    gsb = sb1("gsb", (128, HP, 128), BF16)
    gstke = sb1("gstke", (128, NSC, 16), BF16)
    asb = [sb1(f"asb{p}", (128, 128), BF16) for p in range(HP)]
    s0 = sb1("s0", (128, HP, 130), BF16)
    vnstk = sb1("vnstk", (128, W, 132), BF16)
    vnnat = sb1("vnnat", (128, HP, 130), BF16)
    khh = sb1("khh", (128, 128), BF16); t1b = sb1("t1b", (128, 128), BF16)
    gwb = sb1("gwb", (128, 128), BF16); xts = sb1("xts", (128, 128), BF16)
    mus = sb1("mus", (128, 1), F32); nv = sb1("nv", (128, 1), F32)
    rsd = sb1("rsd", (128, 1), F32)
    s2R = sb1("s2R", (128, 1), F32); mm12 = sb1("mm12", (128, 2), F32)
    m12r = sb1("m12r", (128, 2), F32)
    murp = sb1("murp", (1, 128), F32); murn = sb1("murn", (1, 128), BF16)
    one1 = sb1("one1", (1, 1), F32)
    sx2 = sb1("sx2", (128, 1), F32); mux = sb1("mux", (128, 1), F32)
    nvx = sb1("nvx", (128, 1), F32); rstdx = sb1("rstdx", (128, 1), F32)
    ngrx = sb1("ngrx", (128, 1), F32)
    rk2 = sb1("rk2", (128, HP, 1), F32); rkl = sb1("rkl", (128, HP, 1), F32)
    rstdk = sb1("rstdk", (128, HP, 1), F32)
    ebv = sb1("ebv", (128, HP), BF16); dbv = sb1("dbv", (128, HP), F32)
    rbv = sb1("rbv", (128, HP), F32); negb = sb1("negb", (128, HP), F32)
    qbv = sb1("qbv", (128, 512), BF16)
    muo = sb1("muo", (128, HP), F32); so2 = sb1("so2", (128, HP), F32)
    nvo = sb1("nvo", (128, HP), F32); rso = sb1("rso", (128, HP), F32)
    lno = sb1("lno", (128, HP), F32)
    qhh = sb1("qhh", (128, 128), BF16); obuf = sb1("obuf", (128, 128), BF16)
    onat = sb1("onat", (128, 512), BF16)
    ogb = sb1("ogb", (128, 512), BF16); ogc = sb1("ogc", (128, 512), BF16)
    ro2 = sb1("ro2", (128, HP, 1), F32); rol = sb1("rol", (128, HP, 1), F32)
    rsog = sb1("rsog", (128, HP, 1), F32)

    def g_load(dst, src):
        P.op("gpsimd", lambda e, d=dst, s=src: e.dma_start(out=d, in_=s), inc="dg", by=16)
    g_load(wqs[:, :, :], ap["wq"].rearrange("(a p) n -> p a n", p=128))
    g_load(wks[:, :, :], ap["wk"].rearrange("(a p) n -> p a n", p=128))
    g_load(wvs[:, :, :], ap["wv"].rearrange("(a p) n -> p a n", p=128))
    g_load(wbs[:, :, :], ap["wbeta"].rearrange("(a p) n -> p a n", p=128))
    g_load(wg1s[:, :, :], ap["wg1"].rearrange("(a p) n -> p a n", p=128))
    g_load(wg2s[:, :], ap["wg2"][:, :])
    g_load(idb[:, :], c_ident[:, :])
    g_load(esl[:, :, :], c_esel[:, :, :])
    g_load(tri[:, :], c_triu[:, :])
    g_load(on1[:, :], c_onesb[:, :])
    g_load(o116[:, :], c_o116[:, :])
    g_load(o1r[:, :], c_o1r[:, :])
    g_load(one1[:, :], c_one1[:, :])
    g_load(zer[:, :], c_zer[:, :])
    g_load(cvals[:, :], c_cvals[:, :])
    g_load(slh[:, :, :, :], c_selh.rearrange("(h q p) n -> p h q n", h=2, q=HP))
    g_load(wos3[:, :, :], ap["wo"].rearrange("(a p) n -> p a n", p=128))
    dg_wo3_val = P.val("dg")
    g_load(lnwr[:, :, :], ap["lnw"][:, :, :])
    g_load(lnbr[:, :, :], ap["lnb"][:, :, :])
    g_load(wstks[:, :], ap["wstk"][:, :])
    g_load(wolnb[:, :, :], ap["woln"][:, :, :])
    g_load(bolnn[:, :], ap["boln"][:, :])
    g_load(wnatb[:, :], ap["wnat"][:, :])
    g_load(nrmb[:, :], ap["nrm"][:, :])
    n_wload = P.val("dg")
    P.op("sync", lambda e: e.dma_start(out=xtm[0][:, :], in_=ap["xb"][0:128, :]), inc="dy0", by=16)
    P.op("gpsimd", lambda e: e.dma_start(out=xfm[0][:, :, :],
         in_=ap["xbT"].rearrange("(a p) t -> p a t", p=128)[:, :, 0:128]), inc="dx0", by=16)

    P.wait("vector", "dg", n_wload)
    P.op("vector", lambda e: e.tensor_tensor(out=wbrow[:, :, :], in0=lnwr[:, :, :],
         in1=lnbr[:, :, :], op=ALU.mult), inc="v")
    P.op("vector", lambda e: e.memset(s0[:, :, :], 0.0), inc="v")
    P.op("vector", lambda e: e.memset(vnstk[:, :, :], 0.0), inc="v")
    P.op("vector", lambda e: e.memset(vnnat[:, :, :], 0.0), inc="v")
    for p_ in (pKH[:, :, :], pWBT[:, :, :], pGAQ[:, :, :], pS[:, :, :],
               pVnA[:, :, :], pVnB[:, :, :], pA[:, :], pB[:, :]):
        P.op("vector", lambda e, pp=p_: e.memset(pp, 0.0), inc="v")
    v_init = P.val("v")
    P.pe("vector", "v", "pA", "pB", "pKH", "pS", "pGAQ", "pWBT", "pVnA", "pVnB")
    P.wait("scalar", "dg", n_wload)
    P.op("scalar", lambda e: e.activation(out=w2sb[:, :], in_=wstks[:, :], func=AF.Square), inc="s")
    P.wait("tensor", "dg", n_wload)
    P.wait("tensor", "v", v_init)
    for wsb, width, off in [(wqs, 512, 0), (wks, 512, 512), (wvs, 512, 1024),
                            (wbs, HP, 1536), (wg1s, 128, 1540)]:
        P.wait("tensor", "v")
        P.pb("tensor", "pA")
        for kt in range(NK):
            P.op("tensor", lambda e, w_=wsb, k_=kt, n_=width: e.matmul(
                pA[0:1, 0:n_], on1[:, :], w_[:, k_, 0:n_], start=(k_ == 0),
                stop=(k_ == NK - 1)), inc="t" if kt == NK - 1 else None)
        P.pe("tensor", "t", "pA")
        P.wait("vector", "t")
        P.pb("vector", "pA")
        P.op("vector", lambda e, n_=width, o_=off: e.tensor_scalar(
            out=csw[0:1, o_:o_ + n_], in0=pA[0:1, 0:n_], scalar1=1.0, scalar2=None,
            op0=ALU.mult, op1=ALU.bypass), inc="v")
        P.pe("vector", "v", "pA")
    P.pb("tensor", "pS")
    P.op("tensor", lambda e: e.matmul(pS.ap().rearrange("p a b -> p (a b)"), zer[:, :],
         wqs[:, 0, :], start=True, stop=True, skip_group_check=True), inc="t")
    P.pe("tensor", "t", "pS")
    P.wait("tensor", "v")

    st = {"buf": 0, "xtm_dy_0": 16, "xfm_dx_0": 16,
          "pA_free_s": 0, "pB_free_v": 0, "pB_free_s": 0,
          "gaq_free_v": 0, "s0_ready_v": 0, "s0_ready_s": 0,
          "pS_free_v": 0, "pS_free_s": 0}

    def stats_phase(wi):
        buf = st["buf"]
        P.wait("tensor", "dg", n_wload)
        P.wait("tensor", f"dx{buf}", st[f"xfm_dx_{buf}"])
        P.wait("tensor", "s", st["pA_free_s"])
        P.pb("tensor", "pA")
        for kt in range(NK):
            P.op("tensor", lambda e, k_=kt, b_=buf: e.matmul(
                pA[0:1, 0:128], on1[:, :], xfm[b_][:, k_, :], start=(k_ == 0),
                stop=(k_ == NK - 1)), inc="t" if kt == NK - 1 else None)
        P.pe("tensor", "t", "pA")
        P.wait("vector", "t")
        P.pb("vector", "pA")
        P.op("vector", lambda e: e.tensor_scalar(out=murp[:, :], in0=pA[0:1, 0:128],
             scalar1=1.0 / D, scalar2=None, op0=ALU.mult, op1=ALU.bypass), inc="v")
        P.op("vector", lambda e: e.tensor_scalar(out=murn[:, :], in0=pA[0:1, 0:128],
             scalar1=-1.0 / D, scalar2=None, op0=ALU.mult, op1=ALU.bypass), inc="v")
        v_mur = P.val("v")
        P.pe("vector", "v", "pA")
        P.wait("scalar", f"dy{buf}", st[f"xtm_dy_{buf}"])
        P.op("scalar", lambda e, b_=buf: e.activation(out=xtm[1 - b_][:, :], in_=xtm[b_][:, :],
             func=AF.Square, scale=float(1.0 / np.sqrt(D)), accum_out=sx2[:, :]), inc="s")
        st["sx2_s"] = P.val("s")
        P.wait("tensor", "v", v_mur)
        P.pb("tensor", "pKH")
        P.op("tensor", lambda e: e.transpose(pKH[:, 0, 131:132], murp[:, :], one1[:, :]), inc="t")
        P.pe("tensor", "t", "pKH")
        P.wait("vector", "t")
        P.pb("vector", "pKH")
        P.op("vector", lambda e: e.tensor_scalar(out=mux[:, :], in0=pKH[:, 0, 131:132],
             scalar1=1.0, scalar2=None, op0=ALU.mult, op1=ALU.bypass), inc="v")
        P.pe("vector", "v", "pKH")
        P.wait("vector", "s", st["sx2_s"])
        P.op("vector", lambda e: e.scalar_tensor_tensor(out=nvx[:, :], in0=mux[:, :],
             scalar=mux[:, :], in1=sx2[:, :], op0=ALU.mult, op1=ALU.subtract), inc="v")
        P.wait("scalar", "v")
        P.op("scalar", lambda e: e.activation(out=lnv[:, :], in_=nvx[:, :], func=AF.Ln,
             bias=LNEPS, scale=-1.0), inc="s")
        P.op("scalar", lambda e: e.activation(out=rstdx[:, :], in_=lnv[:, :], func=AF.Exp,
             scale=-0.5), inc="s")
        P.wait("vector", "s")
        P.op("vector", lambda e: e.tensor_scalar(out=ngrx[:, :], in0=rstdx[:, :],
             scalar1=-1.0, scalar2=None, op0=ALU.mult, op1=ALU.bypass), inc="v")

    def load_tile(wi):
        b = wi % 2
        P.wait("sync", "s", st.get("sx2_s", 0))
        P.op("sync", lambda e, w_=wi, b_=b: e.dma_start(out=xtm[b_][:, :],
             in_=ap["xb"][TW*w_:TW*(w_+1), :]), inc=f"dy{b}", by=16)
        st[f"xtm_dy_{b}"] = P.val(f"dy{b}")
        P.wait("gpsimd", "t", st.get(f"xfm_free_{b}", 0))
        P.op("gpsimd", lambda e, w_=wi, b_=b: e.dma_start(out=xfm[b_][:, :, :],
             in_=ap["xbT"].rearrange("(a p) t -> p a t", p=128)[:, :, TW*w_:TW*(w_+1)]),
             inc=f"dx{b}", by=16)
        st[f"xfm_dx_{b}"] = P.val(f"dx{b}")

    def proj_phase(wi):
        buf = st["buf"]
        P.wait("tensor", "v")
        P.pb("tensor", "pA")
        for kt in range(NK):
            P.op("tensor", lambda e, k_=kt, b_=buf: e.matmul(
                pA[:, 0:512], xfm[b_][:, k_, :], wqs[:, k_, :], start=(k_ == 0), stop=False))
        P.op("tensor", lambda e: e.matmul(pA[:, 0:512], murn[:, :], csw[0:1, 0:512],
             start=False, stop=True), inc="t")
        t_q = P.val("t")
        P.pe("tensor", "t", "pA")
        P.wait("tensor", "v", st["pB_free_v"]); P.wait("tensor", "s", st["pB_free_s"])
        P.pb("tensor", "pB")
        for kt in range(NK):
            P.op("tensor", lambda e, k_=kt, b_=buf: e.matmul(
                pB[:, 0:512], xfm[b_][:, k_, :], wks[:, k_, :], start=(k_ == 0), stop=False))
        P.op("tensor", lambda e: e.matmul(pB[:, 0:512], murn[:, :], csw[0:1, 512:1024],
             start=False, stop=True), inc="t")
        t_k = P.val("t")
        P.pe("tensor", "t", "pB")
        P.wait("scalar", "t", t_q)
        P.pb("scalar", "pA")
        P.op("scalar", lambda e: e.activation(out=sgm[:, :], in_=pA[:, 0:512], func=AF.Exp,
             scale=ngrx[:, :]), inc="s")
        P.pe("scalar", "s", "pA")
        P.op("scalar", lambda e: e.activation(out=sg2[:, :], in_=sgm[:, :], func=AF.Ln,
             bias=1.0, scale=1.0), inc="s")
        P.op("scalar", lambda e: e.activation(out=rec[:, :], in_=sg2[:, :], func=AF.Exp,
             scale=-1.0), inc="s")
        P.wait("vector", "s")
        P.pb("vector", "pA")
        P.op("vector", lambda e: e.scalar_tensor_tensor(out=qtm[:, :], in0=pA[:, 0:512],
             scalar=rstdx[:, :], in1=rec[:, :], op0=ALU.mult, op1=ALU.mult), inc="v")
        v_q = P.val("v")
        P.pe("vector", "v", "pA")
        P.wait("tensor", "v", v_q)
        P.pb("tensor", "pA")
        for kt in range(NK):
            P.op("tensor", lambda e, k_=kt, b_=buf: e.matmul(
                pA[:, 0:512], xfm[b_][:, k_, :], wvs[:, k_, :], start=(k_ == 0), stop=False))
        P.op("tensor", lambda e: e.matmul(pA[:, 0:512], murn[:, :], csw[0:1, 1024:1536],
             start=False, stop=True), inc="t")
        t_v = P.val("t")
        P.pe("tensor", "t", "pA")
        P.wait("scalar", "t", t_k)
        P.pb("scalar", "pB")
        P.op("scalar", lambda e: e.activation(out=jk[:, :], in_=pB[:, 0:512], func=AF.Square,
             scale=rstdx[:, :]), inc="s")
        P.pe("scalar", "s", "pB")
        P.wait("vector", "s")
        P.op("vector", lambda e: e.reduce_sum(out=rk2[:, :, :],
             in_=jk.ap().rearrange("p (h e) -> p h e", h=HP), axis=AX.X), inc="v")
        P.wait("scalar", "v")
        P.op("scalar", lambda e: e.activation(out=rkl[:, :, :], in_=rk2[:, :, :], func=AF.Ln,
             bias=EPS), inc="s")
        P.op("scalar", lambda e: e.activation(out=rstdk[:, :, :], in_=rkl[:, :, :],
             func=AF.Exp, scale=-0.5), inc="s")
        P.wait("vector", "s")
        P.pb("vector", "pB")
        for h in range(HP):
            P.op("vector", lambda e, h_=h: e.tensor_scalar(out=ktm[:, 128*h_:128*(h_+1)],
                 in0=pB[:, 128*h_:128*(h_+1)], scalar1=rstdx[:, :], scalar2=rstdk[:, h_, :],
                 op0=ALU.mult, op1=ALU.mult), inc="v")
        v_k = P.val("v")
        P.pe("vector", "v", "pB")
        P.wait("tensor", "v", v_k)
        P.pb("tensor", "pB")
        for kt in range(NK):
            P.op("tensor", lambda e, k_=kt, b_=buf: e.matmul(
                pB[:, 0:HP], xfm[b_][:, k_, :], wbs[:, k_, :], start=(k_ == 0), stop=False))
        P.op("tensor", lambda e: e.matmul(pB[:, 0:HP], murn[:, :], csw[0:1, 1536:1536+HP],
             start=False, stop=True), inc="t")
        for kt in range(NK):
            P.op("tensor", lambda e, k_=kt, b_=buf: e.matmul(
                pB[:, HP:HP+128], xfm[b_][:, k_, :], wg1s[:, k_, :], start=(k_ == 0), stop=False))
        P.op("tensor", lambda e: e.matmul(pB[:, HP:HP+128], murn[:, :], csw[0:1, 1540:1668],
             start=False, stop=True), inc="t")
        t_misc = P.val("t")
        P.pe("tensor", "t", "pB")
        st[f"xfm_free_{buf}"] = t_misc
        P.wait("vector", "t", t_v)
        P.pb("vector", "pA")
        P.op("vector", lambda e: e.tensor_scalar(out=vtm[:, :], in0=pA[:, 0:512],
             scalar1=rstdx[:, :], scalar2=None, op0=ALU.mult, op1=ALU.bypass), inc="v")
        P.pe("vector", "v", "pA")
        P.op("vector", lambda e: e.tensor_tensor(out=wtg[:, :], in0=vtm[:, :], in1=ktm[:, :],
             op=ALU.subtract), inc="v")
        P.op("vector", lambda e: e.tensor_tensor(out=wtg[:, :], in0=wtg[:, :],
             in1=wnatb[:, :], op=ALU.mult), inc="v")
        st["v_wtg"] = P.val("v")
        P.wait("scalar", "t", t_misc)
        P.pb("scalar", "pB")
        P.op("scalar", lambda e: e.activation(out=ebv[:, :], in_=pB[:, 0:HP], func=AF.Exp,
             scale=ngrx[:, :]), inc="s")
        st["pB_free_s"] = P.val("s")
        P.pe("scalar", "s", "pB")
        P.wait("vector", "s")
        P.op("vector", lambda e: e.tensor_scalar(out=dbv[:, :], in0=ebv[:, :], scalar1=1.0,
             scalar2=None, op0=ALU.add, op1=ALU.bypass), inc="v")
        P.op("vector", lambda e: e.reciprocal(out=rbv[:, :], in_=dbv[:, :]), inc="v")
        P.op("vector", lambda e: e.tensor_scalar(out=negb[:, :], in0=rbv[:, :],
             scalar1=-2.0, scalar2=None, op0=ALU.mult, op1=ALU.bypass), inc="v")
        for h in range(HP):
            P.op("vector", lambda e, h_=h: e.tensor_scalar(out=kie[:, 128*h_:128*(h_+1)],
                 in0=ktm[:, 128*h_:128*(h_+1)], scalar1=negb[:, h_:h_+1], scalar2=None,
                 op0=ALU.mult, op1=ALU.bypass), inc="v")
        P.pb("vector", "pB")
        P.op("vector", lambda e: e.tensor_scalar(out=xgt[:, :], in0=pB[:, HP:HP+128],
             scalar1=rstdx[:, :], scalar2=None, op0=ALU.mult, op1=ALU.bypass), inc="v")
        st["pB_free_v"] = P.val("v")
        P.pe("vector", "v", "pB")
        P.op("vector", lambda e: e.tensor_tensor(out=qbv[:, :], in0=qtm[:, :],
             in1=bolnn[:, :], op=ALU.add), inc="v")
        v_all = P.val("v")
        P.wait("sync", "v", v_q)
        for p in range(HP):
            P.op("sync", lambda e, p_=p: e.dma_start(out=qfm[p_][:, :],
                 in_=qtm[:, 128*p_:128*(p_+1)], transpose=True), inc="dz", by=16)
        P.wait("sync", "v", v_k)
        for p in range(HP):
            P.op("sync", lambda e, p_=p: e.dma_start(out=kfm[p_][:, :],
                 in_=ktm[:, 128*p_:128*(p_+1)], transpose=True), inc="dz", by=16)
        P.wait("sync", "v", v_all)
        for p in range(HP):
            P.op("sync", lambda e, p_=p: e.dma_start(out=kefm[p_][:, :],
                 in_=kie[:, 128*p_:128*(p_+1)], transpose=True), inc="dz", by=16)
        P.op("sync", lambda e: e.dma_start(out=xgf[:, :], in_=xgt[:, :], transpose=True),
             inc="dz", by=16)

    def scan_phase(wi, debug):
        if stages < 1:
            return
        dz_fm = P.val("dz")
        P.wait("tensor", "v", st.get("pvn_evac_v", 0))
        P.wait("tensor", "s", st.get("pvn_evac_s", 0))
        P.pb("tensor", "pVnA", "pVnB")
        P.op("tensor", lambda e: e.matmul(pVnA.ap().rearrange("p a b -> p (a b)"),
             zer[:, :], wqs[:, 0, :], start=True, stop=True, skip_group_check=True))
        P.op("tensor", lambda e: e.matmul(pVnB.ap().rearrange("p a b -> p (a b)"),
             zer[:, :], wqs[:, 0, :], start=True, stop=True, skip_group_check=True), inc="t")
        P.pe("tensor", "t", "pVnA", "pVnB")
        P.wait("tensor", "dz", dz_fm)
        P.wait("tensor", "v", st["gaq_free_v"])
        P.pb("tensor", "pGAQ")
        for p in range(HP):
            P.op("tensor", lambda e, p_=p: e.matmul(pGAQ[:, p_, :], kefm[p_][:, :],
                 kfm[p_][:, :], start=True, stop=True), inc="t" if p == HP-1 else None)
        P.pe("tensor", "t", "pGAQ")
        P.wait("vector", "t")
        P.pb("vector", "pGAQ")
        P.op("vector", lambda e: e.tensor_scalar(out=gsb[:, :, :], in0=pGAQ[:, :, :],
             scalar1=1.0, scalar2=None, op0=ALU.mult, op1=ALU.bypass), inc="v")
        P.pe("vector", "v", "pGAQ")
        P.wait("tensor", "v")
        P.pb("tensor", "pGAQ")
        for p in range(HP):
            P.op("tensor", lambda e, p_=p: e.matmul(pGAQ[:, p_, :], kefm[p_][:, :],
                 qfm[p_][:, :], start=True, stop=True), inc="t" if p == HP-1 else None)
        P.pe("tensor", "t", "pGAQ")
        P.wait("vector", "t")
        P.pb("vector", "pGAQ")
        for p in range(HP):
            P.op("vector", lambda e, p_=p: e.tensor_tensor(out=asb[p_][:, :],
                 in0=pGAQ[:, p_, :], in1=tri[:, :], op=ALU.mult), inc="v")
        v_asb = P.val("v")
        P.pe("vector", "v", "pGAQ")
        P.wait("tensor", "v", v_asb)
        P.pb("tensor", "pGAQ")
        for k in range(NSC):
            for p in range(HP):
                P.op("tensor", lambda e, k_=k, p_=p: e.matmul(
                    pGAQ[32*p_:32*p_+16, k_, 0:16], kefm[p_][:, 32*k_:32*k_+16],
                    kfm[p_][:, 16*(2*k_+1):16*(2*k_+2)], start=True, stop=True,
                    tile_position=(0, 32*p_)),
                    inc="t" if (k == NSC-1 and p == HP-1) else None)
        P.pe("tensor", "t", "pGAQ")
        P.wait("vector", "t")
        P.pb("vector", "pGAQ")
        P.op("vector", lambda e: e.tensor_scalar(out=gstke[:, :, :], in0=pGAQ[:, 0:NSC, 0:16],
             scalar1=1.0, scalar2=None, op0=ALU.mult, op1=ALU.bypass), inc="v")
        st["gaq_free_v"] = P.val("v")
        P.pe("vector", "v", "pGAQ")
        wbt_ready = {}
        def wbt_mm(c):
            P.pb("tensor", "pWBT")
            for p in range(HP):
                P.op("tensor", lambda e, c_=c, p_=p: e.matmul(
                    pWBT[32*p_:32*p_+16, c_ % 2, 0:128], esl[:, c_, :],
                    wtg[:, 128*p_:128*(p_+1)], start=True, stop=False,
                    tile_position=(0, 32*p_)))
                P.op("tensor", lambda e, c_=c, p_=p: e.matmul(
                    pWBT[32*p_:32*p_+16, c_ % 2, 0:128], o116[:, :],
                    wbrow[0:1, p_, :], start=False, stop=True,
                    tile_position=(0, 32*p_)),
                    inc="t" if p == HP-1 else None)
            wbt_ready[c] = P.val("t")
            P.pe("tensor", "t", "pWBT")
        P.wait("tensor", "v", st["v_wtg"])
        wbt_mm(0); wbt_mm(1)

        if stages < 2:
            return
        chain_v = {}; evac_s = {}
        for c in range(W):
            k = c // 2
            if c >= 1:
                P.wait("tensor", "v", chain_v[c - 1])
                if c % 2 == 0 and k >= 1:
                    P.wait("tensor", "s", evac_s.get(k - 1, 0))
            else:
                P.wait("tensor", "v", st["s0_ready_v"])
                P.wait("tensor", "s", st["s0_ready_s"])
            P.pb("tensor", "pKH")
            for p in range(HP):
                last = (p == HP - 1)
                Kc = 32 * k
                if Kc > 0:
                    P.op("tensor", lambda e, c_=c, p_=p, K_=Kc: e.matmul(
                        pKH[32*p_:32*p_+16, c_ % 2, 0:129], gsb[0:K_, p_, 16*c_:16*(c_+1)],
                        vnnat[0:K_, p_, 0:129], start=True, stop=False,
                        tile_position=(0, 32*p_)))
                P.op("tensor", lambda e, c_=c, p_=p, st_=(Kc == 0): e.matmul(
                    pKH[32*p_:32*p_+16, c_ % 2, 0:129], kfm[p_][:, 16*c_:16*(c_+1)],
                    s0[:, p_, 0:129], start=st_, stop=(c_ % 2 == 0),
                    tile_position=(0, 32*p_)),
                    inc="t" if (last and c % 2 == 0) else None)
                if c % 2 == 1:
                    P.op("tensor", lambda e, c_=c, p_=p, k_=k: e.matmul(
                        pKH[32*p_:32*p_+16, c_ % 2, 0:129], gstke[32*p_:32*p_+16, k_, :],
                        vnstk[32*p_:32*p_+16, 2*k_, 0:129], start=False, stop=True,
                        tile_position=(32*p_, 32*p_)),
                        inc="t" if last else None)
            t_kh = P.val("t")
            P.pe("tensor", "t", "pKH")
            P.wait("vector", "t", t_kh)
            P.pb("vector", "pKH")
            P.op("vector", lambda e, c_=c: e.tensor_scalar(out=mus[:, :],
                 in0=pKH[:, c_ % 2, 128:129], scalar1=1.0/128.0, scalar2=None,
                 op0=ALU.mult, op1=ALU.bypass), inc="v")
            P.pe("vector", "v", "pKH")
            P.wait("scalar", "t", t_kh)
            P.pb("scalar", "pKH")
            P.op("scalar", lambda e, c_=c: e.activation(out=jkb[:, :], in_=pKH[:, c_ % 2, 0:128],
                 func=AF.Square, scale=IRT128, accum_out=s2R[:, :]), inc="s")
            s_sq = P.val("s")
            P.pe("scalar", "s", "pKH")
            P.wait("vector", "s", s_sq)
            P.op("vector", lambda e: e.scalar_tensor_tensor(out=nv[:, :], in0=mus[:, :],
                 scalar=mus[:, :], in1=s2R[:, :], op0=ALU.mult, op1=ALU.subtract), inc="v")
            P.wait("scalar", "v")
            P.op("scalar", lambda e: e.activation(out=lnv[:, :], in_=nv[:, :], func=AF.Ln,
                 bias=EPS, scale=-1.0), inc="s")
            P.op("scalar", lambda e: e.activation(out=rsd[:, :], in_=lnv[:, :], func=AF.Exp,
                 scale=-0.5), inc="s")
            P.wait("vector", "s")
            P.pb("vector", "pKH")
            P.op("vector", lambda e, c_=c: e.tensor_scalar(out=khh[:, :],
                 in0=pKH[:, c_ % 2, 0:128], scalar1=mus[:, :], scalar2=rsd[:, :],
                 op0=ALU.subtract, op1=ALU.mult), inc="v")
            P.pe("vector", "v", "pKH")
            P.op("vector", lambda e: e.tensor_tensor(out=t1b[:, :], in0=khh[:, :],
                 in1=w2sb[:, :], op=ALU.mult), inc="v")
            P.pb("vector", "pWBT")
            P.op("vector", lambda e, c_=c: e.scalar_tensor_tensor(out=gwb[:, :],
                 in0=t1b[:, :], scalar=0.0, in1=pWBT[:, c_ % 2, 0:128], op0=ALU.add,
                 op1=ALU.add, accum_out=mm12[:, 0:1]), inc="v")
            P.pe("vector", "v", "pWBT")
            P.op("vector", lambda e: e.scalar_tensor_tensor(out=jkb[:, :], in0=gwb[:, :],
                 scalar=1.0, in1=khh[:, :], op0=ALU.mult, op1=ALU.mult,
                 accum_out=mm12[:, 1:2]), inc="v")
            P.op("vector", lambda e: e.tensor_scalar(out=m12r[:, :], in0=mm12[:, :],
                 scalar1=rsd[:, :], scalar2=1.0/128.0, op0=ALU.mult, op1=ALU.mult), inc="v")
            P.op("vector", lambda e: e.tensor_scalar(out=xts[:, :], in0=khh[:, :],
                 scalar1=m12r[:, 1:2], scalar2=m12r[:, 0:1], op0=ALU.mult, op1=ALU.add), inc="v")
            P.op("vector", lambda e, c_=c: e.scalar_tensor_tensor(out=vnstk[:, c_, 0:128],
                 in0=gwb[:, :], scalar=rsd[:, :], in1=xts[:, :], op0=ALU.mult,
                 op1=ALU.subtract, accum_out=vnstk[:, c_, 128:129]), inc="v", lp=True)
            chain_v[c] = P.val("v")
            if c + 2 < W:
                P.wait("tensor", "v", chain_v[c])
                wbt_mm(c + 2)
            P.wait("tensor", "v", chain_v[c])
            P.pb("tensor", "pVnA", "pVnB")
            for p in range(HP):
                P.op("tensor", lambda e, c_=c, p_=p, k_=k: e.matmul(
                    pVn[p_ // 2][32*k_:32*k_+32, p_ % 2, 0:129],
                    slh[:, c_ % 2, p_, :], vnstk[:, c_, 0:129],
                    start=False, stop=(c_ % 2 == 1),
                    tile_position=(0, 32*k_), skip_group_check=True),
                    inc="t" if p == HP-1 else None)
            t_rs = P.val("t")
            P.pe("tensor", "t", "pVnA", "pVnB")
            if c % 2 == 1:
                P.wait("vector", "t", t_rs)
                P.pb("vector", "pVnA", "pVnB")
                P.op("vector", lambda e, k_=k: e.tensor_scalar(
                     out=vnnat[32*k_:32*k_+32, 0:2, 0:129], in0=pVnA[32*k_:32*k_+32, :, 0:129],
                     scalar1=1.0, scalar2=None, op0=ALU.mult, op1=ALU.bypass), inc="v")
                P.op("vector", lambda e, k_=k: e.tensor_scalar(
                     out=vnnat[32*k_:32*k_+32, 2:4, 0:129], in0=pVnB[32*k_:32*k_+32, :, 0:129],
                     scalar1=1.0, scalar2=None, op0=ALU.mult, op1=ALU.bypass), inc="v")
                chain_v[c] = P.val("v")
                st["pvn_evac_v"] = P.val("v")
                P.pe("vector", "v", "pVnA", "pVnB")
        v_chain = P.val("v"); s_chain = P.val("s")

        P.wait("tensor", "v", v_chain)
        P.wait("tensor", "s", s_chain)
        P.pb("tensor", "pGAQ", "pWBT")
        for p in range(HP):
            P.op("tensor", lambda e, p_=p: e.matmul(pGAQ[:, p_, 0:128], qfm[p_][:, :],
                 s0[:, p_, 0:128], start=True, stop=False))
            P.op("tensor", lambda e, p_=p: e.matmul(pWBT[:, 0, 128+p_:129+p_], qfm[p_][:, :],
                 s0[:, p_, 128:129], start=True, stop=False))
            P.op("tensor", lambda e, p_=p: e.matmul(pGAQ[:, p_, 0:128], asb[p_][:, :],
                 vnnat[:, p_, 0:128], start=False, stop=True))
            P.op("tensor", lambda e, p_=p: e.matmul(pWBT[:, 0, 128+p_:129+p_], asb[p_][:, :],
                 vnnat[:, p_, 128:129], start=False, stop=True),
                 inc="t" if p == HP-1 else None)
        t_qh = P.val("t")
        P.pe("tensor", "t", "pGAQ", "pWBT")
        P.pb("tensor", "pS")
        for p in range(HP):
            P.op("tensor", lambda e, p_=p, w_=wi: e.matmul(pS[:, p_, :],
                 kie[:, 128*p_:128*(p_+1)], vnnat[:, p_, 0:128],
                 start=False, stop=(w_ == NW - 1),
                 skip_group_check=True), inc="t" if p == HP-1 else None)
        t_ds = P.val("t")
        P.pe("tensor", "t", "pS")
        P.wait("vector", "t", t_ds)
        P.pb("vector", "pS")
        for p in range(HP):
            P.op("vector", lambda e, p_=p: e.tensor_scalar(out=s0[:, p_, 0:128],
                 in0=pS[:, p_, :], scalar1=1.0, scalar2=None, op0=ALU.mult,
                 op1=ALU.add, accum_out=s0[:, p_, 128:129]), inc="v", lp=True)
        st["pS_free_v"] = P.val("v"); st["s0_ready_v"] = P.val("v")
        P.pe("vector", "v", "pS")
        st["pS_free_s"] = 0; st["s0_ready_s"] = 0

        if stages < 4:
            return
        P.wait("vector", "t", t_qh)
        P.pb("vector", "pWBT")
        P.op("vector", lambda e: e.tensor_scalar(out=muo[:, :], in0=pWBT[:, 0, 128:132],
             scalar1=1.0/128.0, scalar2=None, op0=ALU.mult, op1=ALU.bypass), inc="v")
        P.pe("vector", "v", "pWBT")
        P.wait("scalar", "t", t_qh)
        P.pb("scalar", "pGAQ")
        for p in range(HP):
            P.op("scalar", lambda e, p_=p: e.activation(out=jkb[:, :], in_=pGAQ[:, p_, 0:128],
                 func=AF.Square, scale=IRT128, accum_out=so2[:, p_:p_+1]), inc="s")
        P.pe("scalar", "s", "pGAQ")
        P.wait("vector", "s")
        for p in range(HP):
            P.op("vector", lambda e, p_=p: e.scalar_tensor_tensor(out=nvo[:, p_:p_+1],
                 in0=muo[:, p_:p_+1], scalar=muo[:, p_:p_+1], in1=so2[:, p_:p_+1],
                 op0=ALU.mult, op1=ALU.subtract), inc="v")
        P.wait("scalar", "v")
        P.op("scalar", lambda e: e.activation(out=lno[:, :], in_=nvo[:, :], func=AF.Ln,
             bias=EPS, scale=-1.0), inc="s")
        P.op("scalar", lambda e: e.activation(out=rso[:, :], in_=lno[:, :], func=AF.Exp,
             scale=-0.5), inc="s")
        P.wait("vector", "s")
        P.pb("vector", "pGAQ")
        for p in range(HP):
            P.op("vector", lambda e, p_=p: e.tensor_scalar(out=qhh[:, :],
                 in0=pGAQ[:, p_, 0:128], scalar1=muo[:, p_:p_+1], scalar2=rso[:, p_:p_+1],
                 op0=ALU.subtract, op1=ALU.mult), inc="v")
            P.op("vector", lambda e, p_=p: e.tensor_tensor(out=obuf[:, :], in0=qhh[:, :],
                 in1=wolnb[:, p_, :], op=ALU.mult), inc="v")
            P.op("vector", lambda e, p_=p: e.tensor_tensor(out=onat[:, 128*p_:128*(p_+1)],
                 in0=obuf[:, :], in1=qbv[:, 128*p_:128*(p_+1)], op=ALU.add), inc="v")
        st["gaq_free_v"] = P.val("v")
        v_onat = P.val("v")
        P.pe("vector", "v", "pGAQ")

        P.wait("tensor", "dz", dz_fm)
        P.wait("tensor", "s", st["pA_free_s"])
        P.wait("tensor", "v", v_onat)
        P.pb("tensor", "pA")
        P.op("tensor", lambda e: e.matmul(pA[:, 0:512], xgf[:, :], wg2s[:, :],
             start=True, stop=True), inc="t")
        t_gate = P.val("t")
        P.pe("tensor", "t", "pA")
        P.wait("scalar", "t", t_gate)
        P.pb("scalar", "pA")
        P.op("scalar", lambda e: e.activation(out=sgm[:, :], in_=pA[:, 0:512], func=AF.Exp,
             scale=-1.0), inc="s")
        st["pA_free_s"] = P.val("s")
        P.pe("scalar", "s", "pA")
        P.op("scalar", lambda e: e.activation(out=sg2[:, :], in_=sgm[:, :], func=AF.Ln,
             bias=1.0, scale=1.0), inc="s")
        P.op("scalar", lambda e: e.activation(out=rec[:, :], in_=sg2[:, :], func=AF.Exp,
             scale=-1.0), inc="s")
        P.wait("vector", "s")
        P.op("vector", lambda e: e.tensor_tensor(out=ogb[:, :], in0=onat[:, :], in1=rec[:, :],
             op=ALU.mult), inc="v")
        P.wait("scalar", "v")
        P.op("scalar", lambda e: e.activation(out=jk[:, :], in_=ogb[:, :], func=AF.Square,
             scale=IRT128), inc="s")
        P.wait("vector", "s")
        P.op("vector", lambda e: e.reduce_sum(out=ro2[:, :, :],
             in_=jk.ap().rearrange("p (h e) -> p h e", h=HP), axis=AX.X), inc="v")
        P.wait("scalar", "v")
        P.op("scalar", lambda e: e.activation(out=rol[:, :, :], in_=ro2[:, :, :], func=AF.Ln,
             bias=EPS), inc="s")
        P.op("scalar", lambda e: e.activation(out=rsog[:, :, :], in_=rol[:, :, :],
             func=AF.Exp, scale=-0.5), inc="s")
        P.wait("vector", "s")
        for h in range(HP):
            P.op("vector", lambda e, h_=h: e.tensor_scalar(out=ogc[:, 128*h_:128*(h_+1)],
                 in0=ogb[:, 128*h_:128*(h_+1)], scalar1=rsog[:, h_, :], scalar2=None,
                 op0=ALU.mult, op1=ALU.bypass), inc="v")
        P.op("vector", lambda e: e.tensor_tensor(out=ogc[:, :], in0=ogc[:, :], in1=nrmb[:, :],
             op=ALU.mult), inc="v")
        v_ogc = P.val("v")
        P.wait("sync", "v", v_ogc)
        P.wait("sync", "dz")
        P.wait("sync", "t", st.get("at_t", 0))
        if debug:
            P.wait("sync", "dg")
        for h in range(HP):
            P.op("sync", lambda e, h_=h: e.dma_start(out=ogfm[h_][:, :],
                 in_=ogc[:, 128*h_:128*(h_+1)], transpose=True), inc="dz", by=16)
        dz_og = P.val("dz")
        if debug:
            P.wait("scalar", "dz", dz_og)
            for h in range(HP):
                P.op("scalar", lambda e, h_=h, w_=wi: e.dma_start(
                     out=og_loc[w_, 128*h_:128*(h_+1), :], in_=ogfm[h_][:, :]), inc="dz", by=16)
            P.wait("gpsimd", "dz", dz_og)
            for h in range(HP):
                P.op("gpsimd", lambda e, h_=h, w_=wi: e.dma_start(
                     out=ap["out"][w_, 128*h_:128*(h_+1), :], in_=ogfm[h_][:, :]),
                     inc="dg", by=16)
            return
        bb = wi % 2
        P.wait("tensor", "dz", dz_og)
        if wi == 0:
            P.wait("tensor", "dg", dg_wo3_val)
        P.wait("tensor", "v", P.val("v"))
        P.wait("tensor", "s", P.val("s"))
        P.pb("tensor", "pA", "pB", "pKH", "pWBT")
        ab = [pA[:, 0:512], pB[:, 0:512],
              pKH.ap().rearrange("p a b -> p (a b)"),
              pWBT.ap().rearrange("p a b -> p (a b)")]
        for q in range(4):
            for h in range(HP):
                P.op("tensor", lambda e, q_=q, h_=h: e.matmul(
                    ab[q_], ogfm[h_][:, :], wos3[:, h_, 512*q_:512*(q_+1)],
                    start=(h_ == 0), stop=(h_ == HP-1)),
                    inc="t" if (q == 3 and h == HP-1) else None)
        st["at_t"] = P.val("t")
        P.pe("tensor", "t", "pA", "pB", "pKH", "pWBT")
        P.wait("vector", "t", P.val("t"))
        P.wait("vector", f"so{bb}", st.get(f"at_so_{bb}", 0))
        P.pb("vector", "pA", "pB", "pKH", "pWBT")
        for q in range(4):
            P.op("vector", lambda e, q_=q, b_=bb, a_=ab[q]: e.tensor_scalar(
                 out=atb3[b_][:, 512*q_:512*(q_+1)], in0=a_, scalar1=1.0,
                 scalar2=None, op0=ALU.mult, op1=ALU.bypass), inc="v", lp=True)
        P.pe("vector", "v", "pA", "pB", "pKH", "pWBT")
        P.wait("scalar", "v", P.val("v"))
        P.op("scalar", lambda e, w_=wi, b_=bb: e.dma_start(
             out=attn_part[TW*w_:TW*(w_+1), :], in_=atb3[b_][:, :]), inc=f"so{bb}", by=16)
        st[f"at_so_{bb}"] = P.val(f"so{bb}")

    for wi in range(NW):
        stats_phase(wi)
        if wi + 1 < NW:
            load_tile(wi + 1)
        proj_phase(wi)
        scan_phase(wi, debug_og)
        st["buf"] = 1 - st["buf"]

    if debug_og:
        for s_ in ("dg", "dz", "dy0", "dy1", "dx0", "dx1", "so0", "so1"):
            P.wait("gpsimd", s_)
        finalize(nc, P)
        return nc, {"NW": NW}

    s12.close()
    NT3 = N3 // TW
    sb3 = lambda name, shape, dt, **kw: outer.enter_context(nc.sbuf_tensor(name, list(shape), dt, **kw))
    def g3_load(dst, src):
        P.op("gpsimd", lambda e, d=dst, s=src: e.dma_start(out=d, in_=s), inc="dg", by=16)
    attnb = [pA[:, 0:512], pB[:, 0:512],
             pGAQ.ap().rearrange("p a b -> p (a b)"),
             pS.ap().rearrange("p a b -> p (a b)")]
    st3 = {}

    P.wait("gpsimd", "so0"); P.wait("gpsimd", "so1")
    P.op("gpsimd", lambda e: e.collective_compute(
        "ReduceScatter", ALU.add,
        replica_groups=[[0, 1, 2, 3], [4, 5, 6, 7]],
        ins=[attn_part[:, :]], outs=[attn_q[:, :]]), inc="cc")

    at3 = [sb3(f"at3{i}", (128, D), BF16) for i in range(2)]
    x3 = [sb3(f"x3{i}", (128, D), F32) for i in range(2)]
    xmid = sb3("xmid", (128, D), F32)
    xf3 = sb3("xf3", (128, D), BF16)
    xff = sb3("xff", (128, NK, 128), BF16, is_transpose_dest=True)
    hmsk = sb3("hmsk", (128, 512), BF16)
    hfm = [sb3(f"hfm{j}", (128, 128), BF16, is_transpose_dest=True) for j in range(HP)]
    out3 = [sb3(f"out3{i}", (128, D), F32) for i in range(2)]
    sx2r = sb3("sx2r", (128, 1), F32)
    mur3 = sb3("mur3", (1, 128), BF16)
    mux3 = sb3("mux3", (128, 1), F32)
    nv3 = sb3("nv3", (128, 1), F32); rstd3 = sb3("rstd3", (128, 1), F32)
    mx8 = sb3("mx8", (128, 1), F32); nmx8 = sb3("nmx8", (128, 1), F32)
    z8 = sb3("z8", (128, 1), F32); wg8 = sb3("wg8", (128, 1), F32)
    ex8 = sb3("ex8", (128, 8), F32)
    msk8 = sb3("msk8", (128, 8), F32)

    g3_load(aws[:, :, :], ap["awr"].rearrange("(a p) n -> p a n", p=128))
    g3_load(bss[:, :, :], ap["bstk"][:, :, :])
    g3_load(brs[:, :], ap["brow"][:, :])
    g3_load(id3[:, :], c_ident[:, :])
    dg_w3 = P.val("dg")

    st3["free_v"] = P.val("v"); st3["free_s"] = P.val("s")
    def load3(j):
        b = j % 2
        P.wait("sync", "cc")
        P.wait("sync", "v", st3.get(f"v_xmid_{b}", 0))
        P.op("sync", lambda e, j_=j, b_=b: e.dma_start(out=at3[b_][:, :],
             in_=attn_q[TW*j_:TW*(j_+1), :]), inc=f"dy{b}", by=16)
        P.op("sync", lambda e, j_=j, b_=b: e.dma_start(out=x3[b_][:, :],
             in_=ap["x3s"][TW*j_:TW*(j_+1), :]), inc=f"dy{b}", by=16)
        st3[f"ld_dy_{b}"] = P.val(f"dy{b}")
    load3(0)

    for j in range(NT3):
        b = j % 2
        if j + 1 < NT3:
            load3(j + 1)
        P.wait("vector", f"dy{b}", st3[f"ld_dy_{b}"])
        P.wait("vector", "s", st3.get("xmid_s", 0))
        P.wait("vector", "v", st3.get("out_done_v", 0))
        for q in range(4):
            P.op("vector", lambda e, q_=q, b_=b: e.tensor_tensor(
                 out=xmid[:, 512*q_:512*(q_+1)], in0=at3[b_][:, 512*q_:512*(q_+1)],
                 in1=x3[b_][:, 512*q_:512*(q_+1)], op=ALU.add), inc="v")
        v_xmid = P.val("v")
        P.op("vector", lambda e: e.reduce_sum(out=mux3[:, :], in_=xmid[:, :], axis=AX.X), inc="v")
        P.op("vector", lambda e: e.tensor_scalar(out=mux3[:, :], in0=mux3[:, :],
             scalar1=1.0/D, scalar2=None, op0=ALU.mult, op1=ALU.bypass), inc="v")
        P.wait("scalar", "v", v_xmid)
        P.op("scalar", lambda e: e.activation(out=xf3[:, :], in_=xmid[:, :], func=AF.Square,
             scale=float(1.0/np.sqrt(D)), accum_out=sx2r[:, :]), inc="s")
        st3["xmid_s"] = P.val("s")
        P.wait("vector", "s")
        P.op("vector", lambda e: e.scalar_tensor_tensor(out=nv3[:, :], in0=mux3[:, :],
             scalar=mux3[:, :], in1=sx2r[:, :], op0=ALU.mult, op1=ALU.subtract), inc="v")
        P.wait("scalar", "v")
        P.op("scalar", lambda e: e.activation(out=lnv[:, :], in_=nv3[:, :], func=AF.Ln,
             bias=LNEPS, scale=-1.0), inc="s")
        P.op("scalar", lambda e: e.activation(out=rstd3[:, :], in_=lnv[:, :], func=AF.Exp,
             scale=-0.5), inc="s")
        P.wait("vector", "s")
        P.op("vector", lambda e: e.tensor_scalar(out=xf3[:, :], in0=xmid[:, :],
             scalar1=mux3[:, :], scalar2=rstd3[:, :], op0=ALU.subtract, op1=ALU.mult), inc="v")
        v_xf = P.val("v")
        P.wait("sync", "v", v_xf)
        for kt in range(NK):
            P.op("sync", lambda e, k_=kt: e.dma_start(out=xff[:, k_, :],
                 in_=xf3[:, 128*k_:128*(k_+1)], transpose=True), inc="dz", by=16)
        dz_xf = P.val("dz")
        P.wait("tensor", "v", P.val("v"))
        P.wait("tensor", "s", st3["free_s"])
        P.wait("tensor", "dz", dz_xf)
        P.pb("tensor", "pA", "pB")
        for kt in range(NK):
            P.op("tensor", lambda e, k_=kt: e.matmul(pA[:, 0:512], xff[:, k_, :],
                 aws[:, k_, 0:512], start=(k_ == 0), stop=(k_ == NK-1)))
            P.op("tensor", lambda e, k_=kt: e.matmul(pB[:, 0:8], xff[:, k_, :],
                 aws[:, k_, 512:520], start=(k_ == 0), stop=False))
        P.op("tensor", lambda e: e.matmul(pB[:, 0:8], o1r[:, :], brs[:, :],
             start=False, stop=True), inc="t")
        t_h = P.val("t")
        P.pe("tensor", "t", "pA", "pB")
        P.wait("vector", "t", t_h)
        P.pb("vector", "pB")
        P.op("vector", lambda e: e.reduce_max(out=mx8[:, :], in_=pB[:, 0:8], axis=AX.X), inc="v")
        P.op("vector", lambda e: e.tensor_scalar(out=nmx8[:, :], in0=mx8[:, :],
             scalar1=-1.0, scalar2=None, op0=ALU.mult, op1=ALU.bypass), inc="v")
        P.pe("vector", "v", "pB")
        P.wait("scalar", "v")
        P.pb("scalar", "pB")
        P.op("scalar", lambda e: e.activation(out=ex8[:, :], in_=pB[:, 0:8], func=AF.Exp,
             bias=nmx8[:, :], accum_out=z8[:, :]), inc="s")
        P.pe("scalar", "s", "pB")
        P.wait("vector", "s")
        P.op("vector", lambda e: e.reciprocal(out=wg8[:, :], in_=z8[:, :]), inc="v")
        P.op("vector", lambda e: e.tensor_scalar(out=wg8[:, :], in0=wg8[:, :],
             scalar1=1.0/64.0, scalar2=None, op0=ALU.mult, op1=ALU.bypass), inc="v")
        P.pb("vector", "pB")
        P.op("vector", lambda e: e.tensor_scalar(out=msk8[:, :], in0=pB[:, 0:8],
             scalar1=mx8[:, :], scalar2=None, op0=ALU.is_equal, op1=ALU.bypass), inc="v")
        P.pe("vector", "v", "pB")
        P.wait("vector", "t", t_h)
        P.pb("vector", "pA")
        for ex in range(8):
            P.op("vector", lambda e, x_=ex: e.tensor_scalar(out=hmsk[:, 64*x_:64*(x_+1)],
                 in0=pA[:, 64*x_:64*(x_+1)], scalar1=msk8[:, x_:x_+1], scalar2=None,
                 op0=ALU.mult, op1=ALU.bypass), inc="v")
        v_hm = P.val("v")
        P.pe("vector", "v", "pA")
        P.wait("sync", "v", v_hm)
        for jj in range(HP):
            P.op("sync", lambda e, j_=jj: e.dma_start(out=hfm[j_][:, :],
                 in_=hmsk[:, 128*j_:128*(j_+1)], transpose=True), inc="dz", by=16)
        dz_h = P.val("dz")
        P.wait("tensor", "v", v_hm)
        P.wait("tensor", "dz", dz_h)
        P.pb("tensor", "pA", "pB", "pGAQ", "pS")
        for jj in range(HP):
            for q in range(4):
                P.op("tensor", lambda e, j_=jj, q_=q: e.matmul(
                    attnb[q_], hfm[j_][:, :], bss[:, j_, 512*q_:512*(q_+1)],
                    start=(j_ == 0), stop=(j_ == HP-1)),
                    inc="t" if (jj == HP-1 and q == 3) else None)
        t_y = P.val("t")
        P.pe("tensor", "t", "pA", "pB", "pGAQ", "pS")
        P.wait("vector", "t", t_y)
        P.wait("vector", f"so{b}", st3.get(f"store_so_{b}", 0))
        P.pb("vector", "pA", "pB", "pGAQ", "pS")
        for q in range(4):
            P.op("vector", lambda e, q_=q, b_=b: e.scalar_tensor_tensor(
                 out=out3[b_][:, 512*q_:512*(q_+1)], in0=attnb[q_], scalar=wg8[:, :],
                 in1=xmid[:, 512*q_:512*(q_+1)], op0=ALU.mult, op1=ALU.add), inc="v")
        st3["free_v"] = P.val("v")
        st3["out_done_v"] = P.val("v")
        P.pe("vector", "v", "pA", "pB", "pGAQ", "pS")
        st3["free_s"] = P.val("s")
        v_out = P.val("v")
        P.wait("scalar", "v", v_out)
        P.op("scalar", lambda e, j_=j, b_=b: e.dma_start(out=ap["out"][TW*j_:TW*(j_+1), :],
             in_=out3[b_][:, :]), inc=f"so{b}", by=16)
        st3[f"store_so_{b}"] = P.val(f"so{b}")
        st3[f"v_xmid_{b}"] = v_xmid

    for s_ in ("dg", "dz", "dy0", "dy1", "dx0", "dx1", "so0", "so1"):
        P.wait("gpsimd", s_)
    finalize(nc, P)
    return nc, {"NW": NW}


def finalize(nc, P):
    with contextlib.ExitStack() as semstack:
        for name in P.cnt:
            P.sems[name] = semstack.enter_context(nc.semaphore(name))
        with nc.Block() as block:
            @block.tensor
            def _(tensor):
                for fn in P.ops["tensor"]:
                    fn(tensor)
            @block.vector
            def _(vector):
                for fn in P.ops["vector"]:
                    fn(vector)
            @block.scalar
            def _(scalar):
                for fn in P.ops["scalar"]:
                    fn(scalar)
            @block.sync
            def _(sync):
                for fn in P.ops["sync"]:
                    fn(sync)
            @block.gpsimd
            def _(gpsimd):
                for fn in P.ops["gpsimd"]:
                    fn(gpsimd)


def prep_core_inputs(core, x, Wq, Wk, Wv, Wo, Wb, ln_w, ln_b, Wg1, Wg2, norm_w,
                     A, Bw, Wr, br):
    b = core // 4
    hg = core % 4
    cs = slice(512 * hg, 512 * (hg + 1))
    hs = slice(4 * hg, 4 * (hg + 1))
    T = x.shape[1]
    f32 = np.float32
    xb = np.ascontiguousarray(x[b]).astype(f32, copy=False)
    xbT = np.ascontiguousarray(x[b].T)
    lw = ln_w[hs]
    lb = ln_b[hs]
    wstk = np.zeros((128, 128), f32)
    for p in range(4):
        wstk[32*p:32*p+16, :] = lw[p][None, :]
    woln = np.broadcast_to(lw[None, :, :], (128, 4, 128)).copy()
    boln = np.broadcast_to(lb.reshape(1, 512), (128, 512)).copy()
    wnat = np.broadcast_to(lw.reshape(1, 512), (128, 512)).copy()
    nrm = np.broadcast_to(norm_w[cs][None, :], (128, 512)).copy()
    awr = np.concatenate([A.transpose(2, 0, 1).reshape(2048, 512), Wr.T], axis=1)
    bstk = np.zeros((128, 4, 2048), f32)
    for jj in range(4):
        bstk[0:64, jj, :] = Bw[2*jj].T
        bstk[64:128, jj, :] = Bw[2*jj+1].T
    q3 = core % 4
    N3 = T // 4
    x3s = np.ascontiguousarray(x[b][N3*q3:N3*(q3+1)])
    return {
        "xb": xb, "xbT": xbT,
        "wq": np.ascontiguousarray(Wq[:, cs]), "wk": np.ascontiguousarray(Wk[:, cs]),
        "wv": np.ascontiguousarray(Wv[:, cs]),
        "wbeta": np.ascontiguousarray(Wb[:, hs]),
        "lnw": np.ascontiguousarray(lw[None]), "lnb": np.ascontiguousarray(lb[None]),
        "wstk": wstk, "woln": woln, "boln": boln, "wnat": wnat, "nrm": nrm,
        "wg1": np.ascontiguousarray(Wg1), "wg2": np.ascontiguousarray(Wg2[:, cs]),
        "wo": np.ascontiguousarray(Wo[cs, :]), "awr": np.ascontiguousarray(awr),
        "bstk": bstk, "brow": np.ascontiguousarray(br[None, :]).astype(f32),
        "x3s": x3s,
    }


T_FULL = 8192
N3_FULL = T_FULL // 4

_cache = {}

def kernel(L0_attn_x, L0_attn_Wq, L0_attn_Wk, L0_attn_Wv, L0_attn_Wo,
           L0_attn_Wbeta, L0_attn_ln_w, L0_attn_ln_b, L0_attn_Wg1, L0_attn_Wg2,
           L0_attn_norm_w, L0_ffn_A, L0_ffn_B, L0_ffn_Wr, L0_ffn_br):
    from concourse.bass_utils import run_bass_kernel_spmd
    x = np.asarray(L0_attn_x, np.float32)
    args = [np.asarray(a, np.float32) for a in
            (L0_attn_Wq, L0_attn_Wk, L0_attn_Wv, L0_attn_Wo, L0_attn_Wbeta,
             L0_attn_ln_w, L0_attn_ln_b, L0_attn_Wg1, L0_attn_Wg2, L0_attn_norm_w,
             L0_ffn_A, L0_ffn_B, L0_ffn_Wr, L0_ffn_br)]
    ims = [prep_core_inputs(c, x, *args) for c in range(8)]
    if "nc" not in _cache:
        _cache["nc"], _ = build(T_FULL, debug_og=False)
    res = run_bass_kernel_spmd(_cache["nc"], ims, core_ids=list(range(8)))
    out = np.empty((2, T_FULL, 2048), np.float32)
    for c in range(8):
        b, q3 = c // 4, c % 4
        out[b, N3_FULL*q3:N3_FULL*(q3+1)] = res.results[c]["out"]
    return out


# version 04 (final)
# speedup vs baseline: 1.0332x; 1.0058x over previous
import numpy as np
import ml_dtypes
import contextlib
import concourse.bass as bass
import concourse.mybir as mybir

F32 = mybir.dt.float32
BF16 = mybir.dt.bfloat16
AF = mybir.ActivationFunctionType
ALU = mybir.AluOpType
AX = mybir.AxisListType
BF = ml_dtypes.bfloat16

CHUNK = 16
W = 8
TW = 128
HP = 4
Dh = 128
D = 2048
NK = 16
EPS = 1e-6
LNEPS = 1e-5
IRT128 = float(np.sqrt(1.0 / 128.0))


class Prog:
    ENGS = ("tensor", "vector", "scalar", "sync", "gpsimd")

    def __init__(self):
        self.ops = {e: [] for e in self.ENGS}
        self.cnt = {}
        self.sems = {}

    def declare(self, *names):
        for n in names:
            self.cnt[n] = 0

    def wait(self, eng, sem, val=None):
        if val is None:
            val = self.cnt[sem]
        if val <= 0:
            return
        self.ops[eng].append(lambda e, s=sem, v=val: e.wait_ge(self.sems[s], v))

    def op(self, eng, build, inc=None, by=1, lp=False):
        if eng == "vector":
            self.wait(eng, "v")
        elif eng == "scalar":
            self.wait(eng, "s")
        if inc is not None:
            self.cnt[inc] += by
        def fn(e, build=build, inc=inc, by=by, lp=lp):
            if lp:
                with self.nc.allow_low_precision(reason="bf16 sum columns by design"):
                    ins = build(e)
            else:
                ins = build(e)
            if inc is not None:
                ins.then_inc(self.sems[inc], by)
        self.ops[eng].append(fn)

    def val(self, sem):
        return self.cnt[sem]

    def pb(self, eng, *banks):
        if not hasattr(self, "bankmarks"):
            self.bankmarks = {}
        for b in banks:
            m = self.bankmarks.get(b)
            if m and m[0] != eng:
                self.wait(eng, m[1], m[2])

    def pe(self, eng, sem, *banks):
        if not hasattr(self, "bankmarks"):
            self.bankmarks = {}
        for b in banks:
            self.bankmarks[b] = (eng, sem, self.cnt[sem])


def host_consts():
    ident = np.eye(128, dtype=BF)
    esel = np.zeros((128, W, 16), dtype=BF)
    for c in range(W):
        for t in range(16):
            esel[16 * c + t, c, t] = -1.0
    triu = np.triu(np.ones((128, 128), np.float32)).astype(BF)
    onesb = np.ones((128, 1), dtype=BF)
    ones116 = np.ones((1, 16), dtype=BF)
    ones1r = np.ones((1, 128), dtype=BF)
    selh = np.zeros((2, HP, 128, 32), dtype=BF)
    for h in range(2):
        for p in range(HP):
            for t in range(16):
                selh[h, p, 32 * p + t, 16 * h + t] = 1.0
    return ident, esel, triu, onesb, ones116, ones1r, selh


def build(T, debug_og=False, stages=99):
    NW = T // TW
    NSC = W // 2
    nc = bass.Bass("TRN2", target_bir_lowering=False)
    P = Prog()
    P.nc = nc
    P.declare("t", "v", "s", "dy0", "dy1", "dx0", "dx1", "so0", "so1", "dg", "dz", "cc")

    ap = {}
    def inp(name, shape, dt=F32):
        ap[name] = nc.dram_tensor(name, list(shape), dt, kind="ExternalInput").ap()
    inp("xb", (T, D)); inp("xbT", (D, T))
    inp("wq", (D, 512)); inp("wk", (D, 512)); inp("wv", (D, 512))
    inp("wbeta", (D, HP))
    inp("lnw", (1, HP, Dh)); inp("lnb", (1, HP, Dh))
    inp("wstk", (128, 128))
    inp("woln", (128, HP, Dh)); inp("boln", (128, HP * Dh))
    inp("wnat", (128, HP * Dh)); inp("nrm", (128, HP * Dh))
    inp("wg1", (D, 128)); inp("wg2", (128, 512))
    inp("wo", (512, D)); inp("awr", (D, 520)); inp("bstk", (128, HP, D))
    inp("brow", (1, 8))
    N3 = T // 4
    inp("x3s", (N3, D))
    if debug_og:
        ap["out"] = nc.dram_tensor("out", [NW, 512, 128], F32, kind="ExternalOutput").ap()
    else:
        ap["out"] = nc.dram_tensor("out", [N3, D], F32, kind="ExternalOutput").ap()

    og_loc = nc.dram_tensor("og_loc", [NW, 512, 128], BF16, kind="Internal").ap()
    attn_part = nc.dram_tensor("attn_part", [T, D], BF16, kind="Internal").ap()
    attn_q = nc.dram_tensor("attn_q", [N3, D], BF16, kind="Internal").ap()

    identN, eselN, triuN, onesbN, o116N, o1rN, selhN = host_consts()
    c_ident = nc.inline_tensor(identN, "c_ident").ap()
    c_esel = nc.inline_tensor(eselN, "c_esel").ap()
    c_triu = nc.inline_tensor(triuN, "c_triu").ap()
    c_onesb = nc.inline_tensor(onesbN, "c_onesb").ap()
    c_o116 = nc.inline_tensor(o116N, "c_o116").ap()
    c_o1r = nc.inline_tensor(o1rN, "c_o1r").ap()
    c_one1 = nc.inline_tensor(np.ones((1, 1), np.float32), "c_one1").ap()
    c_zer = nc.inline_tensor(np.zeros((128, 128), BF), "c_zer").ap()
    _cvals = [LNEPS, EPS, -1.0, -0.5, IRT128, float(1.0/np.sqrt(D)), -2.0, 1.0]
    c_cvals = nc.inline_tensor(
        np.broadcast_to(np.asarray(_cvals, np.float32)[None, :], (128, len(_cvals))).copy(),
        "c_cvals").ap()
    c_selh = nc.inline_tensor(selhN.reshape(2 * HP * 128, 32), "c_selh").ap()

    outer = contextlib.ExitStack()
    s12 = contextlib.ExitStack()
    def sbo(name, shape, dt, **kw): return outer.enter_context(nc.sbuf_tensor(name, list(shape), dt, **kw))
    def sb1(name, shape, dt, **kw): return s12.enter_context(nc.sbuf_tensor(name, list(shape), dt, **kw))
    def pso(name, shape): return outer.enter_context(nc.psum_tensor(name, list(shape), F32))

    jk = sbo("jk", (128, 512), F32)
    wos3 = sbo("wos3", (128, 4, D), BF16)
    atb3 = [sbo(f"atb3{i}", (128, D), BF16) for i in range(2)]
    aws = sbo("aws", (128, NK, 520), BF16)
    bss = sbo("bss", (128, HP, D), BF16)
    brs = sbo("brs", (1, 8), BF16)
    id3 = sbo("id3", (128, 128), BF16)
    cvals = sbo("cvals", (128, 8), F32)
    for _i, _v in enumerate(_cvals):
        nc.const_aps.register(F32, float(_v), cvals[:, _i:_i+1])
    jkb = sbo("jkb", (128, 128), BF16)
    ogfm = [sbo(f"ogfm{h}", (128, 128), BF16, is_transpose_dest=True) for h in range(HP)]
    on1 = sbo("on1", (128, 1), BF16)
    o1r = sbo("o1r", (1, 128), BF16)
    zer = sbo("zer", (128, 128), BF16)
    lnv = sbo("lnv", (128, 1), F32)
    pA = pso("pA", (128, 512)); pB = pso("pB", (128, 512))
    pKH = pso("pKH", (128, 2, 256))
    pS = pso("pS", (128, HP, 128))
    pGAQ = pso("pGAQ", (128, HP, 128))
    pWBT = pso("pWBT", (128, 2, 256))
    pVnA = pso("pVnA", (128, 2, 256)); pVnB = pso("pVnB", (128, 2, 256))
    pVn = [pVnA, pVnB]

    wqs = sb1("wqs", (128, NK, 512), BF16); wks = sb1("wks", (128, NK, 512), BF16)
    wvs = sb1("wvs", (128, NK, 512), BF16)
    wbs = sb1("wbs", (128, NK, HP), BF16); wg1s = sb1("wg1s", (128, NK, 128), BF16)
    wg2s = sb1("wg2s", (128, 512), BF16)
    idb = sb1("idb", (128, 128), BF16); esl = sb1("esl", (128, W, 16), BF16)
    tri = sb1("tri", (128, 128), BF16)
    o116 = sb1("o116", (1, 16), BF16)
    slh = sb1("slh", (128, 2, HP, 32), BF16)
    lnwr = sb1("lnwr", (1, HP, Dh), F32); lnbr = sb1("lnbr", (1, HP, Dh), F32)
    wbrow = sb1("wbrow", (1, HP, Dh), BF16)
    wstks = sb1("wstks", (128, 128), F32); w2sb = sb1("w2sb", (128, 128), BF16)
    wolnb = sb1("wolnb", (128, HP, Dh), BF16)
    bolnn = sb1("bolnn", (128, 512), F32)
    wnatb = sb1("wnatb", (128, 512), BF16); nrmb = sb1("nrmb", (128, 512), BF16)
    csw = sb1("csw", (1, 1668), BF16)
    xtm = [sb1(f"xtm{i}", (128, D), F32) for i in range(2)]
    xfm = [sb1(f"xfm{i}", (128, NK, 128), BF16) for i in range(2)]
    qtm = sb1("qtm", (128, 512), BF16); ktm = sb1("ktm", (128, 512), BF16)
    vtm = sb1("vtm", (128, 512), BF16); kie = sb1("kie", (128, 512), BF16)
    wtg = sb1("wtg", (128, 512), BF16)
    sgm = sb1("sgm", (128, 512), BF16); sg2 = sb1("sg2", (128, 512), F32)
    rec = sb1("rec", (128, 512), F32)
    qfm = [sb1(f"qfm{p}", (128, 128), BF16, is_transpose_dest=True) for p in range(HP)]
    kfm = [sb1(f"kfm{p}", (128, 128), BF16, is_transpose_dest=True) for p in range(HP)]
    kefm = [sb1(f"kefm{p}", (128, 128), BF16, is_transpose_dest=True) for p in range(HP)]
    xgt = sb1("xgt", (128, 128), BF16)
    xgf = sb1("xgf", (128, 128), BF16, is_transpose_dest=True)
    gsb = sb1("gsb", (128, HP, 128), BF16)
    gstke = sb1("gstke", (128, NSC, 16), BF16)
    asb = [sb1(f"asb{p}", (128, 128), BF16) for p in range(HP)]
    s0 = sb1("s0", (128, HP, 130), BF16)
    vnstk = sb1("vnstk", (128, W, 132), BF16)
    vnnat = sb1("vnnat", (128, HP, 130), BF16)
    khh = sb1("khh", (128, 128), BF16); t1b = sb1("t1b", (128, 128), BF16)
    gwb = sb1("gwb", (128, 128), BF16); xts = sb1("xts", (128, 128), BF16)
    mus = sb1("mus", (128, 1), F32); nv = sb1("nv", (128, 1), F32)
    rsd = sb1("rsd", (128, 1), F32)
    s2R = sb1("s2R", (128, 1), F32); mm12 = sb1("mm12", (128, 2), F32)
    m12r = sb1("m12r", (128, 2), F32)
    murp = sb1("murp", (1, 128), F32); murn = sb1("murn", (1, 128), BF16)
    one1 = sb1("one1", (1, 1), F32)
    sx2 = sb1("sx2", (128, 1), F32); mux = sb1("mux", (128, 1), F32)
    nvx = sb1("nvx", (128, 1), F32); rstdx = sb1("rstdx", (128, 1), F32)
    ngrx = sb1("ngrx", (128, 1), F32)
    rk2 = sb1("rk2", (128, HP, 1), F32); rkl = sb1("rkl", (128, HP, 1), F32)
    rstdk = sb1("rstdk", (128, HP, 1), F32)
    ebv = sb1("ebv", (128, HP), BF16); dbv = sb1("dbv", (128, HP), F32)
    rbv = sb1("rbv", (128, HP), F32); negb = sb1("negb", (128, HP), F32)
    qbv = sb1("qbv", (128, 512), BF16)
    muo = sb1("muo", (128, HP), F32); so2 = sb1("so2", (128, HP), F32)
    nvo = sb1("nvo", (128, HP), F32); rso = sb1("rso", (128, HP), F32)
    lno = sb1("lno", (128, HP), F32)
    qhh = sb1("qhh", (128, 128), BF16); obuf = sb1("obuf", (128, 128), BF16)
    onat = sb1("onat", (128, 512), BF16)
    ogb = sb1("ogb", (128, 512), BF16); ogc = sb1("ogc", (128, 512), BF16)
    ro2 = sb1("ro2", (128, HP, 1), F32); rol = sb1("rol", (128, HP, 1), F32)
    rsog = sb1("rsog", (128, HP, 1), F32)

    def g_load(dst, src):
        P.op("gpsimd", lambda e, d=dst, s=src: e.dma_start(out=d, in_=s), inc="dg", by=16)
    g_load(wqs[:, :, :], ap["wq"].rearrange("(a p) n -> p a n", p=128))
    g_load(wks[:, :, :], ap["wk"].rearrange("(a p) n -> p a n", p=128))
    g_load(wvs[:, :, :], ap["wv"].rearrange("(a p) n -> p a n", p=128))
    g_load(wbs[:, :, :], ap["wbeta"].rearrange("(a p) n -> p a n", p=128))
    g_load(wg1s[:, :, :], ap["wg1"].rearrange("(a p) n -> p a n", p=128))
    g_load(wg2s[:, :], ap["wg2"][:, :])
    g_load(idb[:, :], c_ident[:, :])
    g_load(esl[:, :, :], c_esel[:, :, :])
    g_load(tri[:, :], c_triu[:, :])
    g_load(on1[:, :], c_onesb[:, :])
    g_load(o116[:, :], c_o116[:, :])
    g_load(o1r[:, :], c_o1r[:, :])
    g_load(one1[:, :], c_one1[:, :])
    g_load(zer[:, :], c_zer[:, :])
    g_load(cvals[:, :], c_cvals[:, :])
    g_load(slh[:, :, :, :], c_selh.rearrange("(h q p) n -> p h q n", h=2, q=HP))
    g_load(wos3[:, :, :], ap["wo"].rearrange("(a p) n -> p a n", p=128))
    dg_wo3_val = P.val("dg")
    g_load(lnwr[:, :, :], ap["lnw"][:, :, :])
    g_load(lnbr[:, :, :], ap["lnb"][:, :, :])
    g_load(wstks[:, :], ap["wstk"][:, :])
    g_load(wolnb[:, :, :], ap["woln"][:, :, :])
    g_load(bolnn[:, :], ap["boln"][:, :])
    g_load(wnatb[:, :], ap["wnat"][:, :])
    g_load(nrmb[:, :], ap["nrm"][:, :])
    n_wload = P.val("dg")
    P.op("sync", lambda e: e.dma_start(out=xtm[0][:, :], in_=ap["xb"][0:128, :]), inc="dy0", by=16)
    P.op("gpsimd", lambda e: e.dma_start(out=xfm[0][:, :, :],
         in_=ap["xbT"].rearrange("(a p) t -> p a t", p=128)[:, :, 0:128]), inc="dx0", by=16)

    P.wait("vector", "dg", n_wload)
    P.op("vector", lambda e: e.tensor_tensor(out=wbrow[:, :, :], in0=lnwr[:, :, :],
         in1=lnbr[:, :, :], op=ALU.mult), inc="v")
    P.op("vector", lambda e: e.memset(s0[:, :, :], 0.0), inc="v")
    P.op("vector", lambda e: e.memset(vnstk[:, :, :], 0.0), inc="v")
    P.op("vector", lambda e: e.memset(vnnat[:, :, :], 0.0), inc="v")
    for p_ in (pKH[:, :, :], pWBT[:, :, :], pGAQ[:, :, :], pS[:, :, :],
               pVnA[:, :, :], pVnB[:, :, :], pA[:, :], pB[:, :]):
        P.op("vector", lambda e, pp=p_: e.memset(pp, 0.0), inc="v")
    v_init = P.val("v")
    P.pe("vector", "v", "pA", "pB", "pKH", "pS", "pGAQ", "pWBT", "pVnA", "pVnB")
    P.wait("scalar", "dg", n_wload)
    P.op("scalar", lambda e: e.activation(out=w2sb[:, :], in_=wstks[:, :], func=AF.Square), inc="s")
    P.wait("tensor", "dg", n_wload)
    P.wait("tensor", "v", v_init)
    for wsb, width, off in [(wqs, 512, 0), (wks, 512, 512), (wvs, 512, 1024),
                            (wbs, HP, 1536), (wg1s, 128, 1540)]:
        P.wait("tensor", "v")
        P.pb("tensor", "pA")
        for kt in range(NK):
            P.op("tensor", lambda e, w_=wsb, k_=kt, n_=width: e.matmul(
                pA[0:1, 0:n_], on1[:, :], w_[:, k_, 0:n_], start=(k_ == 0),
                stop=(k_ == NK - 1)), inc="t" if kt == NK - 1 else None)
        P.pe("tensor", "t", "pA")
        P.wait("vector", "t")
        P.pb("vector", "pA")
        P.op("vector", lambda e, n_=width, o_=off: e.tensor_scalar(
            out=csw[0:1, o_:o_ + n_], in0=pA[0:1, 0:n_], scalar1=1.0, scalar2=None,
            op0=ALU.mult, op1=ALU.bypass), inc="v")
        P.pe("vector", "v", "pA")
    P.pb("tensor", "pS")
    P.op("tensor", lambda e: e.matmul(pS.ap().rearrange("p a b -> p (a b)"), zer[:, :],
         wqs[:, 0, :], start=True, stop=True, skip_group_check=True), inc="t")
    P.pe("tensor", "t", "pS")
    P.wait("tensor", "v")

    st = {"buf": 0, "xtm_dy_0": 16, "xfm_dx_0": 16,
          "pA_free_s": 0, "pB_free_v": 0, "pB_free_s": 0,
          "gaq_free_v": 0, "s0_ready_v": 0, "s0_ready_s": 0,
          "pS_free_v": 0, "pS_free_s": 0}

    def stats_phase(wi):
        buf = st["buf"]
        P.wait("tensor", "dg", n_wload)
        P.wait("tensor", f"dx{buf}", st[f"xfm_dx_{buf}"])
        P.wait("tensor", "s", st["pA_free_s"])
        P.pb("tensor", "pA")
        for kt in range(NK):
            P.op("tensor", lambda e, k_=kt, b_=buf: e.matmul(
                pA[0:1, 0:128], on1[:, :], xfm[b_][:, k_, :], start=(k_ == 0),
                stop=(k_ == NK - 1)), inc="t" if kt == NK - 1 else None)
        P.pe("tensor", "t", "pA")
        P.wait("vector", "t")
        P.pb("vector", "pA")
        P.op("vector", lambda e: e.tensor_scalar(out=murp[:, :], in0=pA[0:1, 0:128],
             scalar1=1.0 / D, scalar2=None, op0=ALU.mult, op1=ALU.bypass), inc="v")
        P.op("vector", lambda e: e.tensor_scalar(out=murn[:, :], in0=pA[0:1, 0:128],
             scalar1=-1.0 / D, scalar2=None, op0=ALU.mult, op1=ALU.bypass), inc="v")
        v_mur = P.val("v")
        P.pe("vector", "v", "pA")
        P.wait("scalar", f"dy{buf}", st[f"xtm_dy_{buf}"])
        P.op("scalar", lambda e, b_=buf: e.activation(out=xtm[1 - b_][:, :], in_=xtm[b_][:, :],
             func=AF.Square, scale=float(1.0 / np.sqrt(D)), accum_out=sx2[:, :]), inc="s")
        st["sx2_s"] = P.val("s")
        P.wait("tensor", "v", v_mur)
        P.pb("tensor", "pKH")
        P.op("tensor", lambda e: e.transpose(pKH[:, 0, 131:132], murp[:, :], one1[:, :]), inc="t")
        P.pe("tensor", "t", "pKH")
        P.wait("vector", "t")
        P.pb("vector", "pKH")
        P.op("vector", lambda e: e.tensor_scalar(out=mux[:, :], in0=pKH[:, 0, 131:132],
             scalar1=1.0, scalar2=None, op0=ALU.mult, op1=ALU.bypass), inc="v")
        P.pe("vector", "v", "pKH")
        P.wait("vector", "s", st["sx2_s"])
        P.op("vector", lambda e: e.scalar_tensor_tensor(out=nvx[:, :], in0=mux[:, :],
             scalar=mux[:, :], in1=sx2[:, :], op0=ALU.mult, op1=ALU.subtract), inc="v")
        P.wait("scalar", "v")
        P.op("scalar", lambda e: e.activation(out=lnv[:, :], in_=nvx[:, :], func=AF.Ln,
             bias=LNEPS, scale=-1.0), inc="s")
        P.op("scalar", lambda e: e.activation(out=rstdx[:, :], in_=lnv[:, :], func=AF.Exp,
             scale=-0.5), inc="s")
        P.wait("vector", "s")
        P.op("vector", lambda e: e.tensor_scalar(out=ngrx[:, :], in0=rstdx[:, :],
             scalar1=-1.0, scalar2=None, op0=ALU.mult, op1=ALU.bypass), inc="v")

    def load_tile(wi):
        b = wi % 2
        P.wait("sync", "s", st.get("sx2_s", 0))
        P.op("sync", lambda e, w_=wi, b_=b: e.dma_start(out=xtm[b_][:, :],
             in_=ap["xb"][TW*w_:TW*(w_+1), :]), inc=f"dy{b}", by=16)
        st[f"xtm_dy_{b}"] = P.val(f"dy{b}")
        P.wait("gpsimd", "t", st.get(f"xfm_free_{b}", 0))
        P.op("gpsimd", lambda e, w_=wi, b_=b: e.dma_start(out=xfm[b_][:, :, :],
             in_=ap["xbT"].rearrange("(a p) t -> p a t", p=128)[:, :, TW*w_:TW*(w_+1)]),
             inc=f"dx{b}", by=16)
        st[f"xfm_dx_{b}"] = P.val(f"dx{b}")

    def proj_phase(wi):
        buf = st["buf"]
        P.wait("tensor", "v")
        P.pb("tensor", "pA")
        for kt in range(NK):
            P.op("tensor", lambda e, k_=kt, b_=buf: e.matmul(
                pA[:, 0:512], xfm[b_][:, k_, :], wqs[:, k_, :], start=(k_ == 0), stop=False))
        P.op("tensor", lambda e: e.matmul(pA[:, 0:512], murn[:, :], csw[0:1, 0:512],
             start=False, stop=True), inc="t")
        t_q = P.val("t")
        P.pe("tensor", "t", "pA")
        P.wait("tensor", "v", st["pB_free_v"]); P.wait("tensor", "s", st["pB_free_s"])
        P.pb("tensor", "pB")
        for kt in range(NK):
            P.op("tensor", lambda e, k_=kt, b_=buf: e.matmul(
                pB[:, 0:512], xfm[b_][:, k_, :], wks[:, k_, :], start=(k_ == 0), stop=False))
        P.op("tensor", lambda e: e.matmul(pB[:, 0:512], murn[:, :], csw[0:1, 512:1024],
             start=False, stop=True), inc="t")
        t_k = P.val("t")
        P.pe("tensor", "t", "pB")
        P.wait("scalar", "t", t_q)
        P.pb("scalar", "pA")
        P.op("scalar", lambda e: e.activation(out=sgm[:, :], in_=pA[:, 0:512], func=AF.Exp,
             scale=ngrx[:, :]), inc="s")
        P.pe("scalar", "s", "pA")
        P.op("scalar", lambda e: e.activation(out=sg2[:, :], in_=sgm[:, :], func=AF.Ln,
             bias=1.0, scale=1.0), inc="s")
        P.op("scalar", lambda e: e.activation(out=rec[:, :], in_=sg2[:, :], func=AF.Exp,
             scale=-1.0), inc="s")
        P.wait("vector", "s")
        P.pb("vector", "pA")
        P.op("vector", lambda e: e.scalar_tensor_tensor(out=qtm[:, :], in0=pA[:, 0:512],
             scalar=rstdx[:, :], in1=rec[:, :], op0=ALU.mult, op1=ALU.mult), inc="v")
        v_q = P.val("v")
        P.pe("vector", "v", "pA")
        P.wait("tensor", "v", v_q)
        P.pb("tensor", "pA")
        for kt in range(NK):
            P.op("tensor", lambda e, k_=kt, b_=buf: e.matmul(
                pA[:, 0:512], xfm[b_][:, k_, :], wvs[:, k_, :], start=(k_ == 0), stop=False))
        P.op("tensor", lambda e: e.matmul(pA[:, 0:512], murn[:, :], csw[0:1, 1024:1536],
             start=False, stop=True), inc="t")
        t_v = P.val("t")
        P.pe("tensor", "t", "pA")
        P.wait("scalar", "t", t_k)
        P.pb("scalar", "pB")
        P.op("scalar", lambda e: e.activation(out=jk[:, :], in_=pB[:, 0:512], func=AF.Square,
             scale=rstdx[:, :]), inc="s")
        P.pe("scalar", "s", "pB")
        P.wait("vector", "s")
        P.op("vector", lambda e: e.reduce_sum(out=rk2[:, :, :],
             in_=jk.ap().rearrange("p (h e) -> p h e", h=HP), axis=AX.X), inc="v")
        P.wait("scalar", "v")
        P.op("scalar", lambda e: e.activation(out=rkl[:, :, :], in_=rk2[:, :, :], func=AF.Ln,
             bias=EPS), inc="s")
        P.op("scalar", lambda e: e.activation(out=rstdk[:, :, :], in_=rkl[:, :, :],
             func=AF.Exp, scale=-0.5), inc="s")
        P.wait("vector", "s")
        P.pb("vector", "pB")
        for h in range(HP):
            P.op("vector", lambda e, h_=h: e.tensor_scalar(out=ktm[:, 128*h_:128*(h_+1)],
                 in0=pB[:, 128*h_:128*(h_+1)], scalar1=rstdx[:, :], scalar2=rstdk[:, h_, :],
                 op0=ALU.mult, op1=ALU.mult), inc="v")
        v_k = P.val("v")
        P.pe("vector", "v", "pB")
        P.wait("tensor", "v", v_k)
        P.pb("tensor", "pB")
        for kt in range(NK):
            P.op("tensor", lambda e, k_=kt, b_=buf: e.matmul(
                pB[:, 0:HP], xfm[b_][:, k_, :], wbs[:, k_, :], start=(k_ == 0), stop=False))
        P.op("tensor", lambda e: e.matmul(pB[:, 0:HP], murn[:, :], csw[0:1, 1536:1536+HP],
             start=False, stop=True), inc="t")
        for kt in range(NK):
            P.op("tensor", lambda e, k_=kt, b_=buf: e.matmul(
                pB[:, HP:HP+128], xfm[b_][:, k_, :], wg1s[:, k_, :], start=(k_ == 0), stop=False))
        P.op("tensor", lambda e: e.matmul(pB[:, HP:HP+128], murn[:, :], csw[0:1, 1540:1668],
             start=False, stop=True), inc="t")
        t_misc = P.val("t")
        P.pe("tensor", "t", "pB")
        st[f"xfm_free_{buf}"] = t_misc
        P.wait("vector", "t", t_v)
        P.pb("vector", "pA")
        P.op("vector", lambda e: e.tensor_scalar(out=vtm[:, :], in0=pA[:, 0:512],
             scalar1=rstdx[:, :], scalar2=None, op0=ALU.mult, op1=ALU.bypass), inc="v")
        P.pe("vector", "v", "pA")
        P.op("vector", lambda e: e.tensor_tensor(out=wtg[:, :], in0=vtm[:, :], in1=ktm[:, :],
             op=ALU.subtract), inc="v")
        P.op("vector", lambda e: e.tensor_tensor(out=wtg[:, :], in0=wtg[:, :],
             in1=wnatb[:, :], op=ALU.mult), inc="v")
        st["v_wtg"] = P.val("v")
        P.wait("scalar", "t", t_misc)
        P.pb("scalar", "pB")
        P.op("scalar", lambda e: e.activation(out=ebv[:, :], in_=pB[:, 0:HP], func=AF.Exp,
             scale=ngrx[:, :]), inc="s")
        st["pB_free_s"] = P.val("s")
        P.pe("scalar", "s", "pB")
        P.wait("vector", "s")
        P.op("vector", lambda e: e.tensor_scalar(out=dbv[:, :], in0=ebv[:, :], scalar1=1.0,
             scalar2=None, op0=ALU.add, op1=ALU.bypass), inc="v")
        P.op("vector", lambda e: e.reciprocal(out=rbv[:, :], in_=dbv[:, :]), inc="v")
        P.op("vector", lambda e: e.tensor_scalar(out=negb[:, :], in0=rbv[:, :],
             scalar1=-2.0, scalar2=None, op0=ALU.mult, op1=ALU.bypass), inc="v")
        for h in range(HP):
            P.op("vector", lambda e, h_=h: e.tensor_scalar(out=kie[:, 128*h_:128*(h_+1)],
                 in0=ktm[:, 128*h_:128*(h_+1)], scalar1=negb[:, h_:h_+1], scalar2=None,
                 op0=ALU.mult, op1=ALU.bypass), inc="v")
        P.pb("vector", "pB")
        P.op("vector", lambda e: e.tensor_scalar(out=xgt[:, :], in0=pB[:, HP:HP+128],
             scalar1=rstdx[:, :], scalar2=None, op0=ALU.mult, op1=ALU.bypass), inc="v")
        st["pB_free_v"] = P.val("v")
        P.pe("vector", "v", "pB")
        P.op("vector", lambda e: e.tensor_tensor(out=qbv[:, :], in0=qtm[:, :],
             in1=bolnn[:, :], op=ALU.add), inc="v")
        v_all = P.val("v")
        P.wait("sync", "v", v_q)
        for p in range(HP):
            P.op("sync", lambda e, p_=p: e.dma_start(out=qfm[p_][:, :],
                 in_=qtm[:, 128*p_:128*(p_+1)], transpose=True), inc="dz", by=16)
        P.wait("sync", "v", v_k)
        for p in range(HP):
            P.op("sync", lambda e, p_=p: e.dma_start(out=kfm[p_][:, :],
                 in_=ktm[:, 128*p_:128*(p_+1)], transpose=True), inc="dz", by=16)
        P.wait("scalar", "v", v_all)
        for p in range(HP):
            P.op("scalar", lambda e, p_=p: e.dma_start(out=kefm[p_][:, :],
                 in_=kie[:, 128*p_:128*(p_+1)], transpose=True), inc="dz", by=16)
        P.op("scalar", lambda e: e.dma_start(out=xgf[:, :], in_=xgt[:, :], transpose=True),
             inc="dz", by=16)

    def scan_phase(wi, debug):
        if stages < 1:
            return
        dz_fm = P.val("dz")
        P.wait("tensor", "v", st.get("pvn_evac_v", 0))
        P.wait("tensor", "s", st.get("pvn_evac_s", 0))
        P.pb("tensor", "pVnA", "pVnB")
        P.op("tensor", lambda e: e.matmul(pVnA.ap().rearrange("p a b -> p (a b)"),
             zer[:, :], wqs[:, 0, :], start=True, stop=True, skip_group_check=True))
        P.op("tensor", lambda e: e.matmul(pVnB.ap().rearrange("p a b -> p (a b)"),
             zer[:, :], wqs[:, 0, :], start=True, stop=True, skip_group_check=True), inc="t")
        P.pe("tensor", "t", "pVnA", "pVnB")
        P.wait("tensor", "dz", dz_fm)
        P.wait("tensor", "v", st["gaq_free_v"])
        P.pb("tensor", "pGAQ")
        for p in range(HP):
            P.op("tensor", lambda e, p_=p: e.matmul(pGAQ[:, p_, :], kefm[p_][:, :],
                 kfm[p_][:, :], start=True, stop=True), inc="t" if p == HP-1 else None)
        P.pe("tensor", "t", "pGAQ")
        P.wait("vector", "t")
        P.pb("vector", "pGAQ")
        P.op("vector", lambda e: e.tensor_scalar(out=gsb[:, :, :], in0=pGAQ[:, :, :],
             scalar1=1.0, scalar2=None, op0=ALU.mult, op1=ALU.bypass), inc="v")
        P.pe("vector", "v", "pGAQ")
        P.wait("tensor", "v")
        P.pb("tensor", "pGAQ")
        for p in range(HP):
            P.op("tensor", lambda e, p_=p: e.matmul(pGAQ[:, p_, :], kefm[p_][:, :],
                 qfm[p_][:, :], start=True, stop=True), inc="t" if p == HP-1 else None)
        P.pe("tensor", "t", "pGAQ")
        P.wait("vector", "t")
        P.pb("vector", "pGAQ")
        for p in range(HP):
            P.op("vector", lambda e, p_=p: e.tensor_tensor(out=asb[p_][:, :],
                 in0=pGAQ[:, p_, :], in1=tri[:, :], op=ALU.mult), inc="v")
        v_asb = P.val("v")
        P.pe("vector", "v", "pGAQ")
        P.wait("tensor", "v", v_asb)
        P.pb("tensor", "pGAQ")
        for k in range(NSC):
            for p in range(HP):
                P.op("tensor", lambda e, k_=k, p_=p: e.matmul(
                    pGAQ[32*p_:32*p_+16, k_, 0:16], kefm[p_][:, 32*k_:32*k_+16],
                    kfm[p_][:, 16*(2*k_+1):16*(2*k_+2)], start=True, stop=True,
                    tile_position=(0, 32*p_)),
                    inc="t" if (k == NSC-1 and p == HP-1) else None)
        P.pe("tensor", "t", "pGAQ")
        P.wait("vector", "t")
        P.pb("vector", "pGAQ")
        P.op("vector", lambda e: e.tensor_scalar(out=gstke[:, :, :], in0=pGAQ[:, 0:NSC, 0:16],
             scalar1=1.0, scalar2=None, op0=ALU.mult, op1=ALU.bypass), inc="v")
        st["gaq_free_v"] = P.val("v")
        P.pe("vector", "v", "pGAQ")
        wbt_ready = {}
        def wbt_mm(c):
            P.pb("tensor", "pWBT")
            for p in range(HP):
                P.op("tensor", lambda e, c_=c, p_=p: e.matmul(
                    pWBT[32*p_:32*p_+16, c_ % 2, 0:128], esl[:, c_, :],
                    wtg[:, 128*p_:128*(p_+1)], start=True, stop=False,
                    tile_position=(0, 32*p_)))
                P.op("tensor", lambda e, c_=c, p_=p: e.matmul(
                    pWBT[32*p_:32*p_+16, c_ % 2, 0:128], o116[:, :],
                    wbrow[0:1, p_, :], start=False, stop=True,
                    tile_position=(0, 32*p_)),
                    inc="t" if p == HP-1 else None)
            wbt_ready[c] = P.val("t")
            P.pe("tensor", "t", "pWBT")
        P.wait("tensor", "v", st["v_wtg"])
        wbt_mm(0); wbt_mm(1)

        if stages < 2:
            return
        chain_v = {}; evac_s = {}
        for c in range(W):
            k = c // 2
            if c >= 1:
                P.wait("tensor", "v", chain_v[c - 1])
                if c % 2 == 0 and k >= 1:
                    P.wait("tensor", "s", evac_s.get(k - 1, 0))
            else:
                P.wait("tensor", "v", st["s0_ready_v"])
                P.wait("tensor", "s", st["s0_ready_s"])
            P.pb("tensor", "pKH")
            for p in range(HP):
                last = (p == HP - 1)
                Kc = 32 * k
                if Kc > 0:
                    P.op("tensor", lambda e, c_=c, p_=p, K_=Kc: e.matmul(
                        pKH[32*p_:32*p_+16, c_ % 2, 0:129], gsb[0:K_, p_, 16*c_:16*(c_+1)],
                        vnnat[0:K_, p_, 0:129], start=True, stop=False,
                        tile_position=(0, 32*p_)))
                P.op("tensor", lambda e, c_=c, p_=p, st_=(Kc == 0): e.matmul(
                    pKH[32*p_:32*p_+16, c_ % 2, 0:129], kfm[p_][:, 16*c_:16*(c_+1)],
                    s0[:, p_, 0:129], start=st_, stop=(c_ % 2 == 0),
                    tile_position=(0, 32*p_)),
                    inc="t" if (last and c % 2 == 0) else None)
                if c % 2 == 1:
                    P.op("tensor", lambda e, c_=c, p_=p, k_=k: e.matmul(
                        pKH[32*p_:32*p_+16, c_ % 2, 0:129], gstke[32*p_:32*p_+16, k_, :],
                        vnstk[32*p_:32*p_+16, 2*k_, 0:129], start=False, stop=True,
                        tile_position=(32*p_, 32*p_)),
                        inc="t" if last else None)
            t_kh = P.val("t")
            P.pe("tensor", "t", "pKH")
            P.wait("vector", "t", t_kh)
            P.pb("vector", "pKH")
            P.op("vector", lambda e, c_=c: e.tensor_scalar(out=mus[:, :],
                 in0=pKH[:, c_ % 2, 128:129], scalar1=1.0/128.0, scalar2=None,
                 op0=ALU.mult, op1=ALU.bypass), inc="v")
            P.pe("vector", "v", "pKH")
            P.wait("scalar", "t", t_kh)
            P.pb("scalar", "pKH")
            P.op("scalar", lambda e, c_=c: e.activation(out=jkb[:, :], in_=pKH[:, c_ % 2, 0:128],
                 func=AF.Square, scale=IRT128, accum_out=s2R[:, :]), inc="s")
            s_sq = P.val("s")
            P.pe("scalar", "s", "pKH")
            P.wait("vector", "s", s_sq)
            P.op("vector", lambda e: e.scalar_tensor_tensor(out=nv[:, :], in0=mus[:, :],
                 scalar=mus[:, :], in1=s2R[:, :], op0=ALU.mult, op1=ALU.subtract), inc="v")
            P.wait("scalar", "v")
            P.op("scalar", lambda e: e.activation(out=lnv[:, :], in_=nv[:, :], func=AF.Ln,
                 bias=EPS, scale=-1.0), inc="s")
            P.op("scalar", lambda e: e.activation(out=rsd[:, :], in_=lnv[:, :], func=AF.Exp,
                 scale=-0.5), inc="s")
            P.wait("vector", "s")
            P.pb("vector", "pKH")
            P.op("vector", lambda e, c_=c: e.tensor_scalar(out=khh[:, :],
                 in0=pKH[:, c_ % 2, 0:128], scalar1=mus[:, :], scalar2=rsd[:, :],
                 op0=ALU.subtract, op1=ALU.mult), inc="v")
            P.pe("vector", "v", "pKH")
            P.op("vector", lambda e: e.tensor_tensor(out=t1b[:, :], in0=khh[:, :],
                 in1=w2sb[:, :], op=ALU.mult), inc="v")
            P.pb("vector", "pWBT")
            P.op("vector", lambda e, c_=c: e.scalar_tensor_tensor(out=gwb[:, :],
                 in0=t1b[:, :], scalar=0.0, in1=pWBT[:, c_ % 2, 0:128], op0=ALU.add,
                 op1=ALU.add, accum_out=mm12[:, 0:1]), inc="v")
            P.pe("vector", "v", "pWBT")
            P.op("vector", lambda e: e.scalar_tensor_tensor(out=jkb[:, :], in0=gwb[:, :],
                 scalar=1.0, in1=khh[:, :], op0=ALU.mult, op1=ALU.mult,
                 accum_out=mm12[:, 1:2]), inc="v")
            P.op("vector", lambda e: e.tensor_scalar(out=m12r[:, :], in0=mm12[:, :],
                 scalar1=rsd[:, :], scalar2=1.0/128.0, op0=ALU.mult, op1=ALU.mult), inc="v")
            P.op("vector", lambda e: e.tensor_scalar(out=xts[:, :], in0=khh[:, :],
                 scalar1=m12r[:, 1:2], scalar2=m12r[:, 0:1], op0=ALU.mult, op1=ALU.add), inc="v")
            P.op("vector", lambda e, c_=c: e.scalar_tensor_tensor(out=vnstk[:, c_, 0:128],
                 in0=gwb[:, :], scalar=rsd[:, :], in1=xts[:, :], op0=ALU.mult,
                 op1=ALU.subtract, accum_out=vnstk[:, c_, 128:129]), inc="v", lp=True)
            chain_v[c] = P.val("v")
            if c + 2 < W:
                P.wait("tensor", "v", chain_v[c])
                wbt_mm(c + 2)
            P.wait("tensor", "v", chain_v[c])
            P.pb("tensor", "pVnA", "pVnB")
            for p in range(HP):
                P.op("tensor", lambda e, c_=c, p_=p, k_=k: e.matmul(
                    pVn[p_ // 2][32*k_:32*k_+32, p_ % 2, 0:129],
                    slh[:, c_ % 2, p_, :], vnstk[:, c_, 0:129],
                    start=False, stop=(c_ % 2 == 1),
                    tile_position=(0, 32*k_), skip_group_check=True),
                    inc="t" if p == HP-1 else None)
            t_rs = P.val("t")
            P.pe("tensor", "t", "pVnA", "pVnB")
            if c % 2 == 1:
                P.wait("vector", "t", t_rs)
                P.pb("vector", "pVnA", "pVnB")
                P.op("vector", lambda e, k_=k: e.tensor_scalar(
                     out=vnnat[32*k_:32*k_+32, 0:2, 0:129], in0=pVnA[32*k_:32*k_+32, :, 0:129],
                     scalar1=1.0, scalar2=None, op0=ALU.mult, op1=ALU.bypass), inc="v")
                P.op("vector", lambda e, k_=k: e.tensor_scalar(
                     out=vnnat[32*k_:32*k_+32, 2:4, 0:129], in0=pVnB[32*k_:32*k_+32, :, 0:129],
                     scalar1=1.0, scalar2=None, op0=ALU.mult, op1=ALU.bypass), inc="v")
                chain_v[c] = P.val("v")
                st["pvn_evac_v"] = P.val("v")
                P.pe("vector", "v", "pVnA", "pVnB")
        v_chain = P.val("v"); s_chain = P.val("s")

        P.wait("tensor", "v", v_chain)
        P.wait("tensor", "s", s_chain)
        P.pb("tensor", "pGAQ", "pWBT")
        for p in range(HP):
            P.op("tensor", lambda e, p_=p: e.matmul(pGAQ[:, p_, 0:128], qfm[p_][:, :],
                 s0[:, p_, 0:128], start=True, stop=False))
            P.op("tensor", lambda e, p_=p: e.matmul(pWBT[:, 0, 128+p_:129+p_], qfm[p_][:, :],
                 s0[:, p_, 128:129], start=True, stop=False))
            P.op("tensor", lambda e, p_=p: e.matmul(pGAQ[:, p_, 0:128], asb[p_][:, :],
                 vnnat[:, p_, 0:128], start=False, stop=True))
            P.op("tensor", lambda e, p_=p: e.matmul(pWBT[:, 0, 128+p_:129+p_], asb[p_][:, :],
                 vnnat[:, p_, 128:129], start=False, stop=True),
                 inc="t" if p == HP-1 else None)
        t_qh = P.val("t")
        P.pe("tensor", "t", "pGAQ", "pWBT")
        P.pb("tensor", "pS")
        for p in range(HP):
            P.op("tensor", lambda e, p_=p, w_=wi: e.matmul(pS[:, p_, :],
                 kie[:, 128*p_:128*(p_+1)], vnnat[:, p_, 0:128],
                 start=False, stop=(w_ == NW - 1),
                 skip_group_check=True), inc="t" if p == HP-1 else None)
        t_ds = P.val("t")
        P.pe("tensor", "t", "pS")
        P.wait("vector", "t", t_ds)
        P.pb("vector", "pS")
        for p in range(HP):
            P.op("vector", lambda e, p_=p: e.tensor_scalar(out=s0[:, p_, 0:128],
                 in0=pS[:, p_, :], scalar1=1.0, scalar2=None, op0=ALU.mult,
                 op1=ALU.add, accum_out=s0[:, p_, 128:129]), inc="v", lp=True)
        st["pS_free_v"] = P.val("v"); st["s0_ready_v"] = P.val("v")
        P.pe("vector", "v", "pS")
        st["pS_free_s"] = 0; st["s0_ready_s"] = 0

        if stages < 4:
            return
        P.wait("vector", "t", t_qh)
        P.pb("vector", "pWBT")
        P.op("vector", lambda e: e.tensor_scalar(out=muo[:, :], in0=pWBT[:, 0, 128:132],
             scalar1=1.0/128.0, scalar2=None, op0=ALU.mult, op1=ALU.bypass), inc="v")
        P.pe("vector", "v", "pWBT")
        P.wait("scalar", "t", t_qh)
        P.pb("scalar", "pGAQ")
        for p in range(HP):
            P.op("scalar", lambda e, p_=p: e.activation(out=jkb[:, :], in_=pGAQ[:, p_, 0:128],
                 func=AF.Square, scale=IRT128, accum_out=so2[:, p_:p_+1]), inc="s")
        P.pe("scalar", "s", "pGAQ")
        P.wait("vector", "s")
        for p in range(HP):
            P.op("vector", lambda e, p_=p: e.scalar_tensor_tensor(out=nvo[:, p_:p_+1],
                 in0=muo[:, p_:p_+1], scalar=muo[:, p_:p_+1], in1=so2[:, p_:p_+1],
                 op0=ALU.mult, op1=ALU.subtract), inc="v")
        P.wait("scalar", "v")
        P.op("scalar", lambda e: e.activation(out=lno[:, :], in_=nvo[:, :], func=AF.Ln,
             bias=EPS, scale=-1.0), inc="s")
        P.op("scalar", lambda e: e.activation(out=rso[:, :], in_=lno[:, :], func=AF.Exp,
             scale=-0.5), inc="s")
        P.wait("vector", "s")
        P.pb("vector", "pGAQ")
        for p in range(HP):
            P.op("vector", lambda e, p_=p: e.tensor_scalar(out=qhh[:, :],
                 in0=pGAQ[:, p_, 0:128], scalar1=muo[:, p_:p_+1], scalar2=rso[:, p_:p_+1],
                 op0=ALU.subtract, op1=ALU.mult), inc="v")
            P.op("vector", lambda e, p_=p: e.tensor_tensor(out=obuf[:, :], in0=qhh[:, :],
                 in1=wolnb[:, p_, :], op=ALU.mult), inc="v")
            P.op("vector", lambda e, p_=p: e.tensor_tensor(out=onat[:, 128*p_:128*(p_+1)],
                 in0=obuf[:, :], in1=qbv[:, 128*p_:128*(p_+1)], op=ALU.add), inc="v")
        st["gaq_free_v"] = P.val("v")
        v_onat = P.val("v")
        P.pe("vector", "v", "pGAQ")

        P.wait("tensor", "dz", dz_fm)
        P.wait("tensor", "s", st["pA_free_s"])
        P.wait("tensor", "v", v_onat)
        P.pb("tensor", "pA")
        P.op("tensor", lambda e: e.matmul(pA[:, 0:512], xgf[:, :], wg2s[:, :],
             start=True, stop=True), inc="t")
        t_gate = P.val("t")
        P.pe("tensor", "t", "pA")
        P.wait("scalar", "t", t_gate)
        P.pb("scalar", "pA")
        P.op("scalar", lambda e: e.activation(out=sgm[:, :], in_=pA[:, 0:512], func=AF.Exp,
             scale=-1.0), inc="s")
        st["pA_free_s"] = P.val("s")
        P.pe("scalar", "s", "pA")
        P.op("scalar", lambda e: e.activation(out=sg2[:, :], in_=sgm[:, :], func=AF.Ln,
             bias=1.0, scale=1.0), inc="s")
        P.op("scalar", lambda e: e.activation(out=rec[:, :], in_=sg2[:, :], func=AF.Exp,
             scale=-1.0), inc="s")
        P.wait("vector", "s")
        P.op("vector", lambda e: e.tensor_tensor(out=ogb[:, :], in0=onat[:, :], in1=rec[:, :],
             op=ALU.mult), inc="v")
        P.wait("scalar", "v")
        P.op("scalar", lambda e: e.activation(out=jk[:, :], in_=ogb[:, :], func=AF.Square,
             scale=IRT128), inc="s")
        P.wait("vector", "s")
        P.op("vector", lambda e: e.reduce_sum(out=ro2[:, :, :],
             in_=jk.ap().rearrange("p (h e) -> p h e", h=HP), axis=AX.X), inc="v")
        P.wait("scalar", "v")
        P.op("scalar", lambda e: e.activation(out=rol[:, :, :], in_=ro2[:, :, :], func=AF.Ln,
             bias=EPS), inc="s")
        P.op("scalar", lambda e: e.activation(out=rsog[:, :, :], in_=rol[:, :, :],
             func=AF.Exp, scale=-0.5), inc="s")
        P.wait("vector", "s")
        for h in range(HP):
            P.op("vector", lambda e, h_=h: e.tensor_scalar(out=ogc[:, 128*h_:128*(h_+1)],
                 in0=ogb[:, 128*h_:128*(h_+1)], scalar1=rsog[:, h_, :], scalar2=None,
                 op0=ALU.mult, op1=ALU.bypass), inc="v")
        P.op("vector", lambda e: e.tensor_tensor(out=ogc[:, :], in0=ogc[:, :], in1=nrmb[:, :],
             op=ALU.mult), inc="v")
        v_ogc = P.val("v")
        P.wait("sync", "v", v_ogc)
        P.wait("sync", "dz")
        P.wait("sync", "t", st.get("at_t", 0))
        if debug:
            P.wait("sync", "dg")
        for h in range(HP):
            P.op("sync", lambda e, h_=h: e.dma_start(out=ogfm[h_][:, :],
                 in_=ogc[:, 128*h_:128*(h_+1)], transpose=True), inc="dz", by=16)
        dz_og = P.val("dz")
        if debug:
            P.wait("scalar", "dz", dz_og)
            for h in range(HP):
                P.op("scalar", lambda e, h_=h, w_=wi: e.dma_start(
                     out=og_loc[w_, 128*h_:128*(h_+1), :], in_=ogfm[h_][:, :]), inc="dz", by=16)
            P.wait("gpsimd", "dz", dz_og)
            for h in range(HP):
                P.op("gpsimd", lambda e, h_=h, w_=wi: e.dma_start(
                     out=ap["out"][w_, 128*h_:128*(h_+1), :], in_=ogfm[h_][:, :]),
                     inc="dg", by=16)
            return
        bb = wi % 2
        P.wait("tensor", "dz", dz_og)
        if wi == 0:
            P.wait("tensor", "dg", dg_wo3_val)
        P.wait("tensor", "v", P.val("v"))
        P.wait("tensor", "s", P.val("s"))
        P.pb("tensor", "pA", "pB", "pKH", "pWBT")
        ab = [pA[:, 0:512], pB[:, 0:512],
              pKH.ap().rearrange("p a b -> p (a b)"),
              pWBT.ap().rearrange("p a b -> p (a b)")]
        for q in range(4):
            for h in range(HP):
                P.op("tensor", lambda e, q_=q, h_=h: e.matmul(
                    ab[q_], ogfm[h_][:, :], wos3[:, h_, 512*q_:512*(q_+1)],
                    start=(h_ == 0), stop=(h_ == HP-1)),
                    inc="t" if (q == 3 and h == HP-1) else None)
        st["at_t"] = P.val("t")
        P.pe("tensor", "t", "pA", "pB", "pKH", "pWBT")
        P.wait("vector", "t", P.val("t"))
        P.wait("vector", f"so{bb}", st.get(f"at_so_{bb}", 0))
        P.pb("vector", "pA", "pB", "pKH", "pWBT")
        for q in range(4):
            P.op("vector", lambda e, q_=q, b_=bb, a_=ab[q]: e.tensor_scalar(
                 out=atb3[b_][:, 512*q_:512*(q_+1)], in0=a_, scalar1=1.0,
                 scalar2=None, op0=ALU.mult, op1=ALU.bypass), inc="v", lp=True)
        P.pe("vector", "v", "pA", "pB", "pKH", "pWBT")
        P.wait("scalar", "v", P.val("v"))
        P.op("scalar", lambda e, w_=wi, b_=bb: e.dma_start(
             out=attn_part[TW*w_:TW*(w_+1), :], in_=atb3[b_][:, :]), inc=f"so{bb}", by=16)
        st[f"at_so_{bb}"] = P.val(f"so{bb}")

    for wi in range(NW):
        stats_phase(wi)
        if wi + 1 < NW:
            load_tile(wi + 1)
        proj_phase(wi)
        scan_phase(wi, debug_og)
        st["buf"] = 1 - st["buf"]

    if debug_og:
        for s_ in ("dg", "dz", "dy0", "dy1", "dx0", "dx1", "so0", "so1"):
            P.wait("gpsimd", s_)
        finalize(nc, P)
        return nc, {"NW": NW}

    s12.close()
    NT3 = N3 // TW
    sb3 = lambda name, shape, dt, **kw: outer.enter_context(nc.sbuf_tensor(name, list(shape), dt, **kw))
    def g3_load(dst, src):
        P.op("gpsimd", lambda e, d=dst, s=src: e.dma_start(out=d, in_=s), inc="dg", by=16)
    attnb = [pA[:, 0:512], pB[:, 0:512],
             pGAQ.ap().rearrange("p a b -> p (a b)"),
             pS.ap().rearrange("p a b -> p (a b)")]
    st3 = {}

    P.wait("gpsimd", "so0"); P.wait("gpsimd", "so1")
    P.op("gpsimd", lambda e: e.collective_compute(
        "ReduceScatter", ALU.add,
        replica_groups=[[0, 1, 2, 3], [4, 5, 6, 7]],
        ins=[attn_part[:, :]], outs=[attn_q[:, :]]), inc="cc")

    at3 = [sb3(f"at3{i}", (128, D), BF16) for i in range(2)]
    x3 = [sb3(f"x3{i}", (128, D), F32) for i in range(2)]
    xmid = sb3("xmid", (128, D), F32)
    xf3 = sb3("xf3", (128, D), BF16)
    xff = sb3("xff", (128, NK, 128), BF16, is_transpose_dest=True)
    hmsk = sb3("hmsk", (128, 512), BF16)
    hfm = [sb3(f"hfm{j}", (128, 128), BF16, is_transpose_dest=True) for j in range(HP)]
    out3 = [sb3(f"out3{i}", (128, D), F32) for i in range(2)]
    sx2r = sb3("sx2r", (128, 1), F32)
    mur3 = sb3("mur3", (1, 128), BF16)
    mux3 = sb3("mux3", (128, 1), F32)
    nv3 = sb3("nv3", (128, 1), F32); rstd3 = sb3("rstd3", (128, 1), F32)
    mx8 = sb3("mx8", (128, 1), F32); nmx8 = sb3("nmx8", (128, 1), F32)
    z8 = sb3("z8", (128, 1), F32); wg8 = sb3("wg8", (128, 1), F32)
    ex8 = sb3("ex8", (128, 8), F32)
    msk8 = sb3("msk8", (128, 8), F32)

    g3_load(aws[:, :, :], ap["awr"].rearrange("(a p) n -> p a n", p=128))
    g3_load(bss[:, :, :], ap["bstk"][:, :, :])
    g3_load(brs[:, :], ap["brow"][:, :])
    g3_load(id3[:, :], c_ident[:, :])
    dg_w3 = P.val("dg")

    st3["free_v"] = P.val("v"); st3["free_s"] = P.val("s")
    def load3(j):
        b = j % 2
        P.wait("sync", "cc")
        P.wait("sync", "v", st3.get(f"v_xmid_{b}", 0))
        P.op("sync", lambda e, j_=j, b_=b: e.dma_start(out=at3[b_][:, :],
             in_=attn_q[TW*j_:TW*(j_+1), :]), inc=f"dy{b}", by=16)
        P.op("sync", lambda e, j_=j, b_=b: e.dma_start(out=x3[b_][:, :],
             in_=ap["x3s"][TW*j_:TW*(j_+1), :]), inc=f"dy{b}", by=16)
        st3[f"ld_dy_{b}"] = P.val(f"dy{b}")
    load3(0)

    for j in range(NT3):
        b = j % 2
        if j + 1 < NT3:
            load3(j + 1)
        P.wait("vector", f"dy{b}", st3[f"ld_dy_{b}"])
        P.wait("vector", "s", st3.get("xmid_s", 0))
        P.wait("vector", "v", st3.get("out_done_v", 0))
        for q in range(4):
            P.op("vector", lambda e, q_=q, b_=b: e.tensor_tensor(
                 out=xmid[:, 512*q_:512*(q_+1)], in0=at3[b_][:, 512*q_:512*(q_+1)],
                 in1=x3[b_][:, 512*q_:512*(q_+1)], op=ALU.add), inc="v")
        v_xmid = P.val("v")
        P.op("vector", lambda e: e.reduce_sum(out=mux3[:, :], in_=xmid[:, :], axis=AX.X), inc="v")
        P.op("vector", lambda e: e.tensor_scalar(out=mux3[:, :], in0=mux3[:, :],
             scalar1=1.0/D, scalar2=None, op0=ALU.mult, op1=ALU.bypass), inc="v")
        P.wait("scalar", "v", v_xmid)
        P.op("scalar", lambda e: e.activation(out=xf3[:, :], in_=xmid[:, :], func=AF.Square,
             scale=float(1.0/np.sqrt(D)), accum_out=sx2r[:, :]), inc="s")
        st3["xmid_s"] = P.val("s")
        P.wait("vector", "s")
        P.op("vector", lambda e: e.scalar_tensor_tensor(out=nv3[:, :], in0=mux3[:, :],
             scalar=mux3[:, :], in1=sx2r[:, :], op0=ALU.mult, op1=ALU.subtract), inc="v")
        P.wait("scalar", "v")
        P.op("scalar", lambda e: e.activation(out=lnv[:, :], in_=nv3[:, :], func=AF.Ln,
             bias=LNEPS, scale=-1.0), inc="s")
        P.op("scalar", lambda e: e.activation(out=rstd3[:, :], in_=lnv[:, :], func=AF.Exp,
             scale=-0.5), inc="s")
        P.wait("vector", "s")
        P.op("vector", lambda e: e.tensor_scalar(out=xf3[:, :], in0=xmid[:, :],
             scalar1=mux3[:, :], scalar2=rstd3[:, :], op0=ALU.subtract, op1=ALU.mult), inc="v")
        v_xf = P.val("v")
        P.wait("sync", "v", v_xf)
        P.wait("scalar", "v", v_xf)
        for kt in range(NK):
            eng = "sync" if kt % 2 == 0 else "scalar"
            P.op(eng, lambda e, k_=kt: e.dma_start(out=xff[:, k_, :],
                 in_=xf3[:, 128*k_:128*(k_+1)], transpose=True), inc="dz", by=16)
        dz_xf = P.val("dz")
        P.wait("tensor", "v", P.val("v"))
        P.wait("tensor", "s", st3["free_s"])
        P.wait("tensor", "dz", dz_xf)
        P.pb("tensor", "pA", "pB")
        for kt in range(NK):
            P.op("tensor", lambda e, k_=kt: e.matmul(pA[:, 0:512], xff[:, k_, :],
                 aws[:, k_, 0:512], start=(k_ == 0), stop=(k_ == NK-1)))
            P.op("tensor", lambda e, k_=kt: e.matmul(pB[:, 0:8], xff[:, k_, :],
                 aws[:, k_, 512:520], start=(k_ == 0), stop=False))
        P.op("tensor", lambda e: e.matmul(pB[:, 0:8], o1r[:, :], brs[:, :],
             start=False, stop=True), inc="t")
        t_h = P.val("t")
        P.pe("tensor", "t", "pA", "pB")
        P.wait("vector", "t", t_h)
        P.pb("vector", "pB")
        P.op("vector", lambda e: e.reduce_max(out=mx8[:, :], in_=pB[:, 0:8], axis=AX.X), inc="v")
        P.op("vector", lambda e: e.tensor_scalar(out=nmx8[:, :], in0=mx8[:, :],
             scalar1=-1.0, scalar2=None, op0=ALU.mult, op1=ALU.bypass), inc="v")
        P.pe("vector", "v", "pB")
        P.wait("scalar", "v")
        P.pb("scalar", "pB")
        P.op("scalar", lambda e: e.activation(out=ex8[:, :], in_=pB[:, 0:8], func=AF.Exp,
             bias=nmx8[:, :], accum_out=z8[:, :]), inc="s")
        P.pe("scalar", "s", "pB")
        P.wait("vector", "s")
        P.op("vector", lambda e: e.reciprocal(out=wg8[:, :], in_=z8[:, :]), inc="v")
        P.op("vector", lambda e: e.tensor_scalar(out=wg8[:, :], in0=wg8[:, :],
             scalar1=1.0/64.0, scalar2=None, op0=ALU.mult, op1=ALU.bypass), inc="v")
        P.pb("vector", "pB")
        P.op("vector", lambda e: e.tensor_scalar(out=msk8[:, :], in0=pB[:, 0:8],
             scalar1=mx8[:, :], scalar2=None, op0=ALU.is_equal, op1=ALU.bypass), inc="v")
        P.pe("vector", "v", "pB")
        P.wait("vector", "t", t_h)
        P.pb("vector", "pA")
        for ex in range(8):
            P.op("vector", lambda e, x_=ex: e.tensor_scalar(out=hmsk[:, 64*x_:64*(x_+1)],
                 in0=pA[:, 64*x_:64*(x_+1)], scalar1=msk8[:, x_:x_+1], scalar2=None,
                 op0=ALU.mult, op1=ALU.bypass), inc="v")
        v_hm = P.val("v")
        P.pe("vector", "v", "pA")
        P.wait("sync", "v", v_hm)
        P.wait("scalar", "v", v_hm)
        for jj in range(HP):
            eng = "sync" if jj % 2 == 0 else "scalar"
            P.op(eng, lambda e, j_=jj: e.dma_start(out=hfm[j_][:, :],
                 in_=hmsk[:, 128*j_:128*(j_+1)], transpose=True), inc="dz", by=16)
        dz_h = P.val("dz")
        P.wait("tensor", "v", v_hm)
        P.wait("tensor", "dz", dz_h)
        P.pb("tensor", "pA", "pB", "pGAQ", "pS")
        for jj in range(HP):
            for q in range(4):
                P.op("tensor", lambda e, j_=jj, q_=q: e.matmul(
                    attnb[q_], hfm[j_][:, :], bss[:, j_, 512*q_:512*(q_+1)],
                    start=(j_ == 0), stop=(j_ == HP-1)),
                    inc="t" if (jj == HP-1 and q == 3) else None)
        t_y = P.val("t")
        P.pe("tensor", "t", "pA", "pB", "pGAQ", "pS")
        P.wait("vector", "t", t_y)
        P.wait("vector", f"so{b}", st3.get(f"store_so_{b}", 0))
        P.pb("vector", "pA", "pB", "pGAQ", "pS")
        for q in range(4):
            P.op("vector", lambda e, q_=q, b_=b: e.scalar_tensor_tensor(
                 out=out3[b_][:, 512*q_:512*(q_+1)], in0=attnb[q_], scalar=wg8[:, :],
                 in1=xmid[:, 512*q_:512*(q_+1)], op0=ALU.mult, op1=ALU.add), inc="v")
        st3["free_v"] = P.val("v")
        st3["out_done_v"] = P.val("v")
        P.pe("vector", "v", "pA", "pB", "pGAQ", "pS")
        st3["free_s"] = P.val("s")
        v_out = P.val("v")
        P.wait("scalar", "v", v_out)
        P.op("scalar", lambda e, j_=j, b_=b: e.dma_start(out=ap["out"][TW*j_:TW*(j_+1), :],
             in_=out3[b_][:, :]), inc=f"so{b}", by=16)
        st3[f"store_so_{b}"] = P.val(f"so{b}")
        st3[f"v_xmid_{b}"] = v_xmid

    for s_ in ("dg", "dz", "dy0", "dy1", "dx0", "dx1", "so0", "so1"):
        P.wait("gpsimd", s_)
    finalize(nc, P)
    return nc, {"NW": NW}


def finalize(nc, P):
    with contextlib.ExitStack() as semstack:
        for name in P.cnt:
            P.sems[name] = semstack.enter_context(nc.semaphore(name))
        with nc.Block() as block:
            @block.tensor
            def _(tensor):
                for fn in P.ops["tensor"]:
                    fn(tensor)
            @block.vector
            def _(vector):
                for fn in P.ops["vector"]:
                    fn(vector)
            @block.scalar
            def _(scalar):
                for fn in P.ops["scalar"]:
                    fn(scalar)
            @block.sync
            def _(sync):
                for fn in P.ops["sync"]:
                    fn(sync)
            @block.gpsimd
            def _(gpsimd):
                for fn in P.ops["gpsimd"]:
                    fn(gpsimd)


def prep_core_inputs(core, x, Wq, Wk, Wv, Wo, Wb, ln_w, ln_b, Wg1, Wg2, norm_w,
                     A, Bw, Wr, br):
    b = core // 4
    hg = core % 4
    cs = slice(512 * hg, 512 * (hg + 1))
    hs = slice(4 * hg, 4 * (hg + 1))
    T = x.shape[1]
    f32 = np.float32
    xb = np.ascontiguousarray(x[b]).astype(f32, copy=False)
    xbT = np.ascontiguousarray(x[b].T)
    lw = ln_w[hs]
    lb = ln_b[hs]
    wstk = np.zeros((128, 128), f32)
    for p in range(4):
        wstk[32*p:32*p+16, :] = lw[p][None, :]
    woln = np.broadcast_to(lw[None, :, :], (128, 4, 128)).copy()
    boln = np.broadcast_to(lb.reshape(1, 512), (128, 512)).copy()
    wnat = np.broadcast_to(lw.reshape(1, 512), (128, 512)).copy()
    nrm = np.broadcast_to(norm_w[cs][None, :], (128, 512)).copy()
    awr = np.concatenate([A.transpose(2, 0, 1).reshape(2048, 512), Wr.T], axis=1)
    bstk = np.zeros((128, 4, 2048), f32)
    for jj in range(4):
        bstk[0:64, jj, :] = Bw[2*jj].T
        bstk[64:128, jj, :] = Bw[2*jj+1].T
    q3 = core % 4
    N3 = T // 4
    x3s = np.ascontiguousarray(x[b][N3*q3:N3*(q3+1)])
    return {
        "xb": xb, "xbT": xbT,
        "wq": np.ascontiguousarray(Wq[:, cs]), "wk": np.ascontiguousarray(Wk[:, cs]),
        "wv": np.ascontiguousarray(Wv[:, cs]),
        "wbeta": np.ascontiguousarray(Wb[:, hs]),
        "lnw": np.ascontiguousarray(lw[None]), "lnb": np.ascontiguousarray(lb[None]),
        "wstk": wstk, "woln": woln, "boln": boln, "wnat": wnat, "nrm": nrm,
        "wg1": np.ascontiguousarray(Wg1), "wg2": np.ascontiguousarray(Wg2[:, cs]),
        "wo": np.ascontiguousarray(Wo[cs, :]), "awr": np.ascontiguousarray(awr),
        "bstk": bstk, "brow": np.ascontiguousarray(br[None, :]).astype(f32),
        "x3s": x3s,
    }


T_FULL = 8192
N3_FULL = T_FULL // 4

_cache = {}

def kernel(L0_attn_x, L0_attn_Wq, L0_attn_Wk, L0_attn_Wv, L0_attn_Wo,
           L0_attn_Wbeta, L0_attn_ln_w, L0_attn_ln_b, L0_attn_Wg1, L0_attn_Wg2,
           L0_attn_norm_w, L0_ffn_A, L0_ffn_B, L0_ffn_Wr, L0_ffn_br):
    from concourse.bass_utils import run_bass_kernel_spmd
    x = np.asarray(L0_attn_x, np.float32)
    args = [np.asarray(a, np.float32) for a in
            (L0_attn_Wq, L0_attn_Wk, L0_attn_Wv, L0_attn_Wo, L0_attn_Wbeta,
             L0_attn_ln_w, L0_attn_ln_b, L0_attn_Wg1, L0_attn_Wg2, L0_attn_norm_w,
             L0_ffn_A, L0_ffn_B, L0_ffn_Wr, L0_ffn_br)]
    ims = [prep_core_inputs(c, x, *args) for c in range(8)]
    if "nc" not in _cache:
        _cache["nc"], _ = build(T_FULL, debug_og=False)
    res = run_bass_kernel_spmd(_cache["nc"], ims, core_ids=list(range(8)))
    out = np.empty((2, T_FULL, 2048), np.float32)
    for c in range(8):
        b, q3 = c // 4, c % 4
        out[b, N3_FULL*q3:N3_FULL*(q3+1)] = res.results[c]["out"]
    return out
